# version 09
# baseline (speedup 1.0000x reference)
import math
from contextlib import ExitStack

import numpy as np

import concourse.bass as bass
import concourse.mybir as mybir
from concourse.bass_utils import run_bass_kernel_spmd

F32 = mybir.dt.float32
BF16 = mybir.dt.bfloat16
AF = mybir.ActivationFunctionType
ALU = mybir.AluOpType
AX = mybir.AxisListType
ET = mybir.EngineType

B, T, C = 4, 8192, 1024
P = 128
H = 8
NHC = 4
L = 512
NCH = T // L
GAMMA = 0.96875
GL = GAMMA ** L
HALF = T // 2
PCP = 1024
NCP = HALF // PCP
NE = 17
INT = 512
EPS = 1e-5
ROPE_DIM = 64
NCORES = 8

DEBUG_DUMPS = False
P2_FFN = True
P2_ATT = True
CC_SHARED = True



def _rope_perm():
    p = list(range(0, ROPE_DIM, 2)) + list(range(1, ROPE_DIM, 2)) + list(range(ROPE_DIM, C))
    return np.array(p, dtype=np.int64)


def _host_constants():
    inv_freq = 1.0 / 10000.0 ** (np.arange(0, ROPE_DIM, 2, dtype=np.float64) / ROPE_DIM)
    t = np.arange(T, dtype=np.float64)
    ang = inv_freq[:, None] * t[None, :]
    ropeC = np.cos(ang).astype(np.float32)
    ropeS = np.sin(ang).astype(np.float32)
    q = np.arange(L)[:, None].astype(np.float64)
    k = np.arange(L)[None, :].astype(np.float64)
    maskP = np.where(q >= k, GAMMA ** (q - L), 0.0).astype(np.float32)
    r = np.arange(L, dtype=np.float64)
    gvec = np.stack([GAMMA ** r, GAMMA ** (L - r)], axis=1).astype(np.float32)
    identf = np.eye(P, dtype=np.float32)
    onesC = np.full((P, P), 1.0 / C, dtype=np.float32)
    return ropeC, ropeS, maskP, gvec, identf, onesC


def _prep_core_inputs(c, x, wq_full, wk_full, wv_full, gnw, gnb, gate_w, gate_b,
                      w13, w2, w13_s, w2_s, consts):
    b, hg = c // 2, c % 2
    heads = [hg * NHC + k for k in range(NHC)]
    perm = _rope_perm()
    ropeC, ropeS, maskP, gvec, identf, onesC = consts

    xT = np.ascontiguousarray(
        x[b].reshape(NCH, L, 8, P).transpose(0, 3, 2, 1).reshape(NCH * P, 8 * L))
    wq = np.concatenate([wq_full[:, h * C:(h + 1) * C][:, perm] for h in heads], axis=1)
    wq = np.ascontiguousarray(wq.reshape(8, P, NHC, C).transpose(2, 1, 0, 3).reshape(NHC * P, 8 * C))
    wk = np.concatenate([wk_full[:, h * C:(h + 1) * C][:, perm] for h in heads], axis=1)
    wk = np.ascontiguousarray(wk.reshape(8, P, NHC, C).transpose(2, 1, 0, 3).reshape(NHC * P, 8 * C))
    wv = np.concatenate([wv_full[:, h * C:(h + 1) * C] for h in heads], axis=1)
    wv = np.ascontiguousarray(wv.reshape(8, P, NHC, C).transpose(2, 1, 0, 3).reshape(NHC * P, 8 * C))
    w13T = np.stack([w13[e].T for e in range(16)] + [w13_s.T])
    w13T = np.ascontiguousarray(w13T.reshape(NE, 8, P, C).transpose(0, 2, 1, 3).reshape(NE * P, 8 * C))
    w2T = np.stack([w2[e].T for e in range(16)] + [w2_s.T])
    w2T = np.ascontiguousarray(w2T.reshape(NE, 4, P, C).transpose(0, 2, 1, 3).reshape(NE * P, 4 * C))
    return {
        "xT": xT, "wq": wq, "wk": wk, "wv": wv,
        "w13T": w13T, "w2T": w2T,
        "gwT": np.ascontiguousarray(gate_w.T),
        "gbt": np.ascontiguousarray(np.tile(gate_b[None, :], (P, 1))),
        "gnw": np.ascontiguousarray(gnw.reshape(C, 1)),
        "gnb": np.ascontiguousarray(gnb.reshape(C, 1)),
        "identf": identf, "onesC": onesC,
        "ropeC": ropeC, "ropeS": ropeS, "maskP": maskP, "gvec": gvec,
    }


_ROLE_TGT = {}


class Sem:
    def __init__(self, nc, ctx, name):
        self.h = ctx.enter_context(nc.semaphore(name))
        self.n = 0

    def inc(self, instr, k=1):
        instr.then_inc(self.h, k)
        self.n += k
        _ROLE_TGT[instr.ins.name] = (self.h, self.n)
        return self.n


class _Guard:

    def __init__(self, eng, sem):
        self._eng = eng
        self._sem = sem

    def _settle(self):
        p = getattr(self._sem, "pend", None)
        if p is None:
            return
        self._sem.pend = None
        tgt = _ROLE_TGT.pop(p.ins.name, None)
        if tgt is None:
            p.then_inc(self._sem.h, 1)
            self._sem.n += 1
            tgt = (self._sem.h, self._sem.n)
        self._eng.wait_ge(tgt[0], tgt[1])

    def wait_ge(self, h, v):
        self._eng.wait_ge(h, v)

    def nop(self):
        self._settle()
        return self._eng.nop()

    def dma_start(self, *a, **kw):
        return self._eng.dma_start(*a, **kw)

    def snap(self, *a, **kw):
        return self._eng.snap(*a, **kw)

    def __getattr__(self, name):
        f = getattr(self._eng, name)

        def wrap(*a, **kw):
            self._settle()
            ins = f(*a, **kw)
            if hasattr(ins, "then_inc"):
                self._sem.pend = ins
            return ins

        return wrap



def build_program():
    nc = bass.Bass("TRN2", target_bir_lowering=False, debug=False)

    xT = nc.dram_tensor("xT", [NCH * P, 8 * L], F32, kind="ExternalInput").ap()
    wq = nc.dram_tensor("wq", [NHC * P, 8 * C], F32, kind="ExternalInput").ap()
    wk = nc.dram_tensor("wk", [NHC * P, 8 * C], F32, kind="ExternalInput").ap()
    wv = nc.dram_tensor("wv", [NHC * P, 8 * C], F32, kind="ExternalInput").ap()
    w13T = nc.dram_tensor("w13T", [NE * P, 8 * C], F32, kind="ExternalInput").ap()
    w2T = nc.dram_tensor("w2T", [NE * P, 4 * C], F32, kind="ExternalInput").ap()
    gwT = nc.dram_tensor("gwT", [C, 16], F32, kind="ExternalInput").ap()
    gbt = nc.dram_tensor("gbt", [P, 16], F32, kind="ExternalInput").ap()
    gnw = nc.dram_tensor("gnw", [C, 1], F32, kind="ExternalInput").ap()
    gnb = nc.dram_tensor("gnb", [C, 1], F32, kind="ExternalInput").ap()
    identf = nc.dram_tensor("identf", [P, P], F32, kind="ExternalInput").ap()
    onesC = nc.dram_tensor("onesC", [P, P], F32, kind="ExternalInput").ap()
    ropeC = nc.dram_tensor("ropeC", [32, T], F32, kind="ExternalInput").ap()
    ropeS = nc.dram_tensor("ropeS", [32, T], F32, kind="ExternalInput").ap()
    maskP = nc.dram_tensor("maskP", [L, L], F32, kind="ExternalInput").ap()
    gvec = nc.dram_tensor("gvec", [L, 2], F32, kind="ExternalInput").ap()

    outT = nc.dram_tensor("outT", [NCP * P, 2 * 8 * L], F32, kind="ExternalOutput").ap()

    xnb_s = nc.dram_tensor("xnb_s", [NCH * P, 8 * L], BF16, kind="Internal").ap()
    xnf_s = nc.dram_tensor("xnf_s", [NCH * P, 8 * L], F32, kind="Internal").ap()
    part4 = nc.dram_tensor("part4", [NCH * P, 4 * C], F32, kind="Internal").ap()
    wqb_s = nc.dram_tensor("wqb_s", [NHC * P, 8 * C], BF16, kind="Internal").ap()
    wkb_s = nc.dram_tensor("wkb_s", [NHC * P, 8 * C], BF16, kind="Internal").ap()
    wvb_s = nc.dram_tensor("wvb_s", [NHC * P, 8 * C], BF16, kind="Internal").ap()
    w13b_s = nc.dram_tensor("w13b_s", [NE * P, 8 * C], BF16, kind="Internal").ap()
    w2b_s = nc.dram_tensor("w2b_s", [NE * P, 4 * C], BF16, kind="Internal").ap()
    ropeCb = nc.dram_tensor("ropeCb", [32, T], BF16, kind="Internal").ap()
    ropeSb = nc.dram_tensor("ropeSb", [32, T], BF16, kind="Internal").ap()
    cc_sum = nc.dram_tensor("cc_sum", [NCH * P, 4 * C], F32, kind="Internal").ap()

    if DEBUG_DUMPS:
        dump_xnf = nc.dram_tensor("dump_xnf", [C, T], F32, kind="ExternalOutput").ap()
        dump_p4 = nc.dram_tensor("dump_p4", [T, C], F32, kind="ExternalOutput").ap()
        dump_att = nc.dram_tensor("dump_att", [PCP, C], F32, kind="ExternalOutput").ap()
        dump_att2 = nc.dram_tensor("dump_att2", [PCP, C], F32, kind="ExternalOutput").ap()
        dump_cc = nc.dram_tensor("dump_cc", [PCP, C], F32, kind="ExternalOutput").ap()

    dram_xT_chunk = lambda j: xT[j * P:(j + 1) * P]
    dram_xnb_chunk = lambda j: xnb_s[j * P:(j + 1) * P]
    dram_xnf_chunk = lambda j: xnf_s[j * P:(j + 1) * P]

    with ExitStack() as ctx:
        sb = lambda name, shape, dt: ctx.enter_context(nc.sbuf_tensor(name, shape, dt))
        mksem = lambda name: Sem(nc, ctx, name)

        IDB = sb("IDB", [P, P], BF16)
        IDF = sb("IDF", [P, P], F32)
        ONB = sb("ONB", [P, P], BF16)
        GWT = sb("GWT", [P, 8, 16], F32)
        GBT = sb("GBT", [P, 16], F32)
        GNW = sb("GNW", [P, 8, 1], F32)
        GNB = sb("GNB", [P, 8, 1], F32)
        ONE1 = sb("ONE1", [1, P], BF16)
        SFL = sb("SFL", [1, 4], F32)

        ph = mksem("ph")
        vself = mksem("vself")
        sself = mksem("sself")
        oneS = mksem("oneS")
        PH = [0]
        sC = mksem("sC")
        gCa = mksem("gCa")
        gCb = mksem("gCb")
        gCc = mksem("gCc")
        gCbl = [None] + [mksem(f"gCbl{h}") for h in range(1, 4)]

        block = ctx.enter_context(nc.Block())

        @block.sync
        def _(eng):
            eng.dma_start(out=IDF[:], in_=identf).then_inc(sC.h, 16)
            eng.dma_start(out=GWT[:], in_=gwT.rearrange("(a p) e -> p a e", p=P)).then_inc(sC.h, 16)
            eng.dma_start(out=GBT[:], in_=gbt).then_inc(sC.h, 16)
            eng.dma_start(out=GNW[:], in_=gnw.rearrange("(a p) o -> p a o", p=P), allow_slow_non_contiguous=True).then_inc(sC.h, 16)
            eng.dma_start(out=GNB[:], in_=gnb.rearrange("(a p) o -> p a o", p=P), allow_slow_non_contiguous=True).then_inc(sC.h, 16)
            sC.n += 80

        @block.gpsimd
        def _(eng):
            eng.dma_start(out=IDB[:], in_=identf).then_inc(gCa.h, 16)
            eng.dma_start(out=ONB[:], in_=onesC).then_inc(gCa.h, 16)
            gCa.n += 32
            eng.memset(ONE1[:], 1.0)
            eng.memset(SFL[:], 0.0).then_inc(oneS.h, 1)
            oneS.n += 1
            eng.dma_start(out=wqb_s[0:P], in_=wq[0:P]).then_inc(gCb.h, 16)
            eng.dma_start(out=wkb_s[0:P], in_=wk[0:P]).then_inc(gCb.h, 16)
            eng.dma_start(out=wvb_s[0:P], in_=wv[0:P]).then_inc(gCb.h, 16)
            eng.dma_start(out=ropeCb, in_=ropeC).then_inc(gCb.h, 16)
            eng.dma_start(out=ropeSb, in_=ropeS).then_inc(gCb.h, 16)
            gCb.n += 80

        def phase_barrier(waits=None):
            PH[0] += 1
            decs = {"sync": block.sync, "gpsimd": block.gpsimd, "tensor": block.tensor,
                    "vector": block.vector, "scalar": block.scalar}
            for name, dec in decs.items():
                wl = (waits or {}).get(name, [])

                @dec
                def _(eng, wl=wl):
                    for (sh, sv) in wl:
                        eng.wait_ge(sh, sv)
                    eng.nop().then_inc(ph.h, 1)

        with ExitStack() as p0:
            sb0 = lambda n, s, d: p0.enter_context(nc.sbuf_tensor(n, s, d))
            XF = sb0("XF", [P, 4, 8, L], F32)
            XB = sb0("XB", [P, 4, 8, L], BF16)
            SQ = sb0("SQ", [P, 2, 8, L], BF16)
            MSB = sb0("MSB", [P, L], F32)
            M2 = sb0("M2", [P, L], F32)
            VEPS = sb0("VEPS", [P, L], F32)
            REC = sb0("REC", [P, L], F32)
            RSTD = sb0("RSTD", [P, L], F32)
            T1 = sb0("T1", [P, 8, L], F32)
            XNB = sb0("XNB", [P, 2, 8, L], BF16)
            XNF = sb0("XNF", [P, 2, 8, L], F32)
            PM = [p0.enter_context(nc.psum_tensor(f"PM{i}", [P, L], F32)) for i in range(2)]
            PQ = [p0.enter_context(nc.psum_tensor(f"PQ{i}", [P, L], F32)) for i in range(2)]

            s_x = [mksem(f"s_x{i}") for i in range(4)]
            sc_ca = mksem("sc_ca")
            sc_sq = mksem("sc_sq")
            t_pm = mksem("t_pm")
            v_rc = mksem("v_rc")
            sc_rs = mksem("sc_rs")
            v_dn = mksem("v_dn")
            s_st = [mksem("s_st0"), mksem("s_st1")]

            order = list(range(NCH - 1, -1, -1))

            @block.sync
            def _(eng):
                eng.wait_ge(sC.h, sC.n)
                for i, j in enumerate(order):
                    d4 = i % 4
                    if i >= 4:
                        eng.wait_ge(v_dn.h, i - 3)
                        eng.wait_ge(sc_sq.h, i - 3)
                    eng.dma_start(out=XF[:, d4].rearrange("p a t -> p (a t)"), in_=dram_xT_chunk(j)).then_inc(s_x[d4].h, 16)
                    s_x[d4].n += 16
                    if i >= 1:
                        pj = order[i - 1]
                        pd = (i - 1) % 2
                        eng.wait_ge(v_dn.h, i)
                        eng.dma_start(out=dram_xnb_chunk(pj), in_=XNB[:, pd].rearrange("p a t -> p (a t)")).then_inc(s_st[pd].h, 16)
                        eng.dma_start(out=dram_xnf_chunk(pj), in_=XNF[:, pd].rearrange("p a t -> p (a t)")).then_inc(s_st[pd].h, 16)
                        s_st[pd].n += 32
                lj = order[NCH - 1]
                ld = (NCH - 1) % 2
                eng.wait_ge(v_dn.h, NCH)
                eng.dma_start(out=dram_xnb_chunk(lj), in_=XNB[:, ld].rearrange("p a t -> p (a t)")).then_inc(s_st[ld].h, 16)
                eng.dma_start(out=dram_xnf_chunk(lj), in_=XNF[:, ld].rearrange("p a t -> p (a t)")).then_inc(s_st[ld].h, 16)
                s_st[ld].n += 32

            @block.gpsimd
            def _(eng):
                eng.wait_ge(v_dn.h, 4)
                for hh in range(1, NHC):
                    eng.dma_start(out=wqb_s[hh * P:(hh + 1) * P], in_=wq[hh * P:(hh + 1) * P]).then_inc(gCbl[hh].h, 16)
                    eng.dma_start(out=wkb_s[hh * P:(hh + 1) * P], in_=wk[hh * P:(hh + 1) * P]).then_inc(gCbl[hh].h, 16)
                    eng.dma_start(out=wvb_s[hh * P:(hh + 1) * P], in_=wv[hh * P:(hh + 1) * P]).then_inc(gCbl[hh].h, 16)
                    gCbl[hh].n += 48
                eng.wait_ge(v_dn.h, NCH)
                eng.dma_start(out=w13b_s, in_=w13T).then_inc(gCc.h, 16)
                eng.dma_start(out=w2b_s, in_=w2T).then_inc(gCc.h, 16)
                gCc.n += 32

            @block.scalar
            def _(eng):
                eng = _Guard(eng, sself)
                eng.wait_ge(oneS.h, 1)

                def cast(i):
                    d4 = i % 4
                    eng.wait_ge(s_x[d4].h, 16 * (i // 4 + 1))
                    if i >= 4:
                        eng.wait_ge(t_pm.h, i - 3)
                    ins = eng.activation(XB[:, i % 4], XF[:, i % 4], AF.Copy)
                    sc_ca.inc(ins)

                cast(0)
                for i in range(NCH):
                    if i + 1 < NCH:
                        cast(i + 1)
                    d = i % 2
                    if i >= 2:
                        eng.wait_ge(t_pm.h, i - 1)
                    d4 = i % 4
                    ins = eng.activation(SQ[:, d], XB[:, d4], AF.Square)
                    sc_sq.inc(ins)
                    eng.wait_ge(v_rc.h, i + 1)
                    ins = eng.activation(RSTD[:], REC[:], AF.Sqrt)
                    sc_rs.inc(ins)

            @block.tensor
            def _(eng):
                eng.wait_ge(gCa.h, 32)
                for i in range(NCH):
                    d = i % 2
                    d4 = i % 4
                    eng.wait_ge(sc_ca.h, i + 1)
                    eng.wait_ge(sc_sq.h, i + 1)
                    if i >= 2:
                        eng.wait_ge(v_dn.h, i - 1)
                    for ci in range(8):
                        eng.matmul(PM[d][:], ONB[:], XB[:, d4, ci], start=(ci == 0), stop=(ci == 7))
                    for ci in range(8):
                        ins = eng.matmul(PQ[d][:], ONB[:], SQ[:, d, ci], start=(ci == 0), stop=(ci == 7))
                    ins.then_inc(t_pm.h, 1)
                    t_pm.n += 1

            @block.vector
            def _(eng):
                eng = _Guard(eng, vself)
                for i in range(NCH):
                    d = i % 2
                    eng.wait_ge(t_pm.h, i + 1)
                    eng.tensor_copy(MSB[:], PM[d][:])
                    eng.tensor_tensor(M2[:], MSB[:], MSB[:], ALU.mult)
                    eng.scalar_tensor_tensor(VEPS[:], M2[:], -1.0, PQ[d][:], ALU.mult, ALU.add)
                    eng.tensor_scalar(VEPS[:], VEPS[:], EPS, None, ALU.add)
                    ins = eng.reciprocal(REC[:], VEPS[:])
                    v_rc.inc(ins)
                    eng.wait_ge(sc_rs.h, i + 1)
                    d4 = i % 4
                    eng.wait_ge(s_x[d4].h, 16 * (i // 4 + 1))
                    if i >= 2:
                        eng.wait_ge(s_st[d].h, 32 * (i // 2))
                    pmb = PM[d][:].rearrange("p (o t) -> p o t", o=1).to_broadcast([P, 8, L])
                    rsb = RSTD[:].rearrange("p (o t) -> p o t", o=1).to_broadcast([P, 8, L])
                    eng.tensor_tensor(T1[:], XF[:, d4], pmb, ALU.subtract)
                    eng.tensor_tensor(XNB[:, d], T1[:], rsb, ALU.mult)
                    ins = eng.tensor_tensor(XNF[:, d], T1[:], rsb, ALU.mult)
                    v_dn.inc(ins)

            phase_barrier({
                "sync": [(s_st[0].h, s_st[0].n), (s_st[1].h, s_st[1].n)],
                "gpsimd": [],
                "tensor": [(t_pm.h, t_pm.n)],
                "vector": [(v_dn.h, v_dn.n)],
                "scalar": [(sc_rs.h, sc_rs.n)],
            })

        with ExitStack() as p1:
            sb1 = lambda n, s, d: p1.enter_context(nc.sbuf_tensor(n, s, d))
            XN1 = sb1("XN1", [P, 2, 8, L], BF16)
            WQ = sb1("WQ", [P, 2, 8, C], BF16)
            WK = sb1("WK", [P, 2, 8, C], BF16)
            WV = sb1("WV", [P, 8, C], BF16)
            QF = sb1("QF", [P, 8, L], BF16)
            KF = sb1("KF", [P, 8, L], BF16)
            VSB = sb1("VSB", [P, 4, C], BF16)
            QTK = sb1("QTK", [P, 4, C], BF16)
            PMS = sb1("PMS", [P, 4, L], BF16)
            SSB = sb1("SSB", [P, 8, C], BF16)
            OUTS = sb1("OUTS", [P, 2, 4, C], F32)
            RT = sb1("RT", [P, L], F32)
            RA = sb1("RA", [32, L], F32)
            RB = sb1("RB", [32, L], F32)
            MSK = sb1("MSK", [P, 4, L], BF16)
            GV = sb1("GV", [P, 4, 2], F32)
            RCS = sb1("RCS", [32, 2, L], BF16)
            RSS = sb1("RSS", [32, 2, L], BF16)
            PRJ = [p1.enter_context(nc.psum_tensor(f"PRJ{i}", [P, L], F32)) for i in range(2)]
            QTP = [p1.enter_context(nc.psum_tensor(f"QTP{i}", [P, C], BF16)) for i in range(2)]
            PSH = [p1.enter_context(nc.psum_tensor(f"PSH{i}", [P, L], F32)) for i in range(2)]
            OPS = [p1.enter_context(nc.psum_tensor(f"OPS{i}", [P, L], F32)) for i in range(2)]

            sXN = [mksem("sXN0"), mksem("sXN1")]
            sRP = [mksem("sRP0"), mksem("sRP1")]
            gMS = mksem("gMS")
            gW = mksem("gW")
            gA = [mksem(f"gA{j}") for j in range(NCH)]
            tE = mksem("tE")
            scEV = mksem("scEV")
            vRP = mksem("vRP")
            vQT = mksem("vQT")
            vPSH = mksem("vPSH")
            vPF = mksem("vPF")
            scF = mksem("scF")
            scR = mksem("scR")
            vOE = mksem("vOE")

            NT = NHC * NCH
            GT = 56

            def tkt(tt):
                return tt // NCH, tt % NCH, NCH - 1 - (tt % NCH)

            @block.sync
            def _(eng):
                eng.wait_ge(ph.h, 5 * PH[0])
                eng.wait_ge(gCb.h, 80)
                for tt in range(NT):
                    h, jj, j = tkt(tt)
                    d = tt % 2
                    if jj == 0:
                        wd = h % 2
                        if h >= 1:
                            eng.wait_ge(gCbl[h].h, 48)
                        if h >= 2:
                            eng.wait_ge(tE.h, GT * (NCH * (h - 1) - 1) + 24)
                        hh = h
                        eng.dma_start(out=WQ[:, wd].rearrange("p a c -> p (a c)"), in_=wqb_s[hh * P:(hh + 1) * P]).then_inc(gW.h, 16)
                        eng.dma_start(out=WK[:, wd].rearrange("p a c -> p (a c)"), in_=wkb_s[hh * P:(hh + 1) * P]).then_inc(gW.h, 16)
                        gW.n += 32
                        if h > 0:
                            eng.wait_ge(tE.h, GT * (NCH * h - 1) + 24)
                        eng.dma_start(out=WV[:].rearrange("p a c -> p (a c)"), in_=wvb_s[hh * P:(hh + 1) * P]).then_inc(gW.h, 16)
                        gW.n += 16
                    if tt >= 2:
                        eng.wait_ge(tE.h, GT * (tt - 2) + 24)
                        eng.wait_ge(vRP.h, 2 * (tt - 1))
                    eng.dma_start(out=XN1[:, d].rearrange("p a t -> p (a t)"), in_=dram_xnb_chunk(j)).then_inc(sXN[d].h, 16)
                    sXN[d].n += 16
                    eng.dma_start(out=RCS[:, d], in_=ropeCb[:, j * L:(j + 1) * L]).then_inc(sRP[d].h, 16)
                    eng.dma_start(out=RSS[:, d], in_=ropeSb[:, j * L:(j + 1) * L]).then_inc(sRP[d].h, 16)
                    sRP[d].n += 32

            @block.gpsimd
            def _(eng):
                eng.wait_ge(ph.h, 5 * PH[0])
                eng.dma_start(out=MSK[:], in_=maskP.rearrange("(a p) k -> p a k", p=P)).then_inc(gMS.h, 16)
                eng.dma_start(out=GV[:], in_=gvec.rearrange("(a p) c -> p a c", p=P)).then_inc(gMS.h, 16)
                gMS.n += 32
                for h in range(NHC):
                    for jj in range(NCH):
                        tt = h * NCH + jj
                        j = NCH - 1 - jj
                        d = tt % 2
                        eng.wait_ge(vOE.h, 8 * (tt + 1))
                        if h > 0:
                            eng.wait_ge(gA[j].h, 64 * h)
                        dst = part4[j * P:(j + 1) * P]
                        for aa in range(4):
                            da = dst[:, aa * C:(aa + 1) * C]
                            if h == 0:
                                ins = eng.dma_start(out=da, in_=OUTS[:, d, aa, :])
                            else:
                                ins = eng.dma_start(out=da, in_=OUTS[:, d, aa, :], accum_op=ALU.add)
                            ins.then_inc(gA[j].h, 16)
                            gA[j].n += 16

            @block.tensor
            def _(eng):
                eng.wait_ge(ph.h, 5 * PH[0])
                for tt in range(NT):
                    h, jj, j = tkt(tt)
                    d = tt % 2
                    first = jj == 0
                    eng.wait_ge(sXN[d].h, 16 * (tt // 2 + 1))
                    if jj == 0:
                        eng.wait_ge(gW.h, 48 * (h + 1))
                    for ci in range(8):
                        Gp = 24 * tt + ci
                        if Gp >= 2:
                            eng.wait_ge(scEV.h, Gp - 1)
                        bk = PRJ[Gp % 2]
                        for kci in range(8):
                            ins = eng.matmul(bk[:], WQ[:, h % 2, kci, ci * P:(ci + 1) * P], XN1[:, d, kci],
                                             start=(kci == 0), stop=(kci == 7))
                        tE.inc(ins)
                    for ci in range(8):
                        Gp = 24 * tt + 8 + ci
                        if Gp >= 2:
                            eng.wait_ge(scEV.h, Gp - 1)
                        bk = PRJ[Gp % 2]
                        for kci in range(8):
                            ins = eng.matmul(bk[:], WK[:, h % 2, kci, ci * P:(ci + 1) * P], XN1[:, d, kci],
                                             start=(kci == 0), stop=(kci == 7))
                        tE.inc(ins)
                    for gi in range(8):
                        ttile, hf = gi // 2, gi % 2
                        Gp = 24 * tt + 16 + gi
                        if Gp >= 2:
                            eng.wait_ge(scEV.h, Gp - 1)
                        bk = PRJ[Gp % 2]
                        for kci in range(8):
                            ins = eng.matmul(bk[:], XN1[:, d, kci, ttile * P:(ttile + 1) * P],
                                             WV[:, kci, hf * L:(hf + 1) * L],
                                             start=(kci == 0), stop=(kci == 7))
                        tE.inc(ins)
                    eng.wait_ge(scEV.h, 24 * tt + 8)
                    eng.wait_ge(scF.h, 3 * tt + 1)
                    for q in range(4):
                        Gq = 4 * tt + q
                        if Gq >= 2:
                            eng.wait_ge(vQT.h, Gq - 1)
                        bk = QTP[Gq % 2]
                        for ci in range(8):
                            ins = eng.transpose(bk[:, ci * P:(ci + 1) * P],
                                                QF[:, ci, q * P:(q + 1) * P], IDB[:])
                        tE.inc(ins)
                    eng.wait_ge(scEV.h, 24 * tt + 16)
                    eng.wait_ge(scF.h, 3 * tt + 2)
                    for q in range(4):
                        Sp = 20 * tt + q
                        if Sp >= 2:
                            eng.wait_ge(vPSH.h, Sp - 1)
                        bk = PSH[Sp % 2]
                        for ci in range(8):
                            ins = eng.matmul(bk[:], QF[:, ci, q * P:(q + 1) * P], KF[:, ci],
                                             start=(ci == 0), stop=(ci == 7))
                        tE.inc(ins)
                    eng.wait_ge(scEV.h, 24 * (tt + 1))
                    eng.wait_ge(scF.h, 3 * (tt + 1))
                    eng.wait_ge(vPSH.h, 20 * tt + 4)
                    eng.wait_ge(vPF.h, tt + 1)
                    for o in range(8):
                        kt, hf = o // 2, o % 2
                        Op = 8 * tt + o
                        if Op >= 2:
                            eng.wait_ge(vOE.h, Op - 1)
                        bk = OPS[Op % 2]
                        nmm = 4 if first else 12
                        mi = 0
                        for qt in range(4):
                            ins = eng.matmul(bk[:], PMS[:, qt, kt * P:(kt + 1) * P],
                                             VSB[:, qt, hf * L:(hf + 1) * L],
                                             start=(mi == 0), stop=(mi == nmm - 1))
                            mi += 1
                        if not first:
                            for di in range(8):
                                ins = eng.matmul(bk[:], KF[:, di, kt * P:(kt + 1) * P],
                                                 SSB[:, di, hf * L:(hf + 1) * L],
                                                 start=False, stop=(mi == nmm - 1))
                                mi += 1
                        tE.inc(ins)
                    eng.wait_ge(vQT.h, 4 * (tt + 1))
                    for s in range(16):
                        c1i, hf = s // 2, s % 2
                        Sp = 20 * tt + 4 + s
                        if Sp >= 2:
                            eng.wait_ge(vPSH.h, Sp - 1)
                        bk = PSH[Sp % 2]
                        for rt in range(4):
                            ins = eng.matmul(bk[:], QTK[:, rt, c1i * P:(c1i + 1) * P],
                                             VSB[:, rt, hf * L:(hf + 1) * L],
                                             start=(rt == 0), stop=(rt == 3))
                        tE.inc(ins)
                assert tE.n == GT * NT

            @block.scalar
            def _(eng):
                eng = _Guard(eng, sself)
                eng.wait_ge(ph.h, 5 * PH[0])
                for tt in range(NT):
                    for ci in range(8):
                        eng.wait_ge(tE.h, GT * tt + ci + 1)
                        bk = PRJ[(24 * tt + ci) % 2]
                        if ci == 0:
                            eng.wait_ge(vRP.h, 2 * tt + 1)
                            eng.activation(QF[0:64, ci, :], RT[0:64, :], AF.Silu)
                            ins = eng.activation(QF[64:128, ci, :], bk[64:128, :], AF.Silu)
                        else:
                            ins = eng.activation(QF[:, ci, :], bk[:], AF.Silu)
                        scEV.inc(ins)
                    ins = eng.activation(SFL[0:1, 0:1], SFL[0:1, 0:1], AF.Silu)
                    scF.inc(ins)
                    for ci in range(8):
                        eng.wait_ge(tE.h, GT * tt + 8 + ci + 1)
                        bk = PRJ[(24 * tt + 8 + ci) % 2]
                        if ci == 0:
                            eng.wait_ge(vRP.h, 2 * tt + 2)
                            eng.activation(KF[0:64, ci, :], RT[0:64, :], AF.Silu)
                            ins = eng.activation(KF[64:128, ci, :], bk[64:128, :], AF.Silu)
                        else:
                            ins = eng.activation(KF[:, ci, :], bk[:], AF.Silu)
                        scEV.inc(ins)
                    ins = eng.activation(SFL[0:1, 0:1], SFL[0:1, 0:1], AF.Silu)
                    scF.inc(ins)
                    for gi in range(8):
                        ttile, hf = gi // 2, gi % 2
                        eng.wait_ge(tE.h, GT * tt + 16 + gi + 1)
                        bk = PRJ[(24 * tt + 16 + gi) % 2]
                        ins = eng.activation(VSB[:, ttile, hf * L:(hf + 1) * L], bk[:], AF.Silu)
                        scEV.inc(ins)
                    ins = eng.activation(SFL[0:1, 0:1], SFL[0:1, 0:1], AF.Silu)
                    scF.inc(ins)
                    for q in range(4):
                        eng.wait_ge(tE.h, GT * tt + 24 + q + 1)
                        ins = eng.activation(SFL[0:1, 0:1], SFL[0:1, 0:1], AF.Silu)
                        scR.inc(ins)
                assert scEV.n == 24 * NT

            @block.vector
            def _(eng):
                eng = _Guard(eng, vself)
                eng.wait_ge(ph.h, 5 * PH[0])
                eng.wait_ge(gMS.h, 32)
                for tt in range(NT):
                    h, jj, j = tkt(tt)
                    d = tt % 2
                    first = jj == 0
                    eng.wait_ge(sRP[d].h, 32 * (tt // 2 + 1))
                    cs_ = RCS[:, d]
                    sn_ = RSS[:, d]
                    eng.wait_ge(tE.h, GT * tt + 1)
                    bk = PRJ[(24 * tt) % 2]
                    eng.tensor_tensor(RA[:], bk[0:32, :], cs_, ALU.mult)
                    eng.tensor_tensor(RB[:], bk[32:64, :], sn_, ALU.mult)
                    eng.tensor_tensor(RT[0:32, :], RA[:], RB[:], ALU.subtract)
                    eng.tensor_tensor(RA[:], bk[0:32, :], sn_, ALU.mult)
                    eng.tensor_tensor(RB[:], bk[32:64, :], cs_, ALU.mult)
                    ins = eng.tensor_tensor(RT[32:64, :], RA[:], RB[:], ALU.add)
                    vRP.inc(ins)
                    eng.wait_ge(tE.h, GT * tt + 9)
                    eng.wait_ge(scEV.h, 24 * tt + 2)
                    bk = PRJ[(24 * tt + 8) % 2]
                    eng.tensor_tensor(RA[:], bk[0:32, :], cs_, ALU.mult)
                    eng.tensor_tensor(RB[:], bk[32:64, :], sn_, ALU.mult)
                    eng.tensor_tensor(RT[0:32, :], RA[:], RB[:], ALU.subtract)
                    eng.tensor_tensor(RA[:], bk[0:32, :], sn_, ALU.mult)
                    eng.tensor_tensor(RB[:], bk[32:64, :], cs_, ALU.mult)
                    ins = eng.tensor_tensor(RT[32:64, :], RA[:], RB[:], ALU.add)
                    vRP.inc(ins)
                    for q in range(4):
                        eng.wait_ge(tE.h, GT * tt + 24 + q + 1)
                        eng.wait_ge(scR.h, 4 * tt + q + 1)
                        bk = QTP[(4 * tt + q) % 2]
                        ins = eng.tensor_scalar(QTK[:, q, :], bk[:], GV[:, q, 0:1], None, ALU.mult)
                        vQT.inc(ins)
                    for q in range(4):
                        eng.wait_ge(tE.h, GT * tt + 28 + q + 1)
                        bk = PSH[(20 * tt + q) % 2]
                        ins = eng.tensor_tensor(PMS[:, q, :], bk[:], MSK[:, q, :], ALU.mult)
                        vPSH.inc(ins)
                    ins = eng.tensor_copy(RA[0:1, 0:1], RB[0:1, 0:1])
                    vPF.inc(ins)
                    for o in range(8):
                        kt, hf = o // 2, o % 2
                        eng.wait_ge(tE.h, GT * tt + 32 + o + 1)
                        if o == 0 and tt >= 2:
                            pt = tt - 2
                            pj = NCH - 1 - (pt % NCH)
                            ph_ = pt // NCH
                            eng.wait_ge(gA[pj].h, 64 * (ph_ + 1))
                        bk = OPS[(8 * tt + o) % 2]
                        ins = eng.tensor_scalar(OUTS[:, d, kt, hf * L:(hf + 1) * L], bk[:],
                                                GV[:, kt, 1:2], None, ALU.mult)
                        vOE.inc(ins)
                    for s in range(16):
                        c1i, hf = s // 2, s % 2
                        eng.wait_ge(tE.h, GT * tt + 40 + s + 1)
                        bk = PSH[(20 * tt + 4 + s) % 2]
                        dst = SSB[:, c1i, hf * L:(hf + 1) * L]
                        if first:
                            ins = eng.tensor_copy(dst, bk[:])
                        else:
                            ins = eng.scalar_tensor_tensor(dst, dst, GL, bk[:], ALU.mult, ALU.add)
                        vPSH.inc(ins)
            phase_barrier({
                "sync": [(sXN[0].h, sXN[0].n), (sXN[1].h, sXN[1].n),
                         (sRP[0].h, sRP[0].n), (sRP[1].h, sRP[1].n), (gW.h, gW.n)],
                "gpsimd": [(gA[j].h, gA[j].n) for j in range(NCH)] + [(gMS.h, gMS.n)],
                "tensor": [(tE.h, tE.n)],
                "vector": [(vPSH.h, vPSH.n), (vOE.h, vOE.n), (vQT.h, vQT.n)],
                "scalar": [(scEV.h, scEV.n), (scR.h, scR.n), (scF.h, scF.n)],
            })
        with ExitStack() as p2:
            sb2 = lambda n, s, d: p2.enter_context(nc.sbuf_tensor(n, s, d))
            XNB2 = sb2("XNB2", [P, 2, 2, 8, L], BF16)
            XNF2 = sb2("XNF2", [P, 8, L], F32)
            Y = sb2("Y", [P, 2, 8, L], F32)
            ATT = sb2("ATT", [P, 8, PCP], F32)
            SQA = sb2("SQA", [P, 4, PCP], BF16)
            W13B = sb2("W13B", [P, 2, 8, C], BF16)
            W2B = sb2("W2B", [P, 2, 4, C], BF16)
            SG = sb2("SG", [P, 2, L], BF16)
            H2 = sb2("H2", [P, 4, C], BF16)
            SCT = sb2("SCT", [16, PCP], F32)
            BIS = sb2("BIS", [P, 8, 16], F32)
            M8S = sb2("M8S", [P, 8], F32)
            TH = sb2("TH", [P, 1], F32)
            SEL = sb2("SEL", [P, 16], F32)
            WSL = sb2("WSL", [P, 8, 16], F32)
            SW = sb2("SW", [P, 1], F32)
            IV = sb2("IV", [P, 1], F32)
            CMT = sb2("CMT", [P, 8, 16], F32)
            CMB = sb2("CMB", [16, PCP], BF16)
            CMB1 = sb2("CMB1", [1, 2, PCP], BF16)
            CBC = sb2("CBC", [P, PCP], BF16)
            RS1 = sb2("RS1", [P, 8, 16], F32)
            RS2 = sb2("RS2", [P, 8, 16], F32)
            MU = sb2("MU", [P, 8, 16], F32)
            MSQ = sb2("MSQ", [P, 8, 16], F32)
            VAR = sb2("VAR", [P, 8, 16], F32)
            RCg = sb2("RCg", [P, 8, 16], F32)
            M2G = sb2("M2G", [P, 8, 16], F32)
            XGH = sb2("XGH", [P, 2, PCP], F32)
            RSD = sb2("RSD", [P, 8, 16], F32)

            PGEN = [p2.enter_context(nc.psum_tensor(f"PGEN{i}", [P, L], F32)) for i in range(2)]
            HPG = [p2.enter_context(nc.psum_tensor(f"HPG{i}", [P, L], F32)) for i in range(2)]
            HPU = [p2.enter_context(nc.psum_tensor(f"HPU{i}", [P, L], F32)) for i in range(2)]
            YPS = [p2.enter_context(nc.psum_tensor(f"YPS{i}", [P, L], F32)) for i in range(2)]

            csem = mksem("csem")
            gAT = mksem("gAT")
            sXB2 = [mksem("sXB2a"), mksem("sXB2b")]
            sXF = mksem("sXF")
            sY = mksem("sY")
            sW2p = [mksem("sW2a"), mksem("sW2b")]
            sCBp = [mksem("sCBa"), mksem("sCBb")]
            tG = mksem("tG")
            scS = mksem("scS")
            scR2 = mksem("scR2")
            vP2 = mksem("vP2")
            gOUT = mksem("gOUT")

            _hidx = [(cp, e, xx) for cp in range(NCP) for e in range(NE) for xx in range(8)]

            W2WAIT = {}
            CBWAIT = {}
            _w2c = [0, 0]
            _cbc = [0, 0]
            if P2_FFN:
                for _cp in range(NCP):
                    for _e in range(NE):
                        _de = _e % 2
                        _w2c[_de] += 32
                        W2WAIT[(_cp, _e)] = _w2c[_de]
                        if _e < 16:
                            _cbc[_de] += 16
                            CBWAIT[(_cp, _e)] = _cbc[_de]
            GAT_AT = {}
            _ga = 0
            for _cp in range(NCP):
                _ga += 16
                if DEBUG_DUMPS and _cp == 0:
                    _ga += 16
                GAT_AT[_cp] = _ga

            TGI = {}
            _t = 0
            for cp in range(NCP):
                def _m_gating(c):
                    nonlocal _t
                    for hf in range(2):
                        _t += 1
                        TGI[("gate", c, hf)] = _t
                    for tt8 in range(8):
                        _t += 1
                        TGI[("gtp", c, tt8)] = _t
                    for tt8 in range(8):
                        _t += 1
                        TGI[("ctp", c, tt8)] = _t
                if cp == 0:
                    _m_gating(0)
                for e in (range(NE) if P2_FFN else []):
                    if e != 16:
                        for hf in range(2):
                            _t += 1
                            TGI[("cbb", cp, e, hf)] = _t
                    for xx in range(8):
                        _t += 1
                        TGI[("hg", cp, e, xx)] = _t
                        _t += 1
                        TGI[("hu", cp, e, xx)] = _t
                    for di in range(8):
                        for hf in range(2):
                            _t += 1
                            TGI[("y", cp, e, di, hf)] = _t
                    TGI[("edone", cp, e)] = _t
                if not P2_FFN:
                    for e in range(NE):
                        TGI[("edone", cp, e)] = _t
                if cp + 1 < NCP:
                    _m_gating(cp + 1)
                if P2_ATT:
                    for ci in range(8):
                        for ah in range(2):
                            _t += 1
                            TGI[("atp", cp, ci, ah)] = _t
                TGI[("cpdone", cp)] = _t
            TG_TOTAL = _t

            SCI = {}
            SCR2 = {}
            _s = 0
            _r2 = 0
            for cp in range(NCP):
                if cp == 0:
                    for hf in range(2):
                        _s += 1
                        SCI[("sig", 0, hf)] = _s
                    for t8 in range(8):
                        _r2 += 1
                        SCR2[("gtp", 0, t8)] = _r2
                    for t8 in range(8):
                        _r2 += 1
                        SCR2[("ctp", 0, t8)] = _r2
                if P2_FFN:
                    for e in range(NE):
                        for xx in range(8):
                            _s += 1
                            SCI[("silu", cp, e, xx)] = _s
                if P2_ATT:
                    _s += 1
                    SCI[("sq1", cp)] = _s
                    _s += 1
                    SCI[("sq2", cp)] = _s
                    _s += 1
                    SCI[("sqrt", cp)] = _s
                if cp + 1 < NCP:
                    for hf in range(2):
                        _s += 1
                        SCI[("sig", cp + 1, hf)] = _s
                    for t8 in range(8):
                        _r2 += 1
                        SCR2[("gtp", cp + 1, t8)] = _r2
                    for t8 in range(8):
                        _r2 += 1
                        SCR2[("ctp", cp + 1, t8)] = _r2
                if P2_ATT:
                    for ci in range(8):
                        for ah in range(2):
                            _r2 += 1
                            SCR2[("atp", cp, ci, ah)] = _r2
            SC_TOTAL = _s

            VI = {}
            _v = 0
            for cp in range(NCP):
                def _m_vgate(c):
                    nonlocal _v
                    for tt8 in range(8):
                        _v += 1
                        VI[("bis", c, tt8)] = _v
                    for tt8 in range(8):
                        _v += 8
                        VI[("comb", c, tt8)] = _v
                    for tt8 in range(8):
                        _v += 1
                        VI[("cmbc", c, tt8)] = _v
                if cp == 0:
                    _m_vgate(0)
                for ci in range(8):
                    _v += 1
                    VI[("yinit", cp, ci)] = _v
                for e in (range(NE) if P2_FFN else []):
                    if e != 16:
                        _v += 2
                        VI[("cbc", cp, e)] = _v
                    for xx in range(8):
                        _v += 1
                        VI[("hufree", cp, e, xx)] = _v
                        if e != 16:
                            _v += 1
                    VI[("h2done", cp, e)] = _v
                    for di in range(8):
                        for hf in range(2):
                            _v += 1
                            VI[("yadd", cp, e, di, hf)] = _v
                if P2_ATT:
                    _v += 1
                    VI[("rs2a", cp)] = _v
                    _v += 2
                    VI[("gnred", cp)] = _v
                    _v += 5
                    _v += 1
                    VI[("gnrec", cp)] = _v
                    _v += 9
                    VI[("gnnorm", cp)] = _v
                if cp + 1 < NCP:
                    _m_vgate(cp + 1)
                if P2_ATT:
                    for ci in range(8):
                        for ah in range(2):
                            _v += 1
                            VI[("stt", cp, ci, ah)] = _v
                VI[("vdone", cp)] = _v
            V_TOTAL = _v

            PGEN_USES = []
            for cp in range(NCP):
                PGEN_USES.append((("gate", cp, 0), ("sc", SCI[("sig", cp, 0)])))
                PGEN_USES.append((("gate", cp, 1), ("sc", SCI[("sig", cp, 1)])))
                for tt8 in range(8):
                    PGEN_USES.append((("gtp", cp, tt8), ("v", VI[("bis", cp, tt8)])))
                for tt8 in range(8):
                    PGEN_USES.append((("ctp", cp, tt8), ("v", VI[("cmbc", cp, tt8)])))
                if P2_FFN:
                    for e in range(16):
                        PGEN_USES.append((("cbb", cp, e, 0), ("v", VI[("cbc", cp, e)])))
                        PGEN_USES.append((("cbb", cp, e, 1), ("v", VI[("cbc", cp, e)])))
            PGEN_IDX = {u[0]: i for i, u in enumerate(PGEN_USES)}

            YPS_USES = []
            for cp in range(NCP):
                if P2_FFN:
                    for e in range(NE):
                        for di in range(8):
                            for hf in range(2):
                                YPS_USES.append((("y", cp, e, di, hf), ("v", VI[("yadd", cp, e, di, hf)])))
                if P2_ATT:
                    for ci in range(8):
                        for ah in range(2):
                            YPS_USES.append((("atp", cp, ci, ah), ("v", VI[("stt", cp, ci, ah)])))
            YPS_IDX = {u[0]: i for i, u in enumerate(YPS_USES)}

            def puse_wait(eng, uses, idx_map, key):
                i = idx_map[key]
                if i >= 2:
                    kind, val = uses[i - 2][1]
                    eng.wait_ge(scS.h if kind == "sc" else vP2.h, val)

            def hpg_wait(eng, cp, e, xx):
                gi = (cp * NE + e) * 8 + xx
                if gi >= 2:
                    pcp, pe, pxx = _hidx[gi - 2]
                    eng.wait_ge(scS.h, SCI[("silu", pcp, pe, pxx)])

            def hpu_wait(eng, cp, e, xx):
                gi = (cp * NE + e) * 8 + xx
                if gi >= 2:
                    pcp, pe, pxx = _hidx[gi - 2]
                    eng.wait_ge(vP2.h, VI[("hufree", pcp, pe, pxx)])

            @block.scalar
            def _(eng):
                eng = _Guard(eng, sself)
                def emit_sig(c):
                    for hf in range(2):
                        eng.wait_ge(tG.h, TGI[("gate", c, hf)])
                        ins = eng.activation(SCT[:, hf * L:(hf + 1) * L], PGEN[hf % 2][0:16, :], AF.Sigmoid)
                        scS.inc(ins)
                    for tt8 in range(8):
                        eng.wait_ge(tG.h, TGI[("gtp", c, tt8)])
                        ins = eng.activation(SFL[0:1, 1:2], SFL[0:1, 1:2], AF.Copy)
                        scR2.inc(ins)
                    for tt8 in range(8):
                        eng.wait_ge(tG.h, TGI[("ctp", c, tt8)])
                        ins = eng.activation(SFL[0:1, 1:2], SFL[0:1, 1:2], AF.Copy)
                        scR2.inc(ins)

                for cp in range(NCP):
                    if cp == 0:
                        emit_sig(0)
                    if P2_FFN:
                        for e in range(NE):
                            for xx in range(8):
                                eng.wait_ge(tG.h, TGI[("hg", cp, e, xx)])
                                if (cp * NE + e) * 8 + xx >= 2:
                                    pcp, pe, pxx = _hidx[(cp * NE + e) * 8 + xx - 2]
                                    eng.wait_ge(vP2.h, VI[("hufree", pcp, pe, pxx)])
                                ins = eng.activation(SG[:, xx % 2, :], HPG[xx % 2][:], AF.Silu)
                                scS.inc(ins)
                    if P2_ATT:
                        eng.wait_ge(gAT.h, GAT_AT[cp])
                        ins = eng.activation(SQA[:], ATT[:, 0:4, :], AF.Square)
                        scS.inc(ins)
                        eng.wait_ge(vP2.h, VI[("rs2a", cp)])
                        ins = eng.activation(SQA[:], ATT[:, 4:8, :], AF.Square)
                        scS.inc(ins)
                        eng.wait_ge(vP2.h, VI[("gnrec", cp)])
                        ins = eng.activation(RSD[:], RCg[:], AF.Sqrt)
                        scS.inc(ins)
                    if cp + 1 < NCP:
                        emit_sig(cp + 1)
                    if P2_ATT:
                        for ci in range(8):
                            for ah in range(2):
                                eng.wait_ge(tG.h, TGI[("atp", cp, ci, ah)])
                                ins = eng.activation(SFL[0:1, 2:3], SFL[0:1, 2:3], AF.Copy)
                                scR2.inc(ins)
                assert scS.n == SC_TOTAL

            @block.gpsimd
            def _(eng):
                eng.wait_ge(ph.h, 5 * PH[0])
                pid_g = eng.snap(nc.partition_id(engines=[ET.Pool]), min_val=0, max_val=NCORES - 1)
                hg_g = pid_g % 2
                own0 = hg_g * (8 * P)
                for _j in range(NCH):
                    eng.wait_ge(gA[_j].h, 64 * NHC)
                eng.collective_compute(
                    "AllReduce", ALU.add,
                    replica_groups=[[2 * i, 2 * i + 1] for i in range(NCORES // 2)],
                    ins=[part4.opt()], outs=[cc_sum.opt()],
                ).then_inc(csem.h, 1)
                csem.n += 1
                for cp in range(NCP):
                    if cp > 0:
                        eng.wait_ge(tG.h, TGI[("cpdone", cp - 1)])
                    eng.wait_ge(csem.h, 1)
                    att2 = ATT[:].rearrange("p (c g) f -> p c (g f)", c=2)
                    eng.dma_start(out=att2,
                                  in_=cc_sum[bass.ds(own0 + cp * 2 * P, 2 * P), :].rearrange("(c p) f -> p c f", p=P)).then_inc(gAT.h, 16)
                    gAT.n += 16
                    if DEBUG_DUMPS and cp == 0:
                        eng.wait_ge(gAT.h, gAT.n)
                        eng.dma_start(out=dump_att.rearrange("(a p) c -> p a c", p=P), in_=ATT[:]).then_inc(gAT.h, 16)
                        gAT.n += 16
                    eng.wait_ge(vP2.h, VI[("vdone", cp)])
                    eng.dma_start(out=outT[cp * P:(cp + 1) * P].rearrange("p (c f) -> p c f", c=2),
                                  in_=Y[:].rearrange("p c a t -> p c (a t)")).then_inc(gOUT.h, 16)
                    gOUT.n += 16
                eng.wait_ge(gOUT.h, 16 * NCP)
                if DEBUG_DUMPS:
                    eng.dma_start(out=dump_xnf, in_=xnf_s).then_inc(gOUT.h, 16)
                    eng.dma_start(out=dump_p4, in_=part4).then_inc(gOUT.h, 16)
                    eng.dma_start(out=dump_cc, in_=cc_sum[bass.ds(own0, 2 * P), :]).then_inc(gOUT.h, 16)
                    gOUT.n += 48
                    eng.wait_ge(gOUT.h, gOUT.n)

            @block.sync
            def _(eng):
                eng.wait_ge(ph.h, 5 * PH[0])
                pid_y = eng.snap(nc.partition_id(engines=[ET.SP]), min_val=0, max_val=NCORES - 1)
                own_r0 = (pid_y % 2) * (8 * P)
                wslot_prev = [None, None]
                eng.wait_ge(gCc.h, 32)
                for cp in range(NCP):
                    d = cp % 2
                    if P2_FFN:
                        if cp >= 2:
                            eng.wait_ge(tG.h, TGI[("edone", cp - 2, NE - 1)])
                        eng.dma_start(out=XNB2[:, d].rearrange("p c a t -> p c (a t)"),
                                      in_=xnb_s[bass.ds(own_r0 + cp * 2 * P, 2 * P), :].rearrange("(c p) f -> p c f", p=P)).then_inc(sXB2[d].h, 16)
                        sXB2[d].n += 16

                    def load_xnf(hf, cp=cp):
                        if cp > 0 or hf > 0:
                            pk = ("gate", cp, hf - 1) if hf > 0 else ("gate", cp - 1, 1)
                            eng.wait_ge(tG.h, TGI[pk])
                        eng.dma_start(out=XNF2[:].rearrange("p a t -> p (a t)"),
                                      in_=xnf_s[bass.ds(own_r0 + (cp * 2 + hf) * P, P), :]).then_inc(sXF.h, 16)
                        sXF.n += 16

                    def load_w(e, cp=cp):
                        de = e % 2
                        if wslot_prev[de] is not None:
                            pcp, pe = wslot_prev[de]
                            eng.wait_ge(tG.h, TGI[("edone", pcp, pe)])
                        wslot_prev[de] = (cp, e)
                        eng.dma_start(out=W13B[:, de].rearrange("p a c -> p (a c)"),
                                      in_=w13b_s[e * P:(e + 1) * P]).then_inc(sW2p[de].h, 16)
                        eng.dma_start(out=W2B[:, de].rearrange("p a c -> p (a c)"),
                                      in_=w2b_s[e * P:(e + 1) * P]).then_inc(sW2p[de].h, 16)
                        sW2p[de].n += 32

                    def load_cmb1(e, cp=cp):
                        de = e % 2
                        eng.wait_ge(vP2.h, VI[("cmbc", cp, 7)])
                        if cp * 16 + e >= 2:
                            eng.wait_ge(tG.h, TGI[("cbb", cp if e >= 2 else cp - 1, (e - 2) % 16, 1)])
                        eng.dma_start(out=CMB1[:, de], in_=CMB[e:e + 1, :]).then_inc(sCBp[de].h, 16)
                        sCBp[de].n += 16

                    load_xnf(0)
                    if P2_FFN:
                        load_w(0)
                        load_w(1)
                        load_xnf(1)
                        load_cmb1(0)
                        load_cmb1(1)
                        if cp > 0:
                            eng.wait_ge(gOUT.h, 16 * cp)
                        eng.dma_start(out=Y[:].rearrange("p c a t -> p c (a t)"),
                                      in_=xT[bass.ds(own_r0 + cp * 2 * P, 2 * P), :].rearrange("(c p) f -> p c f", p=P)).then_inc(sY.h, 16)
                        sY.n += 16
                        for e in range(NE):
                            if 2 <= e < 16:
                                load_cmb1(e)
                            if e + 2 < NE:
                                load_w(e + 2)
                    else:
                        load_xnf(1)
                        if cp > 0:
                            eng.wait_ge(gOUT.h, 16 * cp)
                        eng.dma_start(out=Y[:].rearrange("p c a t -> p c (a t)"),
                                      in_=xT[bass.ds(own_r0 + cp * 2 * P, 2 * P), :].rearrange("(c p) f -> p c f", p=P)).then_inc(sY.h, 16)
                        sY.n += 16
                eng.wait_ge(gOUT.h, 16 * NCP)

            @block.tensor
            def _(eng):
                eng.wait_ge(ph.h, 5 * PH[0])
                eng.wait_ge(oneS.h, 1)

                def emit_gating(c):
                    for hf in range(2):
                        eng.wait_ge(sXF.h, 16 * (2 * c + hf + 1))
                        puse_wait(eng, PGEN_USES, PGEN_IDX, ("gate", c, hf))
                        bk = PGEN[hf % 2]
                        for kci in range(8):
                            ins = eng.matmul(bk[0:16, :], GWT[:, kci, :], XNF2[:, kci],
                                             start=(kci == 0), stop=(kci == 7))
                        tG.inc(ins)
                    for tt8 in range(8):
                        eng.wait_ge(scS.h, SCI[("sig", c, 1)])
                        puse_wait(eng, PGEN_USES, PGEN_IDX, ("gtp", c, tt8))
                        bk = PGEN[tt8 % 2]
                        ins = eng.transpose(bk[:, 0:16], SCT[:, tt8 * P:(tt8 + 1) * P], IDF[0:16, 0:16])
                        tG.inc(ins)
                    for tt8 in range(8):
                        eng.wait_ge(vP2.h, VI[("comb", c, 7)])
                        puse_wait(eng, PGEN_USES, PGEN_IDX, ("ctp", c, tt8))
                        bk = PGEN[tt8 % 2]
                        ins = eng.transpose(bk[0:16, 0:P], CMT[:, tt8, :], IDF[:])
                        tG.inc(ins)

                for cp in range(NCP):
                    d = cp % 2
                    if cp == 0:
                        emit_gating(0)
                    for e in (range(NE) if P2_FFN else []):
                        de = e % 2
                        eng.wait_ge(sW2p[de].h, W2WAIT[(cp, e)])
                        if e != 16:
                            eng.wait_ge(sCBp[de].h, CBWAIT[(cp, e)])
                            for hf in range(2):
                                puse_wait(eng, PGEN_USES, PGEN_IDX, ("cbb", cp, e, hf))
                                ins = eng.matmul(PGEN[hf % 2][:], ONE1[:],
                                                 CMB1[0:1, de, hf * L:(hf + 1) * L],
                                                 start=True, stop=True)
                                tG.inc(ins)
                        for xx in range(8):
                            i, hf = xx // 2, xx % 2
                            if e == 0 and xx == 0:
                                eng.wait_ge(sXB2[d].h, 16 * (cp // 2 + 1))
                            hpg_wait(eng, cp, e, xx)
                            bkg = HPG[xx % 2]
                            for kci in range(8):
                                ins = eng.matmul(bkg[:], W13B[:, de, kci, i * P:(i + 1) * P],
                                                 XNB2[:, d, hf, kci, :],
                                                 start=(kci == 0), stop=(kci == 7))
                            tG.inc(ins)
                            hpu_wait(eng, cp, e, xx)
                            bku = HPU[xx % 2]
                            for kci in range(8):
                                ins = eng.matmul(bku[:], W13B[:, de, kci, (i + 4) * P:(i + 5) * P],
                                                 XNB2[:, d, hf, kci, :],
                                                 start=(kci == 0), stop=(kci == 7))
                            tG.inc(ins)
                        for di in range(8):
                            for hf in range(2):
                                eng.wait_ge(vP2.h, VI[("h2done", cp, e)])
                                puse_wait(eng, YPS_USES, YPS_IDX, ("y", cp, e, di, hf))
                                bk = YPS[(2 * di + hf) % 2]
                                for ii in range(4):
                                    ins = eng.matmul(bk[:], W2B[:, de, ii, di * P:(di + 1) * P],
                                                     H2[:, ii, hf * L:(hf + 1) * L],
                                                     start=(ii == 0), stop=(ii == 3))
                                tG.inc(ins)
                    if not P2_FFN and cp == 0:
                        pass
                    if cp + 1 < NCP:
                        emit_gating(cp + 1)
                    if P2_ATT:
                        eng.wait_ge(vP2.h, VI[("gnnorm", cp)])
                    for ci in (range(8) if P2_ATT else []):
                        for ah in range(2):
                            puse_wait(eng, YPS_USES, YPS_IDX, ("atp", cp, ci, ah))
                            bk = YPS[(2 * ci + ah) % 2]
                            for a in range(4):
                                aa = ah * 4 + a
                                ins = eng.transpose(bk[:, a * P:(a + 1) * P],
                                                    ATT[:, aa, ci * P:(ci + 1) * P], IDF[:])
                            tG.inc(ins)
                assert tG.n == TG_TOTAL

            @block.vector
            def _(eng):
                eng = _Guard(eng, vself)
                eng.wait_ge(ph.h, 5 * PH[0])

                def emit_vgate(c):
                    for tt8 in range(8):
                        eng.wait_ge(tG.h, TGI[("gtp", c, tt8)])
                        eng.wait_ge(scR2.h, SCR2[("gtp", c, tt8)])
                        bk = PGEN[tt8 % 2]
                        ins = eng.tensor_tensor(BIS[:, tt8, :], bk[:, 0:16], GBT[:], ALU.add)
                        vP2.inc(ins)
                    for tt8 in range(8):
                        ins = eng.max(out=M8S[:], in_=BIS[:, tt8, :])
                        vP2.inc(ins)
                        ins = eng.tensor_copy(TH[:], M8S[:, 3:4])
                        vP2.inc(ins)
                        ins = eng.tensor_scalar(SEL[:], BIS[:, tt8, :], TH[:, 0:1], None, ALU.is_ge)
                        vP2.inc(ins)
                        ins = eng.tensor_tensor(WSL[:, tt8, :], BIS[:, tt8, :], GBT[:], ALU.subtract)
                        vP2.inc(ins)
                        ins = eng.tensor_tensor(WSL[:, tt8, :], WSL[:, tt8, :], SEL[:], ALU.mult)
                        vP2.inc(ins)
                        ins = eng.reduce_sum(SW[:], WSL[:, tt8, :], axis=AX.X)
                        vP2.inc(ins)
                        ins = eng.reciprocal(IV[:], SW[:])
                        vP2.inc(ins)
                        ins = eng.tensor_scalar(CMT[:, tt8, :], WSL[:, tt8, :], IV[:, 0:1], 2.5,
                                                ALU.mult, ALU.mult)
                        vP2.inc(ins)
                        assert vP2.n == VI[("comb", c, tt8)]
                    for tt8 in range(8):
                        eng.wait_ge(tG.h, TGI[("ctp", c, tt8)])
                        eng.wait_ge(scR2.h, SCR2[("ctp", c, tt8)])
                        bk = PGEN[tt8 % 2]
                        ins = eng.tensor_copy(CMB[:, tt8 * P:(tt8 + 1) * P], bk[0:16, 0:P])
                        vP2.inc(ins)
                        assert vP2.n == VI[("cmbc", c, tt8)]

                for cp in range(NCP):
                    d = cp % 2
                    if cp == 0:
                        emit_vgate(0)
                    eng.wait_ge(sY.h, 16 * (cp + 1))
                    for ci in range(8):
                        ins = eng.tensor_scalar(Y[:, :, ci, :], Y[:, :, ci, :], GNB[:, ci, 0:1], None, ALU.add)
                        vP2.inc(ins)
                    for e in (range(NE) if P2_FFN else []):
                        if e != 16:
                            for hf in range(2):
                                eng.wait_ge(tG.h, TGI[("cbb", cp, e, hf)])
                                ins = eng.tensor_copy(CBC[:, hf * L:(hf + 1) * L], PGEN[hf % 2][:])
                                vP2.inc(ins)
                            assert vP2.n == VI[("cbc", cp, e)]
                        for xx in range(8):
                            i, hf = xx // 2, xx % 2
                            eng.wait_ge(scS.h, SCI[("silu", cp, e, xx)])
                            eng.wait_ge(tG.h, TGI[("hu", cp, e, xx)])
                            ins = eng.tensor_tensor(H2[:, i, hf * L:(hf + 1) * L],
                                                    SG[:, xx % 2, :], HPU[xx % 2][:], ALU.mult)
                            vP2.inc(ins)
                            if e != 16:
                                ins = eng.tensor_tensor(H2[:, i, hf * L:(hf + 1) * L],
                                                        H2[:, i, hf * L:(hf + 1) * L],
                                                        CBC[:, hf * L:(hf + 1) * L], ALU.mult)
                                vP2.inc(ins)
                        assert vP2.n == VI[("h2done", cp, e)]
                        for di in range(8):
                            for hf in range(2):
                                eng.wait_ge(tG.h, TGI[("y", cp, e, di, hf)])
                                bk = YPS[(2 * di + hf) % 2]
                                ins = eng.tensor_tensor(Y[:, hf, di, :],
                                                        Y[:, hf, di, :], bk[:], ALU.add)
                                vP2.inc(ins)
                    if P2_ATT:
                        eng.wait_ge(scS.h, SCI[("sq1", cp)])
                        att4 = ATT[:].rearrange("p a (g k) -> p a g k", k=64)
                        sqa4 = SQA[:].rearrange("p a (g k) -> p a g k", k=64)
                        ins = eng.reduce_sum(RS2[:, 0:4, :], sqa4, axis=AX.X)
                        vP2.inc(ins)
                        assert vP2.n == VI[("rs2a", cp)]
                        eng.wait_ge(scS.h, SCI[("sq2", cp)])
                        ins = eng.reduce_sum(RS1[:], att4, axis=AX.X)
                        vP2.inc(ins)
                        ins = eng.reduce_sum(RS2[:, 4:8, :], sqa4, axis=AX.X)
                        vP2.inc(ins)
                        ins = eng.tensor_scalar(MU[:], RS1[:], 1.0 / 64, None, ALU.mult)
                        vP2.inc(ins)
                        ins = eng.tensor_scalar(MSQ[:], RS2[:], 1.0 / 64, None, ALU.mult)
                        vP2.inc(ins)
                        ins = eng.tensor_tensor(M2G[:], MU[:], MU[:], ALU.mult)
                        vP2.inc(ins)
                        ins = eng.tensor_tensor(VAR[:], MSQ[:], M2G[:], ALU.subtract)
                        vP2.inc(ins)
                        ins = eng.tensor_scalar(RS1[:], VAR[:], EPS, None, ALU.add)
                        vP2.inc(ins)
                        ins = eng.reciprocal(RCg[:], RS1[:])
                        vP2.inc(ins)
                        assert vP2.n == VI[("gnrec", cp)]
                        eng.wait_ge(scS.h, SCI[("sqrt", cp)])
                        for ah in range(4):
                            sl = slice(2 * ah, 2 * ah + 2)
                            a4 = ATT[:, sl, :].rearrange("p a (g k) -> p a g k", k=64)
                            x4 = XGH[:].rearrange("p a (g k) -> p a g k", k=64)
                            mub = MU[:, sl, :].rearrange("p a (g o) -> p a g o", o=1).to_broadcast([P, 2, 16, 64])
                            rsb = RSD[:, sl, :].rearrange("p a (g o) -> p a g o", o=1).to_broadcast([P, 2, 16, 64])
                            ins = eng.tensor_tensor(x4, a4, mub, ALU.subtract)
                            vP2.inc(ins)
                            ins = eng.tensor_tensor(a4, x4, rsb, ALU.mult)
                            vP2.inc(ins)
                        ins = eng.tensor_copy(TH[:], TH[:])
                        vP2.inc(ins)
                        assert vP2.n == VI[("gnnorm", cp)]
                    if cp + 1 < NCP:
                        emit_vgate(cp + 1)
                    for ci in (range(8) if P2_ATT else []):
                        for ah in range(2):
                            eng.wait_ge(tG.h, TGI[("atp", cp, ci, ah)])
                            eng.wait_ge(scR2.h, SCR2[("atp", cp, ci, ah)])
                            bk = YPS[(2 * ci + ah) % 2]
                            ins = eng.scalar_tensor_tensor(Y[:, ah, ci, :], bk[:],
                                                           GNW[:, ci, 0:1],
                                                           Y[:, ah, ci, :],
                                                           ALU.mult, ALU.add)
                            vP2.inc(ins)
                    assert vP2.n == VI[("vdone", cp)]
                assert vP2.n == V_TOTAL

    return nc



def kernel(L0_attn_X_Q, L0_attn_W_Q, L0_attn_W_K, L0_attn_W_V,
           L0_attn_gn_weight, L0_attn_gn_bias,
           L0_ffn_input_ids, L0_ffn_gate_w, L0_ffn_gate_b,
           L0_ffn_w13, L0_ffn_w2, L0_ffn_w13_s, L0_ffn_w2_s):
    x = np.asarray(L0_attn_X_Q, np.float32)
    _ = L0_ffn_input_ids
    consts = _host_constants()
    in_maps = []
    for c in range(NCORES):
        in_maps.append(_prep_core_inputs(
            c, x, np.asarray(L0_attn_W_Q, np.float32), np.asarray(L0_attn_W_K, np.float32),
            np.asarray(L0_attn_W_V, np.float32),
            np.asarray(L0_attn_gn_weight, np.float32), np.asarray(L0_attn_gn_bias, np.float32),
            np.asarray(L0_ffn_gate_w, np.float32), np.asarray(L0_ffn_gate_b, np.float32),
            np.asarray(L0_ffn_w13, np.float32), np.asarray(L0_ffn_w2, np.float32),
            np.asarray(L0_ffn_w13_s, np.float32), np.asarray(L0_ffn_w2_s, np.float32),
            consts))
    nc = build_program()
    res = run_bass_kernel_spmd(nc, in_maps, core_ids=list(range(NCORES)))
    out = np.empty((B, T, C), np.float32)
    for c in range(NCORES):
        b, hg = c // 2, c % 2
        r = np.asarray(res.results[c]["outT"], np.float32).reshape(NCP, P, 2, 8, L)
        out[b, hg * HALF:(hg + 1) * HALF, :] = r.transpose(0, 2, 4, 3, 1).reshape(HALF, C)
    return out


# version 10
# speedup vs baseline: 1.1736x; 1.1736x over previous
import math
from contextlib import ExitStack

import numpy as np

import concourse.bass as bass
import concourse.mybir as mybir
from concourse.bass_utils import run_bass_kernel_spmd

F32 = mybir.dt.float32
BF16 = mybir.dt.bfloat16
AF = mybir.ActivationFunctionType
ALU = mybir.AluOpType
AX = mybir.AxisListType
ET = mybir.EngineType

B, T, C = 4, 8192, 1024
P = 128
H = 8
NHC = 4
L = 512
NCH = T // L
GAMMA = 0.96875
GL = GAMMA ** L
HALF = T // 2
PCP = 1024
NCP = HALF // PCP
NE = 17
INT = 512
EPS = 1e-5
ROPE_DIM = 64
NCORES = 8

DEBUG_DUMPS = False
P2_FFN = True
P2_ATT = True
CC_SHARED = True



def _rope_perm():
    p = list(range(0, ROPE_DIM, 2)) + list(range(1, ROPE_DIM, 2)) + list(range(ROPE_DIM, C))
    return np.array(p, dtype=np.int64)


def _host_constants():
    inv_freq = 1.0 / 10000.0 ** (np.arange(0, ROPE_DIM, 2, dtype=np.float64) / ROPE_DIM)
    t = np.arange(T, dtype=np.float64)
    ang = inv_freq[:, None] * t[None, :]
    ropeC = np.cos(ang).astype(np.float32)
    ropeS = np.sin(ang).astype(np.float32)
    q = np.arange(L)[:, None].astype(np.float64)
    k = np.arange(L)[None, :].astype(np.float64)
    maskP = np.where(q >= k, GAMMA ** (q - L), 0.0).astype(np.float32)
    r = np.arange(L, dtype=np.float64)
    gvec = np.stack([GAMMA ** r, GAMMA ** (L - r)], axis=1).astype(np.float32)
    identf = np.eye(P, dtype=np.float32)
    onesC = np.full((P, P), 1.0 / C, dtype=np.float32)
    return ropeC, ropeS, maskP, gvec, identf, onesC


def _prep_core_inputs(c, x, wq_full, wk_full, wv_full, gnw, gnb, gate_w, gate_b,
                      w13, w2, w13_s, w2_s, consts):
    b, hg = c // 2, c % 2
    heads = [hg * NHC + k for k in range(NHC)]
    perm = _rope_perm()
    ropeC, ropeS, maskP, gvec, identf, onesC = consts

    xT = np.ascontiguousarray(
        x[b].reshape(NCH, L, 8, P).transpose(0, 3, 2, 1).reshape(NCH * P, 8 * L))
    wq = np.concatenate([wq_full[:, h * C:(h + 1) * C][:, perm] for h in heads], axis=1)
    wq = np.ascontiguousarray(wq.reshape(8, P, NHC, C).transpose(2, 1, 0, 3).reshape(NHC * P, 8 * C))
    wk = np.concatenate([wk_full[:, h * C:(h + 1) * C][:, perm] for h in heads], axis=1)
    wk = np.ascontiguousarray(wk.reshape(8, P, NHC, C).transpose(2, 1, 0, 3).reshape(NHC * P, 8 * C))
    wv = np.concatenate([wv_full[:, h * C:(h + 1) * C] for h in heads], axis=1)
    wv = np.ascontiguousarray(wv.reshape(8, P, NHC, C).transpose(2, 1, 0, 3).reshape(NHC * P, 8 * C))
    w13T = np.stack([w13[e].T for e in range(16)] + [w13_s.T])
    w13T = np.ascontiguousarray(w13T.reshape(NE, 8, P, C).transpose(0, 2, 1, 3).reshape(NE * P, 8 * C))
    w2T = np.stack([w2[e].T for e in range(16)] + [w2_s.T])
    w2T = np.ascontiguousarray(w2T.reshape(NE, 4, P, C).transpose(0, 2, 1, 3).reshape(NE * P, 4 * C))
    return {
        "xT": xT, "wq": wq, "wk": wk, "wv": wv,
        "w13T": w13T, "w2T": w2T,
        "gwT": np.ascontiguousarray(gate_w.T),
        "gbt": np.ascontiguousarray(np.tile(gate_b[None, :], (P, 1))),
        "gnw": np.ascontiguousarray(gnw.reshape(C, 1)),
        "gnb": np.ascontiguousarray(gnb.reshape(C, 1)),
        "identf": identf, "onesC": onesC,
        "ropeC": ropeC, "ropeS": ropeS, "maskP": maskP, "gvec": gvec,
    }


_ROLE_TGT = {}


class Sem:
    def __init__(self, nc, ctx, name):
        self.h = ctx.enter_context(nc.semaphore(name))
        self.n = 0

    def inc(self, instr, k=1):
        instr.then_inc(self.h, k)
        self.n += k
        _ROLE_TGT[instr.ins.name] = (self.h, self.n)
        return self.n


class _Guard:

    def __init__(self, eng, sem):
        self._eng = eng
        self._sem = sem

    def _settle(self):
        p = getattr(self._sem, "pend", None)
        if p is None:
            return
        self._sem.pend = None
        tgt = _ROLE_TGT.pop(p.ins.name, None)
        if tgt is None:
            p.then_inc(self._sem.h, 1)
            self._sem.n += 1
            tgt = (self._sem.h, self._sem.n)
        self._eng.wait_ge(tgt[0], tgt[1])

    def wait_ge(self, h, v):
        self._eng.wait_ge(h, v)

    def nop(self):
        self._settle()
        return self._eng.nop()

    def dma_start(self, *a, **kw):
        return self._eng.dma_start(*a, **kw)

    def snap(self, *a, **kw):
        return self._eng.snap(*a, **kw)

    def __getattr__(self, name):
        f = getattr(self._eng, name)

        def wrap(*a, **kw):
            self._settle()
            ins = f(*a, **kw)
            if hasattr(ins, "then_inc"):
                self._sem.pend = ins
            return ins

        return wrap



def build_program():
    nc = bass.Bass("TRN2", target_bir_lowering=False, debug=False)

    xT = nc.dram_tensor("xT", [NCH * P, 8 * L], F32, kind="ExternalInput").ap()
    wq = nc.dram_tensor("wq", [NHC * P, 8 * C], F32, kind="ExternalInput").ap()
    wk = nc.dram_tensor("wk", [NHC * P, 8 * C], F32, kind="ExternalInput").ap()
    wv = nc.dram_tensor("wv", [NHC * P, 8 * C], F32, kind="ExternalInput").ap()
    w13T = nc.dram_tensor("w13T", [NE * P, 8 * C], F32, kind="ExternalInput").ap()
    w2T = nc.dram_tensor("w2T", [NE * P, 4 * C], F32, kind="ExternalInput").ap()
    gwT = nc.dram_tensor("gwT", [C, 16], F32, kind="ExternalInput").ap()
    gbt = nc.dram_tensor("gbt", [P, 16], F32, kind="ExternalInput").ap()
    gnw = nc.dram_tensor("gnw", [C, 1], F32, kind="ExternalInput").ap()
    gnb = nc.dram_tensor("gnb", [C, 1], F32, kind="ExternalInput").ap()
    identf = nc.dram_tensor("identf", [P, P], F32, kind="ExternalInput").ap()
    onesC = nc.dram_tensor("onesC", [P, P], F32, kind="ExternalInput").ap()
    ropeC = nc.dram_tensor("ropeC", [32, T], F32, kind="ExternalInput").ap()
    ropeS = nc.dram_tensor("ropeS", [32, T], F32, kind="ExternalInput").ap()
    maskP = nc.dram_tensor("maskP", [L, L], F32, kind="ExternalInput").ap()
    gvec = nc.dram_tensor("gvec", [L, 2], F32, kind="ExternalInput").ap()

    outT = nc.dram_tensor("outT", [NCP * P, 2 * 8 * L], F32, kind="ExternalOutput").ap()

    xnb_s = nc.dram_tensor("xnb_s", [NCH * P, 8 * L], BF16, kind="Internal").ap()
    xnf_s = nc.dram_tensor("xnf_s", [NCH * P, 8 * L], F32, kind="Internal").ap()
    part4 = nc.dram_tensor("part4", [NCH * P, 4 * C], F32, kind="Internal").ap()
    wqb_s = nc.dram_tensor("wqb_s", [NHC * P, 8 * C], BF16, kind="Internal").ap()
    wkb_s = nc.dram_tensor("wkb_s", [NHC * P, 8 * C], BF16, kind="Internal").ap()
    wvb_s = nc.dram_tensor("wvb_s", [NHC * P, 8 * C], BF16, kind="Internal").ap()
    w13b_s = nc.dram_tensor("w13b_s", [NE * P, 8 * C], BF16, kind="Internal").ap()
    w2b_s = nc.dram_tensor("w2b_s", [NE * P, 4 * C], BF16, kind="Internal").ap()
    ropeCb = nc.dram_tensor("ropeCb", [32, T], BF16, kind="Internal").ap()
    ropeSb = nc.dram_tensor("ropeSb", [32, T], BF16, kind="Internal").ap()
    cc_sum = nc.dram_tensor("cc_sum", [NCH * P, 4 * C], F32, kind="Internal").ap()

    if DEBUG_DUMPS:
        dump_xnf = nc.dram_tensor("dump_xnf", [C, T], F32, kind="ExternalOutput").ap()
        dump_p4 = nc.dram_tensor("dump_p4", [T, C], F32, kind="ExternalOutput").ap()
        dump_att = nc.dram_tensor("dump_att", [PCP, C], F32, kind="ExternalOutput").ap()
        dump_att2 = nc.dram_tensor("dump_att2", [PCP, C], F32, kind="ExternalOutput").ap()
        dump_cc = nc.dram_tensor("dump_cc", [PCP, C], F32, kind="ExternalOutput").ap()

    dram_xT_chunk = lambda j: xT[j * P:(j + 1) * P]
    dram_xnb_chunk = lambda j: xnb_s[j * P:(j + 1) * P]
    dram_xnf_chunk = lambda j: xnf_s[j * P:(j + 1) * P]

    with ExitStack() as ctx:
        sb = lambda name, shape, dt: ctx.enter_context(nc.sbuf_tensor(name, shape, dt))
        mksem = lambda name: Sem(nc, ctx, name)

        IDB = sb("IDB", [P, P], BF16)
        IDF = sb("IDF", [P, P], F32)
        ONB = sb("ONB", [P, P], BF16)
        GWT = sb("GWT", [P, 8, 16], F32)
        GBT = sb("GBT", [P, 16], F32)
        GNW = sb("GNW", [P, 8, 1], F32)
        GNB = sb("GNB", [P, 8, 1], F32)
        ONE1 = sb("ONE1", [1, P], BF16)
        SFL = sb("SFL", [1, 4], F32)

        ph = mksem("ph")
        vself = mksem("vself")
        sself = mksem("sself")
        oneS = mksem("oneS")
        PH = [0]
        sC = mksem("sC")
        gCa = mksem("gCa")
        gCb = mksem("gCb")
        gCc = mksem("gCc")
        gCbl = [None] + [mksem(f"gCbl{h}") for h in range(1, 4)]

        block = ctx.enter_context(nc.Block())

        @block.sync
        def _(eng):
            eng.dma_start(out=IDF[:], in_=identf).then_inc(sC.h, 16)
            eng.dma_start(out=GWT[:], in_=gwT.rearrange("(a p) e -> p a e", p=P)).then_inc(sC.h, 16)
            eng.dma_start(out=GBT[:], in_=gbt).then_inc(sC.h, 16)
            eng.dma_start(out=GNW[:], in_=gnw.rearrange("(a p) o -> p a o", p=P), allow_slow_non_contiguous=True).then_inc(sC.h, 16)
            eng.dma_start(out=GNB[:], in_=gnb.rearrange("(a p) o -> p a o", p=P), allow_slow_non_contiguous=True).then_inc(sC.h, 16)
            sC.n += 80

        @block.gpsimd
        def _(eng):
            eng.dma_start(out=IDB[:], in_=identf).then_inc(gCa.h, 16)
            eng.dma_start(out=ONB[:], in_=onesC).then_inc(gCa.h, 16)
            gCa.n += 32
            eng.memset(ONE1[:], 1.0)
            eng.memset(SFL[:], 0.0).then_inc(oneS.h, 1)
            oneS.n += 1
            eng.dma_start(out=wqb_s[0:P], in_=wq[0:P]).then_inc(gCb.h, 16)
            eng.dma_start(out=wkb_s[0:P], in_=wk[0:P]).then_inc(gCb.h, 16)
            eng.dma_start(out=wvb_s[0:P], in_=wv[0:P]).then_inc(gCb.h, 16)
            eng.dma_start(out=ropeCb, in_=ropeC).then_inc(gCb.h, 16)
            eng.dma_start(out=ropeSb, in_=ropeS).then_inc(gCb.h, 16)
            gCb.n += 80

        def phase_barrier(waits=None):
            PH[0] += 1
            decs = {"sync": block.sync, "gpsimd": block.gpsimd, "tensor": block.tensor,
                    "vector": block.vector, "scalar": block.scalar}
            for name, dec in decs.items():
                wl = (waits or {}).get(name, [])

                @dec
                def _(eng, wl=wl):
                    for (sh, sv) in wl:
                        eng.wait_ge(sh, sv)
                    eng.nop().then_inc(ph.h, 1)

        with ExitStack() as p0:
            sb0 = lambda n, s, d: p0.enter_context(nc.sbuf_tensor(n, s, d))
            XF = sb0("XF", [P, 4, 8, L], F32)
            XB = sb0("XB", [P, 4, 8, L], BF16)
            SQ = sb0("SQ", [P, 2, 8, L], BF16)
            MSB = sb0("MSB", [P, L], F32)
            M2 = sb0("M2", [P, L], F32)
            VEPS = sb0("VEPS", [P, L], F32)
            REC = sb0("REC", [P, L], F32)
            RSTD = sb0("RSTD", [P, L], F32)
            T1 = sb0("T1", [P, 8, L], F32)
            XNB = sb0("XNB", [P, 2, 8, L], BF16)
            XNF = sb0("XNF", [P, 2, 8, L], F32)
            PM = [p0.enter_context(nc.psum_tensor(f"PM{i}", [P, L], F32)) for i in range(2)]
            PQ = [p0.enter_context(nc.psum_tensor(f"PQ{i}", [P, L], F32)) for i in range(2)]

            s_x = [mksem(f"s_x{i}") for i in range(4)]
            sc_ca = mksem("sc_ca")
            sc_sq = mksem("sc_sq")
            t_pm = mksem("t_pm")
            v_rc = mksem("v_rc")
            sc_rs = mksem("sc_rs")
            v_dn = mksem("v_dn")
            sc_xb = mksem("sc_xb")
            sSTB = mksem("sSTB")
            vSTF = mksem("vSTF")

            order = list(range(NCH - 1, -1, -1))

            @block.sync
            def _(eng):
                eng.wait_ge(sC.h, sC.n)
                for i, j in enumerate(order):
                    d4 = i % 4
                    if i >= 4:
                        eng.wait_ge(v_dn.h, i - 3)
                        eng.wait_ge(sc_sq.h, i - 3)
                    eng.dma_start(out=XF[:, d4].rearrange("p a t -> p (a t)"), in_=dram_xT_chunk(j)).then_inc(s_x[d4].h, 16)
                    s_x[d4].n += 16

            @block.gpsimd
            def _(eng):
                eng.wait_ge(v_dn.h, 4)
                for hh in range(1, NHC):
                    eng.dma_start(out=wqb_s[hh * P:(hh + 1) * P], in_=wq[hh * P:(hh + 1) * P]).then_inc(gCbl[hh].h, 16)
                    eng.dma_start(out=wkb_s[hh * P:(hh + 1) * P], in_=wk[hh * P:(hh + 1) * P]).then_inc(gCbl[hh].h, 16)
                    eng.dma_start(out=wvb_s[hh * P:(hh + 1) * P], in_=wv[hh * P:(hh + 1) * P]).then_inc(gCbl[hh].h, 16)
                    gCbl[hh].n += 48
                eng.wait_ge(v_dn.h, NCH)
                eng.dma_start(out=w13b_s, in_=w13T).then_inc(gCc.h, 16)
                eng.dma_start(out=w2b_s, in_=w2T).then_inc(gCc.h, 16)
                gCc.n += 32

            @block.scalar
            def _(eng):
                eng = _Guard(eng, sself)
                eng.wait_ge(oneS.h, 1)

                def cast(i):
                    d4 = i % 4
                    eng.wait_ge(s_x[d4].h, 16 * (i // 4 + 1))
                    if i >= 4:
                        eng.wait_ge(t_pm.h, i - 3)
                    ins = eng.activation(XB[:, i % 4], XF[:, i % 4], AF.Copy)
                    sc_ca.inc(ins)

                cast(0)
                for i in range(NCH):
                    if i + 1 < NCH:
                        cast(i + 1)
                    d = i % 2
                    if i >= 2:
                        eng.wait_ge(t_pm.h, i - 1)
                    d4 = i % 4
                    ins = eng.activation(SQ[:, d], XB[:, d4], AF.Square)
                    sc_sq.inc(ins)
                    eng.wait_ge(v_rc.h, i + 1)
                    ins = eng.activation(RSTD[:], REC[:], AF.Sqrt)
                    sc_rs.inc(ins)
                    if i >= 2:
                        eng.wait_ge(sSTB.h, 16 * (i - 1))
                    eng.wait_ge(v_dn.h, i + 1)
                    ins = eng.activation(XNB[:, d], XNF[:, d], AF.Copy)
                    sc_xb.inc(ins)
                    eng.wait_ge(sc_xb.h, sc_xb.n)
                    eng.dma_start(out=dram_xnb_chunk(NCH - 1 - i),
                                  in_=XNB[:, d].rearrange("p a t -> p (a t)")).then_inc(sSTB.h, 16)
                    sSTB.n += 16
                    eng.dma_start(out=dram_xnf_chunk(NCH - 1 - i),
                                  in_=XNF[:, d].rearrange("p a t -> p (a t)")).then_inc(vSTF.h, 16)
                    vSTF.n += 16

            @block.tensor
            def _(eng):
                eng.wait_ge(gCa.h, 32)
                for i in range(NCH):
                    d = i % 2
                    d4 = i % 4
                    eng.wait_ge(sc_ca.h, i + 1)
                    eng.wait_ge(sc_sq.h, i + 1)
                    if i >= 2:
                        eng.wait_ge(v_dn.h, i - 1)
                    for ci in range(8):
                        eng.matmul(PM[d][:], ONB[:], XB[:, d4, ci], start=(ci == 0), stop=(ci == 7))
                    for ci in range(8):
                        ins = eng.matmul(PQ[d][:], ONB[:], SQ[:, d, ci], start=(ci == 0), stop=(ci == 7))
                    ins.then_inc(t_pm.h, 1)
                    t_pm.n += 1

            @block.vector
            def _(eng):
                eng = _Guard(eng, vself)
                for i in range(NCH):
                    d = i % 2
                    eng.wait_ge(t_pm.h, i + 1)
                    eng.tensor_copy(MSB[:], PM[d][:])
                    eng.tensor_tensor(M2[:], MSB[:], MSB[:], ALU.mult)
                    eng.scalar_tensor_tensor(VEPS[:], M2[:], -1.0, PQ[d][:], ALU.mult, ALU.add)
                    eng.tensor_scalar(VEPS[:], VEPS[:], EPS, None, ALU.add)
                    ins = eng.reciprocal(REC[:], VEPS[:])
                    v_rc.inc(ins)
                    eng.wait_ge(sc_rs.h, i + 1)
                    d4 = i % 4
                    eng.wait_ge(s_x[d4].h, 16 * (i // 4 + 1))
                    if i >= 2:
                        eng.wait_ge(vSTF.h, 16 * (i - 1))
                        eng.wait_ge(sc_xb.h, i - 1)
                    pmb = PM[d][:].rearrange("p (o t) -> p o t", o=1).to_broadcast([P, 8, L])
                    rsb = RSTD[:].rearrange("p (o t) -> p o t", o=1).to_broadcast([P, 8, L])
                    eng.tensor_tensor(T1[:], XF[:, d4], pmb, ALU.subtract)
                    ins = eng.tensor_tensor(XNF[:, d], T1[:], rsb, ALU.mult)
                    v_dn.inc(ins)

            phase_barrier({
                "sync": [(vSTF.h, vSTF.n), (sSTB.h, sSTB.n)],
                "gpsimd": [],
                "tensor": [(t_pm.h, t_pm.n)],
                "vector": [(v_dn.h, v_dn.n)],
                "scalar": [(sc_rs.h, sc_rs.n), (sc_xb.h, sc_xb.n)],
            })

        with ExitStack() as p1:
            sb1 = lambda n, s, d: p1.enter_context(nc.sbuf_tensor(n, s, d))
            XN1 = sb1("XN1", [P, 2, 8, L], BF16)
            WQ = sb1("WQ", [P, 2, 8, C], BF16)
            WK = sb1("WK", [P, 2, 8, C], BF16)
            WV = sb1("WV", [P, 8, C], BF16)
            QF = sb1("QF", [P, 8, L], BF16)
            KF = sb1("KF", [P, 8, L], BF16)
            VSB = sb1("VSB", [P, 4, C], BF16)
            QTK = sb1("QTK", [P, 4, C], BF16)
            PMS = sb1("PMS", [P, 4, L], BF16)
            SSB = sb1("SSB", [P, 8, C], BF16)
            OUTS = sb1("OUTS", [P, 2, 4, C], F32)
            RT = sb1("RT", [P, L], F32)
            RA = sb1("RA", [32, L], F32)
            RB = sb1("RB", [32, L], F32)
            MSK = sb1("MSK", [P, 4, L], BF16)
            GV = sb1("GV", [P, 4, 2], F32)
            RCS = sb1("RCS", [32, 2, L], BF16)
            RSS = sb1("RSS", [32, 2, L], BF16)
            PRJ = [p1.enter_context(nc.psum_tensor(f"PRJ{i}", [P, L], F32)) for i in range(2)]
            QTP = [p1.enter_context(nc.psum_tensor(f"QTP{i}", [P, C], BF16)) for i in range(2)]
            PSH = [p1.enter_context(nc.psum_tensor(f"PSH{i}", [P, L], F32)) for i in range(2)]
            OPS = [p1.enter_context(nc.psum_tensor(f"OPS{i}", [P, L], F32)) for i in range(2)]

            sXN = [mksem("sXN0"), mksem("sXN1")]
            sRP = [mksem("sRP0"), mksem("sRP1")]
            gMS = mksem("gMS")
            gW = mksem("gW")
            gA = [mksem(f"gA{j}") for j in range(NCH)]
            tE = mksem("tE")
            scEV = mksem("scEV")
            vRP = mksem("vRP")
            vQT = mksem("vQT")
            vPSH = mksem("vPSH")
            vPF = mksem("vPF")
            scF = mksem("scF")
            scR = mksem("scR")
            vOE = mksem("vOE")

            NT = NHC * NCH
            GT = 56

            def tkt(tt):
                return tt // NCH, tt % NCH, NCH - 1 - (tt % NCH)

            @block.sync
            def _(eng):
                eng.wait_ge(ph.h, 5 * PH[0])
                eng.wait_ge(gCb.h, 80)
                for tt in range(NT):
                    h, jj, j = tkt(tt)
                    d = tt % 2
                    if jj == 0:
                        wd = h % 2
                        if h >= 1:
                            eng.wait_ge(gCbl[h].h, 48)
                        if h >= 2:
                            eng.wait_ge(tE.h, GT * (NCH * (h - 1) - 1) + 24)
                        hh = h
                        eng.dma_start(out=WQ[:, wd].rearrange("p a c -> p (a c)"), in_=wqb_s[hh * P:(hh + 1) * P]).then_inc(gW.h, 16)
                        eng.dma_start(out=WK[:, wd].rearrange("p a c -> p (a c)"), in_=wkb_s[hh * P:(hh + 1) * P]).then_inc(gW.h, 16)
                        gW.n += 32
                        if h > 0:
                            eng.wait_ge(tE.h, GT * (NCH * h - 1) + 24)
                        eng.dma_start(out=WV[:].rearrange("p a c -> p (a c)"), in_=wvb_s[hh * P:(hh + 1) * P]).then_inc(gW.h, 16)
                        gW.n += 16
                    if tt >= 2:
                        eng.wait_ge(tE.h, GT * (tt - 2) + 24)
                        eng.wait_ge(vRP.h, 2 * (tt - 1))
                    eng.dma_start(out=XN1[:, d].rearrange("p a t -> p (a t)"), in_=dram_xnb_chunk(j)).then_inc(sXN[d].h, 16)
                    sXN[d].n += 16
                    eng.dma_start(out=RCS[:, d], in_=ropeCb[:, j * L:(j + 1) * L]).then_inc(sRP[d].h, 16)
                    eng.dma_start(out=RSS[:, d], in_=ropeSb[:, j * L:(j + 1) * L]).then_inc(sRP[d].h, 16)
                    sRP[d].n += 32

            @block.gpsimd
            def _(eng):
                eng.wait_ge(ph.h, 5 * PH[0])
                eng.dma_start(out=MSK[:], in_=maskP.rearrange("(a p) k -> p a k", p=P)).then_inc(gMS.h, 16)
                eng.dma_start(out=GV[:], in_=gvec.rearrange("(a p) c -> p a c", p=P)).then_inc(gMS.h, 16)
                gMS.n += 32
                for h in range(NHC):
                    for jj in range(NCH):
                        tt = h * NCH + jj
                        j = NCH - 1 - jj
                        d = tt % 2
                        eng.wait_ge(vOE.h, 8 * (tt + 1))
                        if h > 0:
                            eng.wait_ge(gA[j].h, 64 * h)
                        dst = part4[j * P:(j + 1) * P]
                        for aa in range(4):
                            da = dst[:, aa * C:(aa + 1) * C]
                            if h == 0:
                                ins = eng.dma_start(out=da, in_=OUTS[:, d, aa, :])
                            else:
                                ins = eng.dma_start(out=da, in_=OUTS[:, d, aa, :], accum_op=ALU.add)
                            ins.then_inc(gA[j].h, 16)
                            gA[j].n += 16

            @block.tensor
            def _(eng):
                eng.wait_ge(ph.h, 5 * PH[0])
                for tt in range(NT):
                    h, jj, j = tkt(tt)
                    d = tt % 2
                    first = jj == 0
                    eng.wait_ge(sXN[d].h, 16 * (tt // 2 + 1))
                    if jj == 0:
                        eng.wait_ge(gW.h, 48 * (h + 1))
                    for ci in range(8):
                        Gp = 24 * tt + ci
                        if Gp >= 2:
                            eng.wait_ge(scEV.h, Gp - 1)
                        bk = PRJ[Gp % 2]
                        for kci in range(8):
                            ins = eng.matmul(bk[:], WQ[:, h % 2, kci, ci * P:(ci + 1) * P], XN1[:, d, kci],
                                             start=(kci == 0), stop=(kci == 7))
                        tE.inc(ins)
                    for ci in range(8):
                        Gp = 24 * tt + 8 + ci
                        if Gp >= 2:
                            eng.wait_ge(scEV.h, Gp - 1)
                        bk = PRJ[Gp % 2]
                        for kci in range(8):
                            ins = eng.matmul(bk[:], WK[:, h % 2, kci, ci * P:(ci + 1) * P], XN1[:, d, kci],
                                             start=(kci == 0), stop=(kci == 7))
                        tE.inc(ins)
                    for gi in range(8):
                        ttile, hf = gi // 2, gi % 2
                        Gp = 24 * tt + 16 + gi
                        if Gp >= 2:
                            eng.wait_ge(scEV.h, Gp - 1)
                        bk = PRJ[Gp % 2]
                        for kci in range(8):
                            ins = eng.matmul(bk[:], XN1[:, d, kci, ttile * P:(ttile + 1) * P],
                                             WV[:, kci, hf * L:(hf + 1) * L],
                                             start=(kci == 0), stop=(kci == 7))
                        tE.inc(ins)
                    eng.wait_ge(scEV.h, 24 * tt + 8)
                    eng.wait_ge(scF.h, 3 * tt + 1)
                    for q in range(4):
                        Gq = 4 * tt + q
                        if Gq >= 2:
                            eng.wait_ge(vQT.h, Gq - 1)
                        bk = QTP[Gq % 2]
                        for ci in range(8):
                            ins = eng.transpose(bk[:, ci * P:(ci + 1) * P],
                                                QF[:, ci, q * P:(q + 1) * P], IDB[:])
                        tE.inc(ins)
                    eng.wait_ge(scEV.h, 24 * tt + 16)
                    eng.wait_ge(scF.h, 3 * tt + 2)
                    for q in range(4):
                        Sp = 20 * tt + q
                        if Sp >= 2:
                            eng.wait_ge(vPSH.h, Sp - 1)
                        bk = PSH[Sp % 2]
                        for ci in range(8):
                            ins = eng.matmul(bk[:], QF[:, ci, q * P:(q + 1) * P], KF[:, ci],
                                             start=(ci == 0), stop=(ci == 7))
                        tE.inc(ins)
                    eng.wait_ge(scEV.h, 24 * (tt + 1))
                    eng.wait_ge(scF.h, 3 * (tt + 1))
                    eng.wait_ge(vPSH.h, 20 * tt + 4)
                    eng.wait_ge(vPF.h, tt + 1)
                    for o in range(8):
                        kt, hf = o // 2, o % 2
                        Op = 8 * tt + o
                        if Op >= 2:
                            eng.wait_ge(vOE.h, Op - 1)
                        bk = OPS[Op % 2]
                        nmm = 4 if first else 12
                        mi = 0
                        for qt in range(4):
                            ins = eng.matmul(bk[:], PMS[:, qt, kt * P:(kt + 1) * P],
                                             VSB[:, qt, hf * L:(hf + 1) * L],
                                             start=(mi == 0), stop=(mi == nmm - 1))
                            mi += 1
                        if not first:
                            for di in range(8):
                                ins = eng.matmul(bk[:], KF[:, di, kt * P:(kt + 1) * P],
                                                 SSB[:, di, hf * L:(hf + 1) * L],
                                                 start=False, stop=(mi == nmm - 1))
                                mi += 1
                        tE.inc(ins)
                    eng.wait_ge(vQT.h, 4 * (tt + 1))
                    for s in range(16):
                        c1i, hf = s // 2, s % 2
                        Sp = 20 * tt + 4 + s
                        if Sp >= 2:
                            eng.wait_ge(vPSH.h, Sp - 1)
                        bk = PSH[Sp % 2]
                        for rt in range(4):
                            ins = eng.matmul(bk[:], QTK[:, rt, c1i * P:(c1i + 1) * P],
                                             VSB[:, rt, hf * L:(hf + 1) * L],
                                             start=(rt == 0), stop=(rt == 3))
                        tE.inc(ins)
                assert tE.n == GT * NT

            @block.scalar
            def _(eng):
                eng = _Guard(eng, sself)
                eng.wait_ge(ph.h, 5 * PH[0])
                for tt in range(NT):
                    for ci in range(8):
                        eng.wait_ge(tE.h, GT * tt + ci + 1)
                        bk = PRJ[(24 * tt + ci) % 2]
                        if ci == 0:
                            eng.wait_ge(vRP.h, 2 * tt + 1)
                            eng.activation(QF[0:64, ci, :], RT[0:64, :], AF.Silu)
                            ins = eng.activation(QF[64:128, ci, :], bk[64:128, :], AF.Silu)
                        else:
                            ins = eng.activation(QF[:, ci, :], bk[:], AF.Silu)
                        scEV.inc(ins)
                    ins = eng.activation(SFL[0:1, 0:1], SFL[0:1, 0:1], AF.Silu)
                    scF.inc(ins)
                    for ci in range(8):
                        eng.wait_ge(tE.h, GT * tt + 8 + ci + 1)
                        bk = PRJ[(24 * tt + 8 + ci) % 2]
                        if ci == 0:
                            eng.wait_ge(vRP.h, 2 * tt + 2)
                            eng.activation(KF[0:64, ci, :], RT[0:64, :], AF.Silu)
                            ins = eng.activation(KF[64:128, ci, :], bk[64:128, :], AF.Silu)
                        else:
                            ins = eng.activation(KF[:, ci, :], bk[:], AF.Silu)
                        scEV.inc(ins)
                    ins = eng.activation(SFL[0:1, 0:1], SFL[0:1, 0:1], AF.Silu)
                    scF.inc(ins)
                    for gi in range(8):
                        ttile, hf = gi // 2, gi % 2
                        eng.wait_ge(tE.h, GT * tt + 16 + gi + 1)
                        bk = PRJ[(24 * tt + 16 + gi) % 2]
                        ins = eng.activation(VSB[:, ttile, hf * L:(hf + 1) * L], bk[:], AF.Silu)
                        scEV.inc(ins)
                    ins = eng.activation(SFL[0:1, 0:1], SFL[0:1, 0:1], AF.Silu)
                    scF.inc(ins)
                    for q in range(4):
                        eng.wait_ge(tE.h, GT * tt + 24 + q + 1)
                        ins = eng.activation(SFL[0:1, 0:1], SFL[0:1, 0:1], AF.Silu)
                        scR.inc(ins)
                assert scEV.n == 24 * NT

            @block.vector
            def _(eng):
                eng = _Guard(eng, vself)
                eng.wait_ge(ph.h, 5 * PH[0])
                eng.wait_ge(gMS.h, 32)
                for tt in range(NT):
                    h, jj, j = tkt(tt)
                    d = tt % 2
                    first = jj == 0
                    eng.wait_ge(sRP[d].h, 32 * (tt // 2 + 1))
                    cs_ = RCS[:, d]
                    sn_ = RSS[:, d]
                    eng.wait_ge(tE.h, GT * tt + 1)
                    bk = PRJ[(24 * tt) % 2]
                    eng.tensor_tensor(RA[:], bk[0:32, :], cs_, ALU.mult)
                    eng.tensor_tensor(RB[:], bk[32:64, :], sn_, ALU.mult)
                    eng.tensor_tensor(RT[0:32, :], RA[:], RB[:], ALU.subtract)
                    eng.tensor_tensor(RA[:], bk[0:32, :], sn_, ALU.mult)
                    eng.tensor_tensor(RB[:], bk[32:64, :], cs_, ALU.mult)
                    ins = eng.tensor_tensor(RT[32:64, :], RA[:], RB[:], ALU.add)
                    vRP.inc(ins)
                    eng.wait_ge(tE.h, GT * tt + 9)
                    eng.wait_ge(scEV.h, 24 * tt + 2)
                    bk = PRJ[(24 * tt + 8) % 2]
                    eng.tensor_tensor(RA[:], bk[0:32, :], cs_, ALU.mult)
                    eng.tensor_tensor(RB[:], bk[32:64, :], sn_, ALU.mult)
                    eng.tensor_tensor(RT[0:32, :], RA[:], RB[:], ALU.subtract)
                    eng.tensor_tensor(RA[:], bk[0:32, :], sn_, ALU.mult)
                    eng.tensor_tensor(RB[:], bk[32:64, :], cs_, ALU.mult)
                    ins = eng.tensor_tensor(RT[32:64, :], RA[:], RB[:], ALU.add)
                    vRP.inc(ins)
                    for q in range(4):
                        eng.wait_ge(tE.h, GT * tt + 24 + q + 1)
                        eng.wait_ge(scR.h, 4 * tt + q + 1)
                        bk = QTP[(4 * tt + q) % 2]
                        ins = eng.tensor_scalar(QTK[:, q, :], bk[:], GV[:, q, 0:1], None, ALU.mult)
                        vQT.inc(ins)
                    for q in range(4):
                        eng.wait_ge(tE.h, GT * tt + 28 + q + 1)
                        bk = PSH[(20 * tt + q) % 2]
                        ins = eng.tensor_tensor(PMS[:, q, :], bk[:], MSK[:, q, :], ALU.mult)
                        vPSH.inc(ins)
                    ins = eng.tensor_copy(RA[0:1, 0:1], RB[0:1, 0:1])
                    vPF.inc(ins)
                    for o in range(8):
                        kt, hf = o // 2, o % 2
                        eng.wait_ge(tE.h, GT * tt + 32 + o + 1)
                        if o == 0 and tt >= 2:
                            pt = tt - 2
                            pj = NCH - 1 - (pt % NCH)
                            ph_ = pt // NCH
                            eng.wait_ge(gA[pj].h, 64 * (ph_ + 1))
                        bk = OPS[(8 * tt + o) % 2]
                        ins = eng.tensor_scalar(OUTS[:, d, kt, hf * L:(hf + 1) * L], bk[:],
                                                GV[:, kt, 1:2], None, ALU.mult)
                        vOE.inc(ins)
                    for s in range(16):
                        c1i, hf = s // 2, s % 2
                        eng.wait_ge(tE.h, GT * tt + 40 + s + 1)
                        bk = PSH[(20 * tt + 4 + s) % 2]
                        dst = SSB[:, c1i, hf * L:(hf + 1) * L]
                        if first:
                            ins = eng.tensor_copy(dst, bk[:])
                        else:
                            ins = eng.scalar_tensor_tensor(dst, dst, GL, bk[:], ALU.mult, ALU.add)
                        vPSH.inc(ins)
            phase_barrier({
                "sync": [(sXN[0].h, sXN[0].n), (sXN[1].h, sXN[1].n),
                         (sRP[0].h, sRP[0].n), (sRP[1].h, sRP[1].n), (gW.h, gW.n)],
                "gpsimd": [(gA[j].h, gA[j].n) for j in range(NCH)] + [(gMS.h, gMS.n)],
                "tensor": [(tE.h, tE.n)],
                "vector": [(vPSH.h, vPSH.n), (vOE.h, vOE.n), (vQT.h, vQT.n)],
                "scalar": [(scEV.h, scEV.n), (scR.h, scR.n), (scF.h, scF.n)],
            })
        with ExitStack() as p2:
            sb2 = lambda n, s, d: p2.enter_context(nc.sbuf_tensor(n, s, d))
            XNB2 = sb2("XNB2", [P, 2, 2, 8, L], BF16)
            XNF2 = sb2("XNF2", [P, 8, L], F32)
            Y = sb2("Y", [P, 2, 8, L], F32)
            ATT = sb2("ATT", [P, 8, PCP], F32)
            SQA = sb2("SQA", [P, 4, PCP], BF16)
            W13B = sb2("W13B", [P, 2, 8, C], BF16)
            W2B = sb2("W2B", [P, 2, 4, C], BF16)
            SG = sb2("SG", [P, 2, L], BF16)
            H2 = sb2("H2", [P, 4, C], BF16)
            SCT = sb2("SCT", [16, PCP], F32)
            BIS = sb2("BIS", [P, 8, 16], F32)
            M8S = sb2("M8S", [P, 8], F32)
            TH = sb2("TH", [P, 1], F32)
            SEL = sb2("SEL", [P, 16], F32)
            WSL = sb2("WSL", [P, 8, 16], F32)
            SW = sb2("SW", [P, 1], F32)
            IV = sb2("IV", [P, 1], F32)
            CMT = sb2("CMT", [P, 8, 16], F32)
            CMB = sb2("CMB", [16, PCP], BF16)
            CMB1 = sb2("CMB1", [1, 2, PCP], BF16)
            CBC = sb2("CBC", [P, PCP], BF16)
            RS1 = sb2("RS1", [P, 8, 16], F32)
            RS2 = sb2("RS2", [P, 8, 16], F32)
            MU = sb2("MU", [P, 8, 16], F32)
            MSQ = sb2("MSQ", [P, 8, 16], F32)
            VAR = sb2("VAR", [P, 8, 16], F32)
            RCg = sb2("RCg", [P, 8, 16], F32)
            M2G = sb2("M2G", [P, 8, 16], F32)
            XGH = sb2("XGH", [P, 2, PCP], F32)
            RSD = sb2("RSD", [P, 8, 16], F32)

            PGEN = [p2.enter_context(nc.psum_tensor(f"PGEN{i}", [P, L], F32)) for i in range(2)]
            HPG = [p2.enter_context(nc.psum_tensor(f"HPG{i}", [P, L], F32)) for i in range(2)]
            HPU = [p2.enter_context(nc.psum_tensor(f"HPU{i}", [P, L], F32)) for i in range(2)]
            YPS = [p2.enter_context(nc.psum_tensor(f"YPS{i}", [P, L], F32)) for i in range(2)]

            csem = mksem("csem")
            gAT = mksem("gAT")
            sXB2 = [mksem("sXB2a"), mksem("sXB2b")]
            sXF = mksem("sXF")
            sY = mksem("sY")
            sW2p = [mksem("sW2a"), mksem("sW2b")]
            sCBp = [mksem("sCBa"), mksem("sCBb")]
            tG = mksem("tG")
            scS = mksem("scS")
            scR2 = mksem("scR2")
            vP2 = mksem("vP2")
            gOUT = mksem("gOUT")

            _hidx = [(cp, e, xx) for cp in range(NCP) for e in range(NE) for xx in range(8)]

            W2WAIT = {}
            CBWAIT = {}
            _w2c = [0, 0]
            _cbc = [0, 0]
            if P2_FFN:
                for _cp in range(NCP):
                    for _e in range(NE):
                        _de = _e % 2
                        _w2c[_de] += 32
                        W2WAIT[(_cp, _e)] = _w2c[_de]
                        if _e < 16:
                            _cbc[_de] += 16
                            CBWAIT[(_cp, _e)] = _cbc[_de]
            GAT_AT = {}
            _ga = 0
            for _cp in range(NCP):
                _ga += 16
                if DEBUG_DUMPS and _cp == 0:
                    _ga += 16
                GAT_AT[_cp] = _ga

            TGI = {}
            _t = 0
            for cp in range(NCP):
                def _m_gating(c):
                    nonlocal _t
                    for hf in range(2):
                        _t += 1
                        TGI[("gate", c, hf)] = _t
                    for tt8 in range(8):
                        _t += 1
                        TGI[("gtp", c, tt8)] = _t
                    for tt8 in range(8):
                        _t += 1
                        TGI[("ctp", c, tt8)] = _t
                if cp == 0:
                    _m_gating(0)
                for e in (range(NE) if P2_FFN else []):
                    if e != 16:
                        for hf in range(2):
                            _t += 1
                            TGI[("cbb", cp, e, hf)] = _t
                    for xx in range(8):
                        _t += 1
                        TGI[("hg", cp, e, xx)] = _t
                        _t += 1
                        TGI[("hu", cp, e, xx)] = _t
                    for di in range(8):
                        for hf in range(2):
                            _t += 1
                            TGI[("y", cp, e, di, hf)] = _t
                    TGI[("edone", cp, e)] = _t
                if not P2_FFN:
                    for e in range(NE):
                        TGI[("edone", cp, e)] = _t
                if cp + 1 < NCP:
                    _m_gating(cp + 1)
                if P2_ATT:
                    for ci in range(8):
                        for ah in range(2):
                            _t += 1
                            TGI[("atp", cp, ci, ah)] = _t
                TGI[("cpdone", cp)] = _t
            TG_TOTAL = _t

            SCI = {}
            SCR2 = {}
            _s = 0
            _r2 = 0
            for cp in range(NCP):
                if cp == 0:
                    for hf in range(2):
                        _s += 1
                        SCI[("sig", 0, hf)] = _s
                    for t8 in range(8):
                        _r2 += 1
                        SCR2[("gtp", 0, t8)] = _r2
                    for t8 in range(8):
                        _r2 += 1
                        SCR2[("ctp", 0, t8)] = _r2
                if P2_FFN:
                    for e in range(NE):
                        for xx in range(8):
                            _s += 1
                            SCI[("silu", cp, e, xx)] = _s
                if P2_ATT:
                    _s += 1
                    SCI[("sq1", cp)] = _s
                    _s += 1
                    SCI[("sq2", cp)] = _s
                    _s += 1
                    SCI[("sqrt", cp)] = _s
                if cp + 1 < NCP:
                    for hf in range(2):
                        _s += 1
                        SCI[("sig", cp + 1, hf)] = _s
                    for t8 in range(8):
                        _r2 += 1
                        SCR2[("gtp", cp + 1, t8)] = _r2
                    for t8 in range(8):
                        _r2 += 1
                        SCR2[("ctp", cp + 1, t8)] = _r2
                if P2_ATT:
                    for ci in range(8):
                        for ah in range(2):
                            _r2 += 1
                            SCR2[("atp", cp, ci, ah)] = _r2
            SC_TOTAL = _s

            VI = {}
            _v = 0
            for cp in range(NCP):
                def _m_vgate(c):
                    nonlocal _v
                    for tt8 in range(8):
                        _v += 1
                        VI[("bis", c, tt8)] = _v
                    for tt8 in range(8):
                        _v += 8
                        VI[("comb", c, tt8)] = _v
                    for tt8 in range(8):
                        _v += 1
                        VI[("cmbc", c, tt8)] = _v
                if cp == 0:
                    _m_vgate(0)
                for ci in range(8):
                    _v += 1
                    VI[("yinit", cp, ci)] = _v
                for e in (range(NE) if P2_FFN else []):
                    if e != 16:
                        _v += 2
                        VI[("cbc", cp, e)] = _v
                    for xx in range(8):
                        _v += 1
                        VI[("hufree", cp, e, xx)] = _v
                        if e != 16:
                            _v += 1
                    VI[("h2done", cp, e)] = _v
                    for di in range(8):
                        for hf in range(2):
                            _v += 1
                            VI[("yadd", cp, e, di, hf)] = _v
                if P2_ATT:
                    _v += 1
                    VI[("rs2a", cp)] = _v
                    _v += 2
                    VI[("gnred", cp)] = _v
                    _v += 5
                    _v += 1
                    VI[("gnrec", cp)] = _v
                    _v += 9
                    VI[("gnnorm", cp)] = _v
                if cp + 1 < NCP:
                    _m_vgate(cp + 1)
                if P2_ATT:
                    for ci in range(8):
                        for ah in range(2):
                            _v += 1
                            VI[("stt", cp, ci, ah)] = _v
                VI[("vdone", cp)] = _v
            V_TOTAL = _v

            PGEN_USES = []
            for cp in range(NCP):
                PGEN_USES.append((("gate", cp, 0), ("sc", SCI[("sig", cp, 0)])))
                PGEN_USES.append((("gate", cp, 1), ("sc", SCI[("sig", cp, 1)])))
                for tt8 in range(8):
                    PGEN_USES.append((("gtp", cp, tt8), ("v", VI[("bis", cp, tt8)])))
                for tt8 in range(8):
                    PGEN_USES.append((("ctp", cp, tt8), ("v", VI[("cmbc", cp, tt8)])))
                if P2_FFN:
                    for e in range(16):
                        PGEN_USES.append((("cbb", cp, e, 0), ("v", VI[("cbc", cp, e)])))
                        PGEN_USES.append((("cbb", cp, e, 1), ("v", VI[("cbc", cp, e)])))
            PGEN_IDX = {u[0]: i for i, u in enumerate(PGEN_USES)}

            YPS_USES = []
            for cp in range(NCP):
                if P2_FFN:
                    for e in range(NE):
                        for di in range(8):
                            for hf in range(2):
                                YPS_USES.append((("y", cp, e, di, hf), ("v", VI[("yadd", cp, e, di, hf)])))
                if P2_ATT:
                    for ci in range(8):
                        for ah in range(2):
                            YPS_USES.append((("atp", cp, ci, ah), ("v", VI[("stt", cp, ci, ah)])))
            YPS_IDX = {u[0]: i for i, u in enumerate(YPS_USES)}

            def puse_wait(eng, uses, idx_map, key):
                i = idx_map[key]
                if i >= 2:
                    kind, val = uses[i - 2][1]
                    eng.wait_ge(scS.h if kind == "sc" else vP2.h, val)

            def hpg_wait(eng, cp, e, xx):
                gi = (cp * NE + e) * 8 + xx
                if gi >= 2:
                    pcp, pe, pxx = _hidx[gi - 2]
                    eng.wait_ge(scS.h, SCI[("silu", pcp, pe, pxx)])

            def hpu_wait(eng, cp, e, xx):
                gi = (cp * NE + e) * 8 + xx
                if gi >= 2:
                    pcp, pe, pxx = _hidx[gi - 2]
                    eng.wait_ge(vP2.h, VI[("hufree", pcp, pe, pxx)])

            @block.scalar
            def _(eng):
                eng = _Guard(eng, sself)
                def emit_sig(c):
                    for hf in range(2):
                        eng.wait_ge(tG.h, TGI[("gate", c, hf)])
                        ins = eng.activation(SCT[:, hf * L:(hf + 1) * L], PGEN[hf % 2][0:16, :], AF.Sigmoid)
                        scS.inc(ins)
                    for tt8 in range(8):
                        eng.wait_ge(tG.h, TGI[("gtp", c, tt8)])
                        ins = eng.activation(SFL[0:1, 1:2], SFL[0:1, 1:2], AF.Copy)
                        scR2.inc(ins)
                    for tt8 in range(8):
                        eng.wait_ge(tG.h, TGI[("ctp", c, tt8)])
                        ins = eng.activation(SFL[0:1, 1:2], SFL[0:1, 1:2], AF.Copy)
                        scR2.inc(ins)

                for cp in range(NCP):
                    if cp == 0:
                        emit_sig(0)
                    if P2_FFN:
                        for e in range(NE):
                            for xx in range(8):
                                eng.wait_ge(tG.h, TGI[("hg", cp, e, xx)])
                                if (cp * NE + e) * 8 + xx >= 2:
                                    pcp, pe, pxx = _hidx[(cp * NE + e) * 8 + xx - 2]
                                    eng.wait_ge(vP2.h, VI[("hufree", pcp, pe, pxx)])
                                ins = eng.activation(SG[:, xx % 2, :], HPG[xx % 2][:], AF.Silu)
                                scS.inc(ins)
                    if P2_ATT:
                        eng.wait_ge(gAT.h, GAT_AT[cp])
                        ins = eng.activation(SQA[:], ATT[:, 0:4, :], AF.Square)
                        scS.inc(ins)
                        eng.wait_ge(vP2.h, VI[("rs2a", cp)])
                        ins = eng.activation(SQA[:], ATT[:, 4:8, :], AF.Square)
                        scS.inc(ins)
                        eng.wait_ge(vP2.h, VI[("gnrec", cp)])
                        ins = eng.activation(RSD[:], RCg[:], AF.Sqrt)
                        scS.inc(ins)
                    if cp + 1 < NCP:
                        emit_sig(cp + 1)
                    if P2_ATT:
                        for ci in range(8):
                            for ah in range(2):
                                eng.wait_ge(tG.h, TGI[("atp", cp, ci, ah)])
                                ins = eng.activation(SFL[0:1, 2:3], SFL[0:1, 2:3], AF.Copy)
                                scR2.inc(ins)
                assert scS.n == SC_TOTAL

            @block.gpsimd
            def _(eng):
                eng.wait_ge(ph.h, 5 * PH[0])
                pid_g = eng.snap(nc.partition_id(engines=[ET.Pool]), min_val=0, max_val=NCORES - 1)
                hg_g = pid_g % 2
                own0 = hg_g * (8 * P)
                for _j in range(NCH):
                    eng.wait_ge(gA[_j].h, 64 * NHC)
                eng.collective_compute(
                    "AllReduce", ALU.add,
                    replica_groups=[[2 * i, 2 * i + 1] for i in range(NCORES // 2)],
                    ins=[part4.opt()], outs=[cc_sum.opt()],
                ).then_inc(csem.h, 1)
                csem.n += 1
                for cp in range(NCP):
                    def load_cmb1(e, cp=cp):
                        de = e % 2
                        eng.wait_ge(vP2.h, VI[("cmbc", cp, 7)])
                        if cp * 16 + e >= 2:
                            eng.wait_ge(tG.h, TGI[("cbb", cp if e >= 2 else cp - 1, (e - 2) % 16, 1)])
                        eng.dma_start(out=CMB1[:, de], in_=CMB[e:e + 1, :]).then_inc(sCBp[de].h, 16)
                        sCBp[de].n += 16

                    if P2_FFN:
                        load_cmb1(0)
                        load_cmb1(1)
                    if cp > 0:
                        eng.wait_ge(tG.h, TGI[("cpdone", cp - 1)])
                    eng.wait_ge(csem.h, 1)
                    att2 = ATT[:].rearrange("p (c g) f -> p c (g f)", c=2)
                    eng.dma_start(out=att2,
                                  in_=cc_sum[bass.ds(own0 + cp * 2 * P, 2 * P), :].rearrange("(c p) f -> p c f", p=P)).then_inc(gAT.h, 16)
                    gAT.n += 16
                    if P2_FFN:
                        for e in range(2, 16):
                            load_cmb1(e)
                    if DEBUG_DUMPS and cp == 0:
                        eng.wait_ge(gAT.h, gAT.n)
                        eng.dma_start(out=dump_att.rearrange("(a p) c -> p a c", p=P), in_=ATT[:]).then_inc(gAT.h, 16)
                        gAT.n += 16
                    eng.wait_ge(vP2.h, VI[("vdone", cp)])
                    eng.dma_start(out=outT[cp * P:(cp + 1) * P].rearrange("p (c f) -> p c f", c=2),
                                  in_=Y[:].rearrange("p c a t -> p c (a t)")).then_inc(gOUT.h, 16)
                    gOUT.n += 16
                eng.wait_ge(gOUT.h, 16 * NCP)
                if DEBUG_DUMPS:
                    eng.dma_start(out=dump_xnf, in_=xnf_s).then_inc(gOUT.h, 16)
                    eng.dma_start(out=dump_p4, in_=part4).then_inc(gOUT.h, 16)
                    eng.dma_start(out=dump_cc, in_=cc_sum[bass.ds(own0, 2 * P), :]).then_inc(gOUT.h, 16)
                    gOUT.n += 48
                    eng.wait_ge(gOUT.h, gOUT.n)

            @block.sync
            def _(eng):
                eng.wait_ge(ph.h, 5 * PH[0])
                pid_y = eng.snap(nc.partition_id(engines=[ET.SP]), min_val=0, max_val=NCORES - 1)
                own_r0 = (pid_y % 2) * (8 * P)
                wslot_prev = [None, None]
                eng.wait_ge(gCc.h, 32)
                for cp in range(NCP):
                    d = cp % 2
                    if P2_FFN:
                        if cp >= 2:
                            eng.wait_ge(tG.h, TGI[("edone", cp - 2, NE - 1)])
                        eng.dma_start(out=XNB2[:, d].rearrange("p c a t -> p c (a t)"),
                                      in_=xnb_s[bass.ds(own_r0 + cp * 2 * P, 2 * P), :].rearrange("(c p) f -> p c f", p=P)).then_inc(sXB2[d].h, 16)
                        sXB2[d].n += 16

                    def load_xnf(hf, cp=cp):
                        if cp > 0 or hf > 0:
                            pk = ("gate", cp, hf - 1) if hf > 0 else ("gate", cp - 1, 1)
                            eng.wait_ge(tG.h, TGI[pk])
                        eng.dma_start(out=XNF2[:].rearrange("p a t -> p (a t)"),
                                      in_=xnf_s[bass.ds(own_r0 + (cp * 2 + hf) * P, P), :]).then_inc(sXF.h, 16)
                        sXF.n += 16

                    def load_w(e, cp=cp):
                        de = e % 2
                        if wslot_prev[de] is not None:
                            pcp, pe = wslot_prev[de]
                            eng.wait_ge(tG.h, TGI[("edone", pcp, pe)])
                        wslot_prev[de] = (cp, e)
                        eng.dma_start(out=W13B[:, de].rearrange("p a c -> p (a c)"),
                                      in_=w13b_s[e * P:(e + 1) * P]).then_inc(sW2p[de].h, 16)
                        eng.dma_start(out=W2B[:, de].rearrange("p a c -> p (a c)"),
                                      in_=w2b_s[e * P:(e + 1) * P]).then_inc(sW2p[de].h, 16)
                        sW2p[de].n += 32

                    if P2_FFN:
                        load_xnf(0)
                        load_w(0)
                        load_w(1)
                        load_xnf(1)
                        if cp > 0:
                            eng.wait_ge(gOUT.h, 16 * cp)
                        eng.dma_start(out=Y[:].rearrange("p c a t -> p c (a t)"),
                                      in_=xT[bass.ds(own_r0 + cp * 2 * P, 2 * P), :].rearrange("(c p) f -> p c f", p=P)).then_inc(sY.h, 16)
                        sY.n += 16
                        for e in range(2, NE):
                            load_w(e)
                    else:
                        load_xnf(0)
                        load_xnf(1)
                        if cp > 0:
                            eng.wait_ge(gOUT.h, 16 * cp)
                        eng.dma_start(out=Y[:].rearrange("p c a t -> p c (a t)"),
                                      in_=xT[bass.ds(own_r0 + cp * 2 * P, 2 * P), :].rearrange("(c p) f -> p c f", p=P)).then_inc(sY.h, 16)
                        sY.n += 16
                eng.wait_ge(gOUT.h, 16 * NCP)

            @block.tensor
            def _(eng):
                eng.wait_ge(ph.h, 5 * PH[0])
                eng.wait_ge(oneS.h, 1)

                def emit_gating(c):
                    for hf in range(2):
                        eng.wait_ge(sXF.h, 16 * (2 * c + hf + 1))
                        puse_wait(eng, PGEN_USES, PGEN_IDX, ("gate", c, hf))
                        bk = PGEN[hf % 2]
                        for kci in range(8):
                            ins = eng.matmul(bk[0:16, :], GWT[:, kci, :], XNF2[:, kci],
                                             start=(kci == 0), stop=(kci == 7))
                        tG.inc(ins)
                    for tt8 in range(8):
                        eng.wait_ge(scS.h, SCI[("sig", c, 1)])
                        puse_wait(eng, PGEN_USES, PGEN_IDX, ("gtp", c, tt8))
                        bk = PGEN[tt8 % 2]
                        ins = eng.transpose(bk[:, 0:16], SCT[:, tt8 * P:(tt8 + 1) * P], IDF[0:16, 0:16])
                        tG.inc(ins)
                    for tt8 in range(8):
                        eng.wait_ge(vP2.h, VI[("comb", c, 7)])
                        puse_wait(eng, PGEN_USES, PGEN_IDX, ("ctp", c, tt8))
                        bk = PGEN[tt8 % 2]
                        ins = eng.transpose(bk[0:16, 0:P], CMT[:, tt8, :], IDF[:])
                        tG.inc(ins)

                for cp in range(NCP):
                    d = cp % 2
                    if cp == 0:
                        emit_gating(0)
                    for e in (range(NE) if P2_FFN else []):
                        de = e % 2
                        eng.wait_ge(sW2p[de].h, W2WAIT[(cp, e)])
                        if e != 16:
                            eng.wait_ge(sCBp[de].h, CBWAIT[(cp, e)])
                            for hf in range(2):
                                puse_wait(eng, PGEN_USES, PGEN_IDX, ("cbb", cp, e, hf))
                                ins = eng.matmul(PGEN[hf % 2][:], ONE1[:],
                                                 CMB1[0:1, de, hf * L:(hf + 1) * L],
                                                 start=True, stop=True)
                                tG.inc(ins)
                        for xx in range(8):
                            i, hf = xx // 2, xx % 2
                            if e == 0 and xx == 0:
                                eng.wait_ge(sXB2[d].h, 16 * (cp // 2 + 1))
                            hpg_wait(eng, cp, e, xx)
                            bkg = HPG[xx % 2]
                            for kci in range(8):
                                ins = eng.matmul(bkg[:], W13B[:, de, kci, i * P:(i + 1) * P],
                                                 XNB2[:, d, hf, kci, :],
                                                 start=(kci == 0), stop=(kci == 7))
                            tG.inc(ins)
                            hpu_wait(eng, cp, e, xx)
                            bku = HPU[xx % 2]
                            for kci in range(8):
                                ins = eng.matmul(bku[:], W13B[:, de, kci, (i + 4) * P:(i + 5) * P],
                                                 XNB2[:, d, hf, kci, :],
                                                 start=(kci == 0), stop=(kci == 7))
                            tG.inc(ins)
                        for di in range(8):
                            for hf in range(2):
                                eng.wait_ge(vP2.h, VI[("h2done", cp, e)])
                                puse_wait(eng, YPS_USES, YPS_IDX, ("y", cp, e, di, hf))
                                bk = YPS[(2 * di + hf) % 2]
                                for ii in range(4):
                                    ins = eng.matmul(bk[:], W2B[:, de, ii, di * P:(di + 1) * P],
                                                     H2[:, ii, hf * L:(hf + 1) * L],
                                                     start=(ii == 0), stop=(ii == 3))
                                tG.inc(ins)
                    if not P2_FFN and cp == 0:
                        pass
                    if cp + 1 < NCP:
                        emit_gating(cp + 1)
                    if P2_ATT:
                        eng.wait_ge(vP2.h, VI[("gnnorm", cp)])
                    for ci in (range(8) if P2_ATT else []):
                        for ah in range(2):
                            puse_wait(eng, YPS_USES, YPS_IDX, ("atp", cp, ci, ah))
                            bk = YPS[(2 * ci + ah) % 2]
                            for a in range(4):
                                aa = ah * 4 + a
                                ins = eng.transpose(bk[:, a * P:(a + 1) * P],
                                                    ATT[:, aa, ci * P:(ci + 1) * P], IDF[:])
                            tG.inc(ins)
                assert tG.n == TG_TOTAL

            @block.vector
            def _(eng):
                eng = _Guard(eng, vself)
                eng.wait_ge(ph.h, 5 * PH[0])

                def emit_vgate(c):
                    for tt8 in range(8):
                        eng.wait_ge(tG.h, TGI[("gtp", c, tt8)])
                        eng.wait_ge(scR2.h, SCR2[("gtp", c, tt8)])
                        bk = PGEN[tt8 % 2]
                        ins = eng.tensor_tensor(BIS[:, tt8, :], bk[:, 0:16], GBT[:], ALU.add)
                        vP2.inc(ins)
                    for tt8 in range(8):
                        ins = eng.max(out=M8S[:], in_=BIS[:, tt8, :])
                        vP2.inc(ins)
                        ins = eng.tensor_copy(TH[:], M8S[:, 3:4])
                        vP2.inc(ins)
                        ins = eng.tensor_scalar(SEL[:], BIS[:, tt8, :], TH[:, 0:1], None, ALU.is_ge)
                        vP2.inc(ins)
                        ins = eng.tensor_tensor(WSL[:, tt8, :], BIS[:, tt8, :], GBT[:], ALU.subtract)
                        vP2.inc(ins)
                        ins = eng.tensor_tensor(WSL[:, tt8, :], WSL[:, tt8, :], SEL[:], ALU.mult)
                        vP2.inc(ins)
                        ins = eng.reduce_sum(SW[:], WSL[:, tt8, :], axis=AX.X)
                        vP2.inc(ins)
                        ins = eng.reciprocal(IV[:], SW[:])
                        vP2.inc(ins)
                        ins = eng.tensor_scalar(CMT[:, tt8, :], WSL[:, tt8, :], IV[:, 0:1], 2.5,
                                                ALU.mult, ALU.mult)
                        vP2.inc(ins)
                        assert vP2.n == VI[("comb", c, tt8)]
                    for tt8 in range(8):
                        eng.wait_ge(tG.h, TGI[("ctp", c, tt8)])
                        eng.wait_ge(scR2.h, SCR2[("ctp", c, tt8)])
                        bk = PGEN[tt8 % 2]
                        ins = eng.tensor_copy(CMB[:, tt8 * P:(tt8 + 1) * P], bk[0:16, 0:P])
                        vP2.inc(ins)
                        assert vP2.n == VI[("cmbc", c, tt8)]

                for cp in range(NCP):
                    d = cp % 2
                    if cp == 0:
                        emit_vgate(0)
                    eng.wait_ge(sY.h, 16 * (cp + 1))
                    for ci in range(8):
                        ins = eng.tensor_scalar(Y[:, :, ci, :], Y[:, :, ci, :], GNB[:, ci, 0:1], None, ALU.add)
                        vP2.inc(ins)
                    for e in (range(NE) if P2_FFN else []):
                        if e != 16:
                            for hf in range(2):
                                eng.wait_ge(tG.h, TGI[("cbb", cp, e, hf)])
                                ins = eng.tensor_copy(CBC[:, hf * L:(hf + 1) * L], PGEN[hf % 2][:])
                                vP2.inc(ins)
                            assert vP2.n == VI[("cbc", cp, e)]
                        for xx in range(8):
                            i, hf = xx // 2, xx % 2
                            eng.wait_ge(scS.h, SCI[("silu", cp, e, xx)])
                            eng.wait_ge(tG.h, TGI[("hu", cp, e, xx)])
                            ins = eng.tensor_tensor(H2[:, i, hf * L:(hf + 1) * L],
                                                    SG[:, xx % 2, :], HPU[xx % 2][:], ALU.mult)
                            vP2.inc(ins)
                            if e != 16:
                                ins = eng.tensor_tensor(H2[:, i, hf * L:(hf + 1) * L],
                                                        H2[:, i, hf * L:(hf + 1) * L],
                                                        CBC[:, hf * L:(hf + 1) * L], ALU.mult)
                                vP2.inc(ins)
                        assert vP2.n == VI[("h2done", cp, e)]
                        for di in range(8):
                            for hf in range(2):
                                eng.wait_ge(tG.h, TGI[("y", cp, e, di, hf)])
                                bk = YPS[(2 * di + hf) % 2]
                                ins = eng.tensor_tensor(Y[:, hf, di, :],
                                                        Y[:, hf, di, :], bk[:], ALU.add)
                                vP2.inc(ins)
                    if P2_ATT:
                        eng.wait_ge(scS.h, SCI[("sq1", cp)])
                        att4 = ATT[:].rearrange("p a (g k) -> p a g k", k=64)
                        sqa4 = SQA[:].rearrange("p a (g k) -> p a g k", k=64)
                        ins = eng.reduce_sum(RS2[:, 0:4, :], sqa4, axis=AX.X)
                        vP2.inc(ins)
                        assert vP2.n == VI[("rs2a", cp)]
                        eng.wait_ge(scS.h, SCI[("sq2", cp)])
                        ins = eng.reduce_sum(RS1[:], att4, axis=AX.X)
                        vP2.inc(ins)
                        ins = eng.reduce_sum(RS2[:, 4:8, :], sqa4, axis=AX.X)
                        vP2.inc(ins)
                        ins = eng.tensor_scalar(MU[:], RS1[:], 1.0 / 64, None, ALU.mult)
                        vP2.inc(ins)
                        ins = eng.tensor_scalar(MSQ[:], RS2[:], 1.0 / 64, None, ALU.mult)
                        vP2.inc(ins)
                        ins = eng.tensor_tensor(M2G[:], MU[:], MU[:], ALU.mult)
                        vP2.inc(ins)
                        ins = eng.tensor_tensor(VAR[:], MSQ[:], M2G[:], ALU.subtract)
                        vP2.inc(ins)
                        ins = eng.tensor_scalar(RS1[:], VAR[:], EPS, None, ALU.add)
                        vP2.inc(ins)
                        ins = eng.reciprocal(RCg[:], RS1[:])
                        vP2.inc(ins)
                        assert vP2.n == VI[("gnrec", cp)]
                        eng.wait_ge(scS.h, SCI[("sqrt", cp)])
                        for ah in range(4):
                            sl = slice(2 * ah, 2 * ah + 2)
                            a4 = ATT[:, sl, :].rearrange("p a (g k) -> p a g k", k=64)
                            x4 = XGH[:].rearrange("p a (g k) -> p a g k", k=64)
                            mub = MU[:, sl, :].rearrange("p a (g o) -> p a g o", o=1).to_broadcast([P, 2, 16, 64])
                            rsb = RSD[:, sl, :].rearrange("p a (g o) -> p a g o", o=1).to_broadcast([P, 2, 16, 64])
                            ins = eng.tensor_tensor(x4, a4, mub, ALU.subtract)
                            vP2.inc(ins)
                            ins = eng.tensor_tensor(a4, x4, rsb, ALU.mult)
                            vP2.inc(ins)
                        ins = eng.tensor_copy(TH[:], TH[:])
                        vP2.inc(ins)
                        assert vP2.n == VI[("gnnorm", cp)]
                    if cp + 1 < NCP:
                        emit_vgate(cp + 1)
                    for ci in (range(8) if P2_ATT else []):
                        for ah in range(2):
                            eng.wait_ge(tG.h, TGI[("atp", cp, ci, ah)])
                            eng.wait_ge(scR2.h, SCR2[("atp", cp, ci, ah)])
                            bk = YPS[(2 * ci + ah) % 2]
                            ins = eng.scalar_tensor_tensor(Y[:, ah, ci, :], bk[:],
                                                           GNW[:, ci, 0:1],
                                                           Y[:, ah, ci, :],
                                                           ALU.mult, ALU.add)
                            vP2.inc(ins)
                    assert vP2.n == VI[("vdone", cp)]
                assert vP2.n == V_TOTAL

    return nc



def kernel(L0_attn_X_Q, L0_attn_W_Q, L0_attn_W_K, L0_attn_W_V,
           L0_attn_gn_weight, L0_attn_gn_bias,
           L0_ffn_input_ids, L0_ffn_gate_w, L0_ffn_gate_b,
           L0_ffn_w13, L0_ffn_w2, L0_ffn_w13_s, L0_ffn_w2_s):
    x = np.asarray(L0_attn_X_Q, np.float32)
    _ = L0_ffn_input_ids
    consts = _host_constants()
    in_maps = []
    for c in range(NCORES):
        in_maps.append(_prep_core_inputs(
            c, x, np.asarray(L0_attn_W_Q, np.float32), np.asarray(L0_attn_W_K, np.float32),
            np.asarray(L0_attn_W_V, np.float32),
            np.asarray(L0_attn_gn_weight, np.float32), np.asarray(L0_attn_gn_bias, np.float32),
            np.asarray(L0_ffn_gate_w, np.float32), np.asarray(L0_ffn_gate_b, np.float32),
            np.asarray(L0_ffn_w13, np.float32), np.asarray(L0_ffn_w2, np.float32),
            np.asarray(L0_ffn_w13_s, np.float32), np.asarray(L0_ffn_w2_s, np.float32),
            consts))
    nc = build_program()
    res = run_bass_kernel_spmd(nc, in_maps, core_ids=list(range(NCORES)))
    out = np.empty((B, T, C), np.float32)
    for c in range(NCORES):
        b, hg = c // 2, c % 2
        r = np.asarray(res.results[c]["outT"], np.float32).reshape(NCP, P, 2, 8, L)
        out[b, hg * HALF:(hg + 1) * HALF, :] = r.transpose(0, 2, 4, 3, 1).reshape(HALF, C)
    return out


# version 11
# speedup vs baseline: 1.2834x; 1.0935x over previous
import math
from contextlib import ExitStack

import numpy as np

import concourse.bass as bass
import concourse.mybir as mybir
from concourse.bass_utils import run_bass_kernel_spmd

F32 = mybir.dt.float32
BF16 = mybir.dt.bfloat16
AF = mybir.ActivationFunctionType
ALU = mybir.AluOpType
AX = mybir.AxisListType
ET = mybir.EngineType

B, T, C = 4, 8192, 1024
P = 128
H = 8
NHC = 4
L = 512
NCH = T // L
GAMMA = 0.96875
GL = GAMMA ** L
HALF = T // 2
PCP = 1024
NCP = HALF // PCP
NE = 17
INT = 512
EPS = 1e-5
ROPE_DIM = 64
NCORES = 8

DEBUG_DUMPS = False
P2_FFN = True
P2_ATT = True
CC_SHARED = True



def _rope_perm():
    p = list(range(0, ROPE_DIM, 2)) + list(range(1, ROPE_DIM, 2)) + list(range(ROPE_DIM, C))
    return np.array(p, dtype=np.int64)


def _host_constants():
    inv_freq = 1.0 / 10000.0 ** (np.arange(0, ROPE_DIM, 2, dtype=np.float64) / ROPE_DIM)
    t = np.arange(T, dtype=np.float64)
    ang = inv_freq[:, None] * t[None, :]
    ropeC = np.cos(ang).astype(np.float32)
    ropeS = np.sin(ang).astype(np.float32)
    q = np.arange(L)[:, None].astype(np.float64)
    k = np.arange(L)[None, :].astype(np.float64)
    maskP = np.where(q >= k, GAMMA ** (q - L), 0.0).astype(np.float32)
    r = np.arange(L, dtype=np.float64)
    gvec = np.stack([GAMMA ** r, GAMMA ** (L - r)], axis=1).astype(np.float32)
    identf = np.eye(P, dtype=np.float32)
    onesC = np.full((P, P), 1.0 / C, dtype=np.float32)
    return ropeC, ropeS, maskP, gvec, identf, onesC


def _prep_core_inputs(c, x, wq_full, wk_full, wv_full, gnw, gnb, gate_w, gate_b,
                      w13, w2, w13_s, w2_s, consts):
    b, hg = c // 2, c % 2
    heads = [hg * NHC + k for k in range(NHC)]
    perm = _rope_perm()
    ropeC, ropeS, maskP, gvec, identf, onesC = consts

    xT = np.ascontiguousarray(
        x[b].reshape(NCH, L, 8, P).transpose(0, 3, 2, 1).reshape(NCH * P, 8 * L))
    wq = np.concatenate([wq_full[:, h * C:(h + 1) * C][:, perm] for h in heads], axis=1)
    wq = np.ascontiguousarray(wq.reshape(8, P, NHC, C).transpose(2, 1, 0, 3).reshape(NHC * P, 8 * C))
    wk = np.concatenate([wk_full[:, h * C:(h + 1) * C][:, perm] for h in heads], axis=1)
    wk = np.ascontiguousarray(wk.reshape(8, P, NHC, C).transpose(2, 1, 0, 3).reshape(NHC * P, 8 * C))
    wv = np.concatenate([wv_full[:, h * C:(h + 1) * C] for h in heads], axis=1)
    wv = np.ascontiguousarray(wv.reshape(8, P, NHC, C).transpose(2, 1, 0, 3).reshape(NHC * P, 8 * C))
    w13T = np.stack([w13[e].T for e in range(16)] + [w13_s.T])
    w13T = np.ascontiguousarray(w13T.reshape(NE, 8, P, C).transpose(0, 2, 1, 3).reshape(NE * P, 8 * C))
    w2T = np.stack([w2[e].T for e in range(16)] + [w2_s.T])
    w2T = np.ascontiguousarray(w2T.reshape(NE, 4, P, C).transpose(0, 2, 1, 3).reshape(NE * P, 4 * C))
    return {
        "xT": xT, "wq": wq, "wk": wk, "wv": wv,
        "w13T": w13T, "w2T": w2T,
        "gwT": np.ascontiguousarray(gate_w.T),
        "gbt": np.ascontiguousarray(np.tile(gate_b[None, :], (P, 1))),
        "gnw": np.ascontiguousarray(gnw.reshape(C, 1)),
        "gnb": np.ascontiguousarray(gnb.reshape(C, 1)),
        "identf": identf, "onesC": onesC,
        "ropeC": ropeC, "ropeS": ropeS, "maskP": maskP, "gvec": gvec,
    }


_ROLE_TGT = {}


class Sem:
    def __init__(self, nc, ctx, name):
        self.h = ctx.enter_context(nc.semaphore(name))
        self.n = 0

    def inc(self, instr, k=1):
        instr.then_inc(self.h, k)
        self.n += k
        _ROLE_TGT[instr.ins.name] = (self.h, self.n)
        return self.n


class _Guard:

    def __init__(self, eng, sem):
        self._eng = eng
        self._sem = sem

    def _settle(self):
        p = getattr(self._sem, "pend", None)
        if p is None:
            return
        self._sem.pend = None
        tgt = _ROLE_TGT.pop(p.ins.name, None)
        if tgt is None:
            p.then_inc(self._sem.h, 1)
            self._sem.n += 1
            tgt = (self._sem.h, self._sem.n)
        self._eng.wait_ge(tgt[0], tgt[1])

    def wait_ge(self, h, v):
        self._eng.wait_ge(h, v)

    def nop(self):
        self._settle()
        return self._eng.nop()

    def dma_start(self, *a, **kw):
        return self._eng.dma_start(*a, **kw)

    def snap(self, *a, **kw):
        return self._eng.snap(*a, **kw)

    def __getattr__(self, name):
        f = getattr(self._eng, name)

        def wrap(*a, **kw):
            self._settle()
            ins = f(*a, **kw)
            if hasattr(ins, "then_inc"):
                self._sem.pend = ins
            return ins

        return wrap



def build_program():
    nc = bass.Bass("TRN2", target_bir_lowering=False, debug=False)

    xT = nc.dram_tensor("xT", [NCH * P, 8 * L], F32, kind="ExternalInput").ap()
    wq = nc.dram_tensor("wq", [NHC * P, 8 * C], F32, kind="ExternalInput").ap()
    wk = nc.dram_tensor("wk", [NHC * P, 8 * C], F32, kind="ExternalInput").ap()
    wv = nc.dram_tensor("wv", [NHC * P, 8 * C], F32, kind="ExternalInput").ap()
    w13T = nc.dram_tensor("w13T", [NE * P, 8 * C], F32, kind="ExternalInput").ap()
    w2T = nc.dram_tensor("w2T", [NE * P, 4 * C], F32, kind="ExternalInput").ap()
    gwT = nc.dram_tensor("gwT", [C, 16], F32, kind="ExternalInput").ap()
    gbt = nc.dram_tensor("gbt", [P, 16], F32, kind="ExternalInput").ap()
    gnw = nc.dram_tensor("gnw", [C, 1], F32, kind="ExternalInput").ap()
    gnb = nc.dram_tensor("gnb", [C, 1], F32, kind="ExternalInput").ap()
    identf = nc.dram_tensor("identf", [P, P], F32, kind="ExternalInput").ap()
    onesC = nc.dram_tensor("onesC", [P, P], F32, kind="ExternalInput").ap()
    ropeC = nc.dram_tensor("ropeC", [32, T], F32, kind="ExternalInput").ap()
    ropeS = nc.dram_tensor("ropeS", [32, T], F32, kind="ExternalInput").ap()
    maskP = nc.dram_tensor("maskP", [L, L], F32, kind="ExternalInput").ap()
    gvec = nc.dram_tensor("gvec", [L, 2], F32, kind="ExternalInput").ap()

    outT = nc.dram_tensor("outT", [NCP * P, 2 * 8 * L], F32, kind="ExternalOutput").ap()

    xnb_s = nc.dram_tensor("xnb_s", [NCH * P, 8 * L], BF16, kind="Internal").ap()
    xnf_s = nc.dram_tensor("xnf_s", [NCH * P, 8 * L], F32, kind="Internal").ap()
    part4 = nc.dram_tensor("part4", [NCH * P, 4 * C], F32, kind="Internal").ap()
    wqb_s = nc.dram_tensor("wqb_s", [NHC * P, 8 * C], BF16, kind="Internal").ap()
    wkb_s = nc.dram_tensor("wkb_s", [NHC * P, 8 * C], BF16, kind="Internal").ap()
    wvb_s = nc.dram_tensor("wvb_s", [NHC * P, 8 * C], BF16, kind="Internal").ap()
    w13b_s = nc.dram_tensor("w13b_s", [NE * P, 8 * C], BF16, kind="Internal").ap()
    w2b_s = nc.dram_tensor("w2b_s", [NE * P, 4 * C], BF16, kind="Internal").ap()
    ropeCb = nc.dram_tensor("ropeCb", [32, T], BF16, kind="Internal").ap()
    ropeSb = nc.dram_tensor("ropeSb", [32, T], BF16, kind="Internal").ap()
    cc_sum = nc.dram_tensor("cc_sum", [NCH * P, 4 * C], F32, kind="Internal").ap()

    if DEBUG_DUMPS:
        dump_xnf = nc.dram_tensor("dump_xnf", [C, T], F32, kind="ExternalOutput").ap()
        dump_p4 = nc.dram_tensor("dump_p4", [T, C], F32, kind="ExternalOutput").ap()
        dump_att = nc.dram_tensor("dump_att", [PCP, C], F32, kind="ExternalOutput").ap()
        dump_att2 = nc.dram_tensor("dump_att2", [PCP, C], F32, kind="ExternalOutput").ap()
        dump_cc = nc.dram_tensor("dump_cc", [PCP, C], F32, kind="ExternalOutput").ap()

    dram_xT_chunk = lambda j: xT[j * P:(j + 1) * P]
    dram_xnb_chunk = lambda j: xnb_s[j * P:(j + 1) * P]
    dram_xnf_chunk = lambda j: xnf_s[j * P:(j + 1) * P]

    with ExitStack() as ctx:
        sb = lambda name, shape, dt: ctx.enter_context(nc.sbuf_tensor(name, shape, dt))
        mksem = lambda name: Sem(nc, ctx, name)

        IDB = sb("IDB", [P, P], BF16)
        IDF = sb("IDF", [P, P], F32)
        ONB = sb("ONB", [P, P], BF16)
        GWT = sb("GWT", [P, 8, 16], F32)
        GBT = sb("GBT", [P, 16], F32)
        GNW = sb("GNW", [P, 8, 1], F32)
        GNB = sb("GNB", [P, 8, 1], F32)
        ONE1 = sb("ONE1", [1, P], BF16)
        SFL = sb("SFL", [1, 4], F32)

        ph = mksem("ph")
        vself = mksem("vself")
        sself = mksem("sself")
        oneS = mksem("oneS")
        PH = [0]
        sC = mksem("sC")
        gCa = mksem("gCa")
        gCb = mksem("gCb")
        gCc = mksem("gCc")
        gCbl = [None] + [mksem(f"gCbl{h}") for h in range(1, 4)]

        block = ctx.enter_context(nc.Block())

        @block.sync
        def _(eng):
            eng.dma_start(out=IDF[:], in_=identf).then_inc(sC.h, 16)
            eng.dma_start(out=GWT[:], in_=gwT.rearrange("(a p) e -> p a e", p=P)).then_inc(sC.h, 16)
            eng.dma_start(out=GBT[:], in_=gbt).then_inc(sC.h, 16)
            eng.dma_start(out=GNW[:], in_=gnw.rearrange("(a p) o -> p a o", p=P), allow_slow_non_contiguous=True).then_inc(sC.h, 16)
            eng.dma_start(out=GNB[:], in_=gnb.rearrange("(a p) o -> p a o", p=P), allow_slow_non_contiguous=True).then_inc(sC.h, 16)
            sC.n += 80

        @block.gpsimd
        def _(eng):
            eng.dma_start(out=IDB[:], in_=identf).then_inc(gCa.h, 16)
            eng.dma_start(out=ONB[:], in_=onesC).then_inc(gCa.h, 16)
            gCa.n += 32
            eng.memset(ONE1[:], 1.0)
            eng.memset(SFL[:], 0.0).then_inc(oneS.h, 1)
            oneS.n += 1
            eng.dma_start(out=wqb_s[0:P], in_=wq[0:P]).then_inc(gCb.h, 16)
            eng.dma_start(out=wkb_s[0:P], in_=wk[0:P]).then_inc(gCb.h, 16)
            eng.dma_start(out=wvb_s[0:P], in_=wv[0:P]).then_inc(gCb.h, 16)
            eng.dma_start(out=ropeCb, in_=ropeC).then_inc(gCb.h, 16)
            eng.dma_start(out=ropeSb, in_=ropeS).then_inc(gCb.h, 16)
            gCb.n += 80

        def phase_barrier(waits=None):
            PH[0] += 1
            decs = {"sync": block.sync, "gpsimd": block.gpsimd, "tensor": block.tensor,
                    "vector": block.vector, "scalar": block.scalar}
            for name, dec in decs.items():
                wl = (waits or {}).get(name, [])

                @dec
                def _(eng, wl=wl):
                    for (sh, sv) in wl:
                        eng.wait_ge(sh, sv)
                    eng.nop().then_inc(ph.h, 1)

        with ExitStack() as p0:
            sb0 = lambda n, s, d: p0.enter_context(nc.sbuf_tensor(n, s, d))
            XF = sb0("XF", [P, 4, 8, L], F32)
            XB = sb0("XB", [P, 4, 8, L], BF16)
            SQ = sb0("SQ", [P, 2, 8, L], BF16)
            MSB = sb0("MSB", [P, L], F32)
            M2 = sb0("M2", [P, L], F32)
            VEPS = sb0("VEPS", [P, L], F32)
            REC = sb0("REC", [P, L], F32)
            RSTD = sb0("RSTD", [P, L], F32)
            T1 = sb0("T1", [P, 8, L], F32)
            XNB = sb0("XNB", [P, 2, 8, L], BF16)
            XNF = sb0("XNF", [P, 2, 8, L], F32)
            PM = [p0.enter_context(nc.psum_tensor(f"PM{i}", [P, L], F32)) for i in range(2)]
            PQ = [p0.enter_context(nc.psum_tensor(f"PQ{i}", [P, L], F32)) for i in range(2)]

            s_x = [mksem(f"s_x{i}") for i in range(4)]
            sc_ca = mksem("sc_ca")
            sc_sq = mksem("sc_sq")
            t_pm = mksem("t_pm")
            v_rc = mksem("v_rc")
            sc_rs = mksem("sc_rs")
            v_dn = mksem("v_dn")
            sc_xb = mksem("sc_xb")
            sSTB = mksem("sSTB")
            vSTF = mksem("vSTF")

            order = list(range(NCH - 1, -1, -1))

            @block.sync
            def _(eng):
                eng.wait_ge(sC.h, sC.n)
                for i, j in enumerate(order):
                    d4 = i % 4
                    if i >= 4:
                        eng.wait_ge(v_dn.h, i - 3)
                        eng.wait_ge(sc_sq.h, i - 3)
                    eng.dma_start(out=XF[:, d4].rearrange("p a t -> p (a t)"), in_=dram_xT_chunk(j)).then_inc(s_x[d4].h, 16)
                    s_x[d4].n += 16

            @block.gpsimd
            def _(eng):
                eng.wait_ge(v_dn.h, 4)
                for hh in range(1, NHC):
                    eng.dma_start(out=wqb_s[hh * P:(hh + 1) * P], in_=wq[hh * P:(hh + 1) * P]).then_inc(gCbl[hh].h, 16)
                    eng.dma_start(out=wkb_s[hh * P:(hh + 1) * P], in_=wk[hh * P:(hh + 1) * P]).then_inc(gCbl[hh].h, 16)
                    eng.dma_start(out=wvb_s[hh * P:(hh + 1) * P], in_=wv[hh * P:(hh + 1) * P]).then_inc(gCbl[hh].h, 16)
                    gCbl[hh].n += 48
                eng.wait_ge(v_dn.h, NCH)
                eng.dma_start(out=w13b_s, in_=w13T).then_inc(gCc.h, 16)
                eng.dma_start(out=w2b_s, in_=w2T).then_inc(gCc.h, 16)
                gCc.n += 32

            @block.scalar
            def _(eng):
                eng = _Guard(eng, sself)
                eng.wait_ge(oneS.h, 1)

                def cast(i):
                    d4 = i % 4
                    eng.wait_ge(s_x[d4].h, 16 * (i // 4 + 1))
                    if i >= 4:
                        eng.wait_ge(t_pm.h, i - 3)
                    ins = eng.activation(XB[:, i % 4], XF[:, i % 4], AF.Copy)
                    sc_ca.inc(ins)

                cast(0)
                for i in range(NCH):
                    if i + 1 < NCH:
                        cast(i + 1)
                    d = i % 2
                    if i >= 2:
                        eng.wait_ge(t_pm.h, i - 1)
                    d4 = i % 4
                    ins = eng.activation(SQ[:, d], XB[:, d4], AF.Square)
                    sc_sq.inc(ins)
                    eng.wait_ge(v_rc.h, i + 1)
                    ins = eng.activation(RSTD[:], REC[:], AF.Sqrt)
                    sc_rs.inc(ins)
                    if i >= 2:
                        eng.wait_ge(sSTB.h, 16 * (i - 1))
                    eng.wait_ge(v_dn.h, i + 1)
                    ins = eng.activation(XNB[:, d], XNF[:, d], AF.Copy)
                    sc_xb.inc(ins)
                    eng.wait_ge(sc_xb.h, sc_xb.n)
                    eng.dma_start(out=dram_xnb_chunk(NCH - 1 - i),
                                  in_=XNB[:, d].rearrange("p a t -> p (a t)")).then_inc(sSTB.h, 16)
                    sSTB.n += 16
                    eng.dma_start(out=dram_xnf_chunk(NCH - 1 - i),
                                  in_=XNF[:, d].rearrange("p a t -> p (a t)")).then_inc(vSTF.h, 16)
                    vSTF.n += 16

            @block.tensor
            def _(eng):
                eng.wait_ge(gCa.h, 32)
                for i in range(NCH):
                    d = i % 2
                    d4 = i % 4
                    eng.wait_ge(sc_ca.h, i + 1)
                    eng.wait_ge(sc_sq.h, i + 1)
                    if i >= 2:
                        eng.wait_ge(v_dn.h, i - 1)
                    for ci in range(8):
                        eng.matmul(PM[d][:], ONB[:], XB[:, d4, ci], start=(ci == 0), stop=(ci == 7))
                    for ci in range(8):
                        ins = eng.matmul(PQ[d][:], ONB[:], SQ[:, d, ci], start=(ci == 0), stop=(ci == 7))
                    ins.then_inc(t_pm.h, 1)
                    t_pm.n += 1

            @block.vector
            def _(eng):
                eng = _Guard(eng, vself)
                for i in range(NCH):
                    d = i % 2
                    eng.wait_ge(t_pm.h, i + 1)
                    eng.tensor_copy(MSB[:], PM[d][:])
                    eng.tensor_tensor(M2[:], MSB[:], MSB[:], ALU.mult)
                    eng.scalar_tensor_tensor(VEPS[:], M2[:], -1.0, PQ[d][:], ALU.mult, ALU.add)
                    eng.tensor_scalar(VEPS[:], VEPS[:], EPS, None, ALU.add)
                    ins = eng.reciprocal(REC[:], VEPS[:])
                    v_rc.inc(ins)
                    eng.wait_ge(sc_rs.h, i + 1)
                    d4 = i % 4
                    eng.wait_ge(s_x[d4].h, 16 * (i // 4 + 1))
                    if i >= 2:
                        eng.wait_ge(vSTF.h, 16 * (i - 1))
                        eng.wait_ge(sc_xb.h, i - 1)
                    pmb = PM[d][:].rearrange("p (o t) -> p o t", o=1).to_broadcast([P, 8, L])
                    rsb = RSTD[:].rearrange("p (o t) -> p o t", o=1).to_broadcast([P, 8, L])
                    eng.tensor_tensor(T1[:], XF[:, d4], pmb, ALU.subtract)
                    ins = eng.tensor_tensor(XNF[:, d], T1[:], rsb, ALU.mult)
                    v_dn.inc(ins)

            phase_barrier({
                "sync": [(vSTF.h, vSTF.n), (sSTB.h, sSTB.n)],
                "gpsimd": [],
                "tensor": [(t_pm.h, t_pm.n)],
                "vector": [(v_dn.h, v_dn.n)],
                "scalar": [(sc_rs.h, sc_rs.n), (sc_xb.h, sc_xb.n)],
            })

        with ExitStack() as p1:
            sb1 = lambda n, s, d: p1.enter_context(nc.sbuf_tensor(n, s, d))
            XN1 = sb1("XN1", [P, 2, 8, L], BF16)
            WQ = sb1("WQ", [P, 2, 8, C], BF16)
            WK = sb1("WK", [P, 2, 8, C], BF16)
            WV = sb1("WV", [P, 8, C], BF16)
            QF = sb1("QF", [P, 8, L], BF16)
            KF = sb1("KF", [P, 8, L], BF16)
            VSB = sb1("VSB", [P, 4, C], BF16)
            QTK = sb1("QTK", [P, 4, C], BF16)
            PMS = sb1("PMS", [P, 4, L], BF16)
            SSB = sb1("SSB", [P, 8, C], BF16)
            OUTS = sb1("OUTS", [P, 2, 4, C], F32)
            RT = sb1("RT", [P, L], F32)
            RA = sb1("RA", [32, L], F32)
            RB = sb1("RB", [32, L], F32)
            MSK = sb1("MSK", [P, 4, L], BF16)
            GV = sb1("GV", [P, 4, 2], F32)
            RCS = sb1("RCS", [32, 2, L], BF16)
            RSS = sb1("RSS", [32, 2, L], BF16)
            PRJ = [p1.enter_context(nc.psum_tensor(f"PRJ{i}", [P, L], F32)) for i in range(2)]
            QTP = [p1.enter_context(nc.psum_tensor(f"QTP{i}", [P, C], BF16)) for i in range(2)]
            PSH = [p1.enter_context(nc.psum_tensor(f"PSH{i}", [P, L], F32)) for i in range(2)]
            OPS = [p1.enter_context(nc.psum_tensor(f"OPS{i}", [P, L], F32)) for i in range(2)]

            sXN = [mksem("sXN0"), mksem("sXN1")]
            sRP = [mksem("sRP0"), mksem("sRP1")]
            gMS = mksem("gMS")
            gW = mksem("gW")
            gA = [mksem(f"gA{j}") for j in range(NCH)]
            tE = mksem("tE")
            scEV = mksem("scEV")
            vRP = mksem("vRP")
            vQT = mksem("vQT")
            vPSH = mksem("vPSH")
            vPF = mksem("vPF")
            scF = mksem("scF")
            scR = mksem("scR")
            vOE = mksem("vOE")

            NT = NHC * NCH
            GT = 56

            def tkt(tt):
                return tt // NCH, tt % NCH, NCH - 1 - (tt % NCH)

            @block.sync
            def _(eng):
                eng.wait_ge(ph.h, 5 * PH[0])
                eng.wait_ge(gCb.h, 80)
                for tt in range(NT):
                    h, jj, j = tkt(tt)
                    d = tt % 2
                    if jj == 0:
                        wd = h % 2
                        if h >= 1:
                            eng.wait_ge(gCbl[h].h, 48)
                        if h >= 2:
                            eng.wait_ge(tE.h, GT * (NCH * (h - 1) - 1) + 24)
                        hh = h
                        eng.dma_start(out=WQ[:, wd].rearrange("p a c -> p (a c)"), in_=wqb_s[hh * P:(hh + 1) * P]).then_inc(gW.h, 16)
                        eng.dma_start(out=WK[:, wd].rearrange("p a c -> p (a c)"), in_=wkb_s[hh * P:(hh + 1) * P]).then_inc(gW.h, 16)
                        gW.n += 32
                        if h > 0:
                            eng.wait_ge(tE.h, GT * (NCH * h - 1) + 24)
                        eng.dma_start(out=WV[:].rearrange("p a c -> p (a c)"), in_=wvb_s[hh * P:(hh + 1) * P]).then_inc(gW.h, 16)
                        gW.n += 16
                    if tt >= 2:
                        eng.wait_ge(tE.h, GT * (tt - 2) + 24)
                        eng.wait_ge(vRP.h, 2 * (tt - 1))
                    eng.dma_start(out=XN1[:, d].rearrange("p a t -> p (a t)"), in_=dram_xnb_chunk(j)).then_inc(sXN[d].h, 16)
                    sXN[d].n += 16
                    eng.dma_start(out=RCS[:, d], in_=ropeCb[:, j * L:(j + 1) * L]).then_inc(sRP[d].h, 16)
                    eng.dma_start(out=RSS[:, d], in_=ropeSb[:, j * L:(j + 1) * L]).then_inc(sRP[d].h, 16)
                    sRP[d].n += 32

            @block.gpsimd
            def _(eng):
                eng.wait_ge(ph.h, 5 * PH[0])
                eng.dma_start(out=MSK[:], in_=maskP.rearrange("(a p) k -> p a k", p=P)).then_inc(gMS.h, 16)
                eng.dma_start(out=GV[:], in_=gvec.rearrange("(a p) c -> p a c", p=P)).then_inc(gMS.h, 16)
                gMS.n += 32
                for h in range(NHC):
                    for jj in range(NCH):
                        tt = h * NCH + jj
                        j = NCH - 1 - jj
                        d = tt % 2
                        eng.wait_ge(vOE.h, 8 * (tt + 1))
                        if h > 0:
                            eng.wait_ge(gA[j].h, 64 * h)
                        dst = part4[j * P:(j + 1) * P]
                        for aa in range(4):
                            da = dst[:, aa * C:(aa + 1) * C]
                            if h == 0:
                                ins = eng.dma_start(out=da, in_=OUTS[:, d, aa, :])
                            else:
                                ins = eng.dma_start(out=da, in_=OUTS[:, d, aa, :], accum_op=ALU.add)
                            ins.then_inc(gA[j].h, 16)
                            gA[j].n += 16

            @block.tensor
            def _(eng):
                eng.wait_ge(ph.h, 5 * PH[0])
                for tt in range(NT):
                    h, jj, j = tkt(tt)
                    d = tt % 2
                    first = jj == 0
                    eng.wait_ge(sXN[d].h, 16 * (tt // 2 + 1))
                    if jj == 0:
                        eng.wait_ge(gW.h, 48 * (h + 1))
                    for ci in range(8):
                        Gp = 24 * tt + ci
                        if Gp >= 2:
                            eng.wait_ge(scEV.h, Gp - 1)
                        bk = PRJ[Gp % 2]
                        for kci in range(8):
                            ins = eng.matmul(bk[:], WQ[:, h % 2, kci, ci * P:(ci + 1) * P], XN1[:, d, kci],
                                             start=(kci == 0), stop=(kci == 7))
                        tE.inc(ins)
                    for ci in range(8):
                        Gp = 24 * tt + 8 + ci
                        if Gp >= 2:
                            eng.wait_ge(scEV.h, Gp - 1)
                        bk = PRJ[Gp % 2]
                        for kci in range(8):
                            ins = eng.matmul(bk[:], WK[:, h % 2, kci, ci * P:(ci + 1) * P], XN1[:, d, kci],
                                             start=(kci == 0), stop=(kci == 7))
                        tE.inc(ins)
                    for gi in range(8):
                        ttile, hf = gi // 2, gi % 2
                        Gp = 24 * tt + 16 + gi
                        if Gp >= 2:
                            eng.wait_ge(scEV.h, Gp - 1)
                        bk = PRJ[Gp % 2]
                        for kci in range(8):
                            ins = eng.matmul(bk[:], XN1[:, d, kci, ttile * P:(ttile + 1) * P],
                                             WV[:, kci, hf * L:(hf + 1) * L],
                                             start=(kci == 0), stop=(kci == 7))
                        tE.inc(ins)
                    eng.wait_ge(scEV.h, 24 * tt + 8)
                    eng.wait_ge(scF.h, 3 * tt + 1)
                    for q in range(4):
                        Gq = 4 * tt + q
                        if Gq >= 2:
                            eng.wait_ge(vQT.h, Gq - 1)
                        bk = QTP[Gq % 2]
                        for ci in range(8):
                            ins = eng.transpose(bk[:, ci * P:(ci + 1) * P],
                                                QF[:, ci, q * P:(q + 1) * P], IDB[:])
                        tE.inc(ins)
                    eng.wait_ge(scEV.h, 24 * tt + 16)
                    eng.wait_ge(scF.h, 3 * tt + 2)
                    for q in range(4):
                        Sp = 20 * tt + q
                        if Sp >= 2:
                            eng.wait_ge(vPSH.h, Sp - 1)
                        bk = PSH[Sp % 2]
                        for ci in range(8):
                            ins = eng.matmul(bk[:], QF[:, ci, q * P:(q + 1) * P], KF[:, ci],
                                             start=(ci == 0), stop=(ci == 7))
                        tE.inc(ins)
                    eng.wait_ge(scEV.h, 24 * (tt + 1))
                    eng.wait_ge(scF.h, 3 * (tt + 1))
                    eng.wait_ge(vPSH.h, 20 * tt + 4)
                    eng.wait_ge(vPF.h, tt + 1)
                    for o in range(8):
                        kt, hf = o // 2, o % 2
                        Op = 8 * tt + o
                        if Op >= 2:
                            eng.wait_ge(vOE.h, Op - 1)
                        bk = OPS[Op % 2]
                        nmm = 4 if first else 12
                        mi = 0
                        for qt in range(4):
                            ins = eng.matmul(bk[:], PMS[:, qt, kt * P:(kt + 1) * P],
                                             VSB[:, qt, hf * L:(hf + 1) * L],
                                             start=(mi == 0), stop=(mi == nmm - 1))
                            mi += 1
                        if not first:
                            for di in range(8):
                                ins = eng.matmul(bk[:], KF[:, di, kt * P:(kt + 1) * P],
                                                 SSB[:, di, hf * L:(hf + 1) * L],
                                                 start=False, stop=(mi == nmm - 1))
                                mi += 1
                        tE.inc(ins)
                    eng.wait_ge(vQT.h, 4 * (tt + 1))
                    for s in range(16):
                        c1i, hf = s // 2, s % 2
                        Sp = 20 * tt + 4 + s
                        if Sp >= 2:
                            eng.wait_ge(vPSH.h, Sp - 1)
                        bk = PSH[Sp % 2]
                        for rt in range(4):
                            ins = eng.matmul(bk[:], QTK[:, rt, c1i * P:(c1i + 1) * P],
                                             VSB[:, rt, hf * L:(hf + 1) * L],
                                             start=(rt == 0), stop=(rt == 3))
                        tE.inc(ins)
                assert tE.n == GT * NT

            @block.scalar
            def _(eng):
                eng = _Guard(eng, sself)
                eng.wait_ge(ph.h, 5 * PH[0])
                for tt in range(NT):
                    for ci in range(8):
                        eng.wait_ge(tE.h, GT * tt + ci + 1)
                        bk = PRJ[(24 * tt + ci) % 2]
                        if ci == 0:
                            eng.wait_ge(vRP.h, 2 * tt + 1)
                            eng.activation(QF[0:64, ci, :], RT[0:64, :], AF.Silu)
                            ins = eng.activation(QF[64:128, ci, :], bk[64:128, :], AF.Silu)
                        else:
                            ins = eng.activation(QF[:, ci, :], bk[:], AF.Silu)
                        scEV.inc(ins)
                    ins = eng.activation(SFL[0:1, 0:1], SFL[0:1, 0:1], AF.Silu)
                    scF.inc(ins)
                    for ci in range(8):
                        eng.wait_ge(tE.h, GT * tt + 8 + ci + 1)
                        bk = PRJ[(24 * tt + 8 + ci) % 2]
                        if ci == 0:
                            eng.wait_ge(vRP.h, 2 * tt + 2)
                            eng.activation(KF[0:64, ci, :], RT[0:64, :], AF.Silu)
                            ins = eng.activation(KF[64:128, ci, :], bk[64:128, :], AF.Silu)
                        else:
                            ins = eng.activation(KF[:, ci, :], bk[:], AF.Silu)
                        scEV.inc(ins)
                    ins = eng.activation(SFL[0:1, 0:1], SFL[0:1, 0:1], AF.Silu)
                    scF.inc(ins)
                    for gi in range(8):
                        ttile, hf = gi // 2, gi % 2
                        eng.wait_ge(tE.h, GT * tt + 16 + gi + 1)
                        bk = PRJ[(24 * tt + 16 + gi) % 2]
                        ins = eng.activation(VSB[:, ttile, hf * L:(hf + 1) * L], bk[:], AF.Silu)
                        scEV.inc(ins)
                    ins = eng.activation(SFL[0:1, 0:1], SFL[0:1, 0:1], AF.Silu)
                    scF.inc(ins)
                    for q in range(4):
                        eng.wait_ge(tE.h, GT * tt + 24 + q + 1)
                        ins = eng.activation(SFL[0:1, 0:1], SFL[0:1, 0:1], AF.Silu)
                        scR.inc(ins)
                assert scEV.n == 24 * NT

            @block.vector
            def _(eng):
                eng = _Guard(eng, vself)
                eng.wait_ge(ph.h, 5 * PH[0])
                eng.wait_ge(gMS.h, 32)
                for tt in range(NT):
                    h, jj, j = tkt(tt)
                    d = tt % 2
                    first = jj == 0
                    eng.wait_ge(sRP[d].h, 32 * (tt // 2 + 1))
                    cs_ = RCS[:, d]
                    sn_ = RSS[:, d]
                    eng.wait_ge(tE.h, GT * tt + 1)
                    bk = PRJ[(24 * tt) % 2]
                    eng.tensor_tensor(RA[:], bk[0:32, :], cs_, ALU.mult)
                    eng.tensor_tensor(RB[:], bk[32:64, :], sn_, ALU.mult)
                    eng.tensor_tensor(RT[0:32, :], RA[:], RB[:], ALU.subtract)
                    eng.tensor_tensor(RA[:], bk[0:32, :], sn_, ALU.mult)
                    eng.tensor_tensor(RB[:], bk[32:64, :], cs_, ALU.mult)
                    ins = eng.tensor_tensor(RT[32:64, :], RA[:], RB[:], ALU.add)
                    vRP.inc(ins)
                    eng.wait_ge(tE.h, GT * tt + 9)
                    eng.wait_ge(scEV.h, 24 * tt + 2)
                    bk = PRJ[(24 * tt + 8) % 2]
                    eng.tensor_tensor(RA[:], bk[0:32, :], cs_, ALU.mult)
                    eng.tensor_tensor(RB[:], bk[32:64, :], sn_, ALU.mult)
                    eng.tensor_tensor(RT[0:32, :], RA[:], RB[:], ALU.subtract)
                    eng.tensor_tensor(RA[:], bk[0:32, :], sn_, ALU.mult)
                    eng.tensor_tensor(RB[:], bk[32:64, :], cs_, ALU.mult)
                    ins = eng.tensor_tensor(RT[32:64, :], RA[:], RB[:], ALU.add)
                    vRP.inc(ins)
                    for q in range(4):
                        eng.wait_ge(tE.h, GT * tt + 24 + q + 1)
                        eng.wait_ge(scR.h, 4 * tt + q + 1)
                        bk = QTP[(4 * tt + q) % 2]
                        ins = eng.tensor_scalar(QTK[:, q, :], bk[:], GV[:, q, 0:1], None, ALU.mult)
                        vQT.inc(ins)
                    for q in range(4):
                        eng.wait_ge(tE.h, GT * tt + 28 + q + 1)
                        bk = PSH[(20 * tt + q) % 2]
                        ins = eng.tensor_tensor(PMS[:, q, :], bk[:], MSK[:, q, :], ALU.mult)
                        vPSH.inc(ins)
                    ins = eng.tensor_copy(RA[0:1, 0:1], RB[0:1, 0:1])
                    vPF.inc(ins)
                    for o in range(8):
                        kt, hf = o // 2, o % 2
                        eng.wait_ge(tE.h, GT * tt + 32 + o + 1)
                        if o == 0 and tt >= 2:
                            pt = tt - 2
                            pj = NCH - 1 - (pt % NCH)
                            ph_ = pt // NCH
                            eng.wait_ge(gA[pj].h, 64 * (ph_ + 1))
                        bk = OPS[(8 * tt + o) % 2]
                        ins = eng.tensor_scalar(OUTS[:, d, kt, hf * L:(hf + 1) * L], bk[:],
                                                GV[:, kt, 1:2], None, ALU.mult)
                        vOE.inc(ins)
                    for s in range(16):
                        c1i, hf = s // 2, s % 2
                        eng.wait_ge(tE.h, GT * tt + 40 + s + 1)
                        bk = PSH[(20 * tt + 4 + s) % 2]
                        dst = SSB[:, c1i, hf * L:(hf + 1) * L]
                        if first:
                            ins = eng.tensor_copy(dst, bk[:])
                        else:
                            ins = eng.scalar_tensor_tensor(dst, dst, GL, bk[:], ALU.mult, ALU.add)
                        vPSH.inc(ins)
            phase_barrier({
                "sync": [(sXN[0].h, sXN[0].n), (sXN[1].h, sXN[1].n),
                         (sRP[0].h, sRP[0].n), (sRP[1].h, sRP[1].n), (gW.h, gW.n)],
                "gpsimd": [(gA[j].h, gA[j].n) for j in range(NCH)] + [(gMS.h, gMS.n)],
                "tensor": [(tE.h, tE.n)],
                "vector": [(vPSH.h, vPSH.n), (vOE.h, vOE.n), (vQT.h, vQT.n)],
                "scalar": [(scEV.h, scEV.n), (scR.h, scR.n), (scF.h, scF.n)],
            })
        with ExitStack() as p2:
            sb2 = lambda n, s, d: p2.enter_context(nc.sbuf_tensor(n, s, d))
            XNB2 = sb2("XNB2", [P, 2, 2, 8, L], BF16)
            XNF2 = sb2("XNF2", [P, 8, L], F32)
            Y = sb2("Y", [P, 2, 8, L], F32)
            ATT = sb2("ATT", [P, 8, PCP], F32)
            SQA = sb2("SQA", [P, 4, PCP], BF16)
            W13B = sb2("W13B", [P, 2, 8, C], BF16)
            W2B = sb2("W2B", [P, 2, 4, C], BF16)
            SG = sb2("SG", [P, 2, L], BF16)
            H2 = sb2("H2", [P, 4, C], BF16)
            SCT = sb2("SCT", [16, PCP], F32)
            BIS = sb2("BIS", [P, 8, 16], F32)
            M8S = sb2("M8S", [P, 8], F32)
            TH = sb2("TH", [P, 1], F32)
            SEL = sb2("SEL", [P, 16], F32)
            WSL = sb2("WSL", [P, 8, 16], F32)
            SW = sb2("SW", [P, 1], F32)
            IV = sb2("IV", [P, 1], F32)
            CMT = sb2("CMT", [P, 8, 16], F32)
            CMB = sb2("CMB", [16, PCP], BF16)
            CMB1 = sb2("CMB1", [1, 2, PCP], BF16)
            CBC = sb2("CBC", [P, PCP], BF16)
            RS1 = sb2("RS1", [P, 8, 16], F32)
            RS2 = sb2("RS2", [P, 8, 16], F32)
            MU = sb2("MU", [P, 8, 16], F32)
            MSQ = sb2("MSQ", [P, 8, 16], F32)
            VAR = sb2("VAR", [P, 8, 16], F32)
            RCg = sb2("RCg", [P, 8, 16], F32)
            M2G = sb2("M2G", [P, 8, 16], F32)
            XGH = sb2("XGH", [P, 2, PCP], F32)
            RSD = sb2("RSD", [P, 8, 16], F32)

            PGEN = [p2.enter_context(nc.psum_tensor(f"PGEN{i}", [P, L], F32)) for i in range(2)]
            HPG = [p2.enter_context(nc.psum_tensor(f"HPG{i}", [P, L], F32)) for i in range(2)]
            HPU = [p2.enter_context(nc.psum_tensor(f"HPU{i}", [P, L], F32)) for i in range(2)]
            YPS = [p2.enter_context(nc.psum_tensor(f"YPS{i}", [P, L], F32)) for i in range(2)]

            csem = mksem("csem")
            gAT = mksem("gAT")
            sXB2 = [mksem("sXB2a"), mksem("sXB2b")]
            sXF = mksem("sXF")
            sY = mksem("sY")
            sW2p = [mksem("sW2a"), mksem("sW2b")]
            sCBp = [mksem("sCBa"), mksem("sCBb")]
            tG = mksem("tG")
            scS = mksem("scS")
            scR2 = mksem("scR2")
            vP2 = mksem("vP2")
            gOUT = mksem("gOUT")

            _hidx = [(cp, e, xx) for cp in range(NCP) for e in range(NE) for xx in range(8)]

            W2WAIT = {}
            CBWAIT = {}
            _w2c = [0, 0]
            _cbc = [0, 0]
            if P2_FFN:
                for _cp in range(NCP):
                    for _e in range(NE):
                        _de = _e % 2
                        _w2c[_de] += 32
                        W2WAIT[(_cp, _e)] = _w2c[_de]
                        if _e < 16:
                            _cbc[_de] += 16
                            CBWAIT[(_cp, _e)] = _cbc[_de]
            GAT_AT = {}
            _ga = 0
            for _cp in range(NCP):
                _ga += 16
                if DEBUG_DUMPS and _cp == 0:
                    _ga += 16
                GAT_AT[_cp] = _ga

            TGI = {}
            _t = 0
            for cp in range(NCP):
                def _m_gating(c):
                    nonlocal _t
                    for hf in range(2):
                        _t += 1
                        TGI[("gate", c, hf)] = _t
                    for tt8 in range(8):
                        _t += 1
                        TGI[("gtp", c, tt8)] = _t
                    for tt8 in range(8):
                        _t += 1
                        TGI[("ctp", c, tt8)] = _t
                if cp == 0:
                    _m_gating(0)
                for e in (range(NE) if P2_FFN else []):
                    if e != 16:
                        for hf in range(2):
                            _t += 1
                            TGI[("cbb", cp, e, hf)] = _t
                    for xx in range(8):
                        _t += 1
                        TGI[("hg", cp, e, xx)] = _t
                        _t += 1
                        TGI[("hu", cp, e, xx)] = _t
                    for di in range(8):
                        for hf in range(2):
                            _t += 1
                            TGI[("y", cp, e, di, hf)] = _t
                    TGI[("edone", cp, e)] = _t
                if not P2_FFN:
                    for e in range(NE):
                        TGI[("edone", cp, e)] = _t
                if cp + 1 < NCP:
                    _m_gating(cp + 1)
                if P2_ATT:
                    for ci in range(8):
                        for ah in range(2):
                            _t += 1
                            TGI[("atp", cp, ci, ah)] = _t
                TGI[("cpdone", cp)] = _t
            TG_TOTAL = _t

            SCI = {}
            SCR2 = {}
            _s = 0
            _r2 = 0
            for cp in range(NCP):
                if cp == 0:
                    for hf in range(2):
                        _s += 1
                        SCI[("sig", 0, hf)] = _s
                    for t8 in range(8):
                        _r2 += 1
                        SCR2[("gtp", 0, t8)] = _r2
                    for t8 in range(8):
                        _r2 += 1
                        SCR2[("ctp", 0, t8)] = _r2
                if P2_FFN:
                    for e in range(NE):
                        for xx in range(8):
                            _s += 1
                            SCI[("silu", cp, e, xx)] = _s
                if P2_ATT:
                    _s += 1
                    SCI[("sq1", cp)] = _s
                    _s += 1
                    SCI[("sq2", cp)] = _s
                    _s += 1
                    SCI[("sqrt", cp)] = _s
                if cp + 1 < NCP:
                    for hf in range(2):
                        _s += 1
                        SCI[("sig", cp + 1, hf)] = _s
                    for t8 in range(8):
                        _r2 += 1
                        SCR2[("gtp", cp + 1, t8)] = _r2
                    for t8 in range(8):
                        _r2 += 1
                        SCR2[("ctp", cp + 1, t8)] = _r2
                if P2_ATT:
                    for ci in range(8):
                        for ah in range(2):
                            _r2 += 1
                            SCR2[("atp", cp, ci, ah)] = _r2
            SC_TOTAL = _s

            VI = {}
            _v = 0
            for cp in range(NCP):
                def _m_vgate(c):
                    nonlocal _v
                    for tt8 in range(8):
                        _v += 1
                        VI[("bis", c, tt8)] = _v
                    for tt8 in range(8):
                        _v += 8
                        VI[("comb", c, tt8)] = _v
                    for tt8 in range(8):
                        _v += 1
                        VI[("cmbc", c, tt8)] = _v
                if cp == 0:
                    _m_vgate(0)
                for ci in range(8):
                    _v += 1
                    VI[("yinit", cp, ci)] = _v
                for e in (range(NE) if P2_FFN else []):
                    if e != 16:
                        _v += 2
                        VI[("cbc", cp, e)] = _v
                    for xx in range(8):
                        _v += 1
                        VI[("hufree", cp, e, xx)] = _v
                        if e != 16:
                            _v += 1
                    VI[("h2done", cp, e)] = _v
                    for di in range(8):
                        for hf in range(2):
                            _v += 1
                            VI[("yadd", cp, e, di, hf)] = _v
                if P2_ATT:
                    _v += 1
                    VI[("rs2a", cp)] = _v
                    _v += 2
                    VI[("gnred", cp)] = _v
                    _v += 5
                    _v += 1
                    VI[("gnrec", cp)] = _v
                    _v += 9
                    VI[("gnnorm", cp)] = _v
                if cp + 1 < NCP:
                    _m_vgate(cp + 1)
                if P2_ATT:
                    for ci in range(8):
                        for ah in range(2):
                            _v += 1
                            VI[("stt", cp, ci, ah)] = _v
                VI[("vdone", cp)] = _v
            V_TOTAL = _v

            PGEN_USES = []
            for cp in range(NCP):
                PGEN_USES.append((("gate", cp, 0), ("sc", SCI[("sig", cp, 0)])))
                PGEN_USES.append((("gate", cp, 1), ("sc", SCI[("sig", cp, 1)])))
                for tt8 in range(8):
                    PGEN_USES.append((("gtp", cp, tt8), ("v", VI[("bis", cp, tt8)])))
                for tt8 in range(8):
                    PGEN_USES.append((("ctp", cp, tt8), ("v", VI[("cmbc", cp, tt8)])))
                if P2_FFN:
                    for e in range(16):
                        PGEN_USES.append((("cbb", cp, e, 0), ("v", VI[("cbc", cp, e)])))
                        PGEN_USES.append((("cbb", cp, e, 1), ("v", VI[("cbc", cp, e)])))
            PGEN_IDX = {u[0]: i for i, u in enumerate(PGEN_USES)}

            YPS_USES = []
            for cp in range(NCP):
                if P2_FFN:
                    for e in range(NE):
                        for di in range(8):
                            for hf in range(2):
                                YPS_USES.append((("y", cp, e, di, hf), ("v", VI[("yadd", cp, e, di, hf)])))
                if P2_ATT:
                    for ci in range(8):
                        for ah in range(2):
                            YPS_USES.append((("atp", cp, ci, ah), ("v", VI[("stt", cp, ci, ah)])))
            YPS_IDX = {u[0]: i for i, u in enumerate(YPS_USES)}

            def puse_wait(eng, uses, idx_map, key):
                i = idx_map[key]
                if i >= 2:
                    kind, val = uses[i - 2][1]
                    eng.wait_ge(scS.h if kind == "sc" else vP2.h, val)

            def hpg_wait(eng, cp, e, xx):
                gi = (cp * NE + e) * 8 + xx
                if gi >= 2:
                    pcp, pe, pxx = _hidx[gi - 2]
                    eng.wait_ge(scS.h, SCI[("silu", pcp, pe, pxx)])

            def hpu_wait(eng, cp, e, xx):
                gi = (cp * NE + e) * 8 + xx
                if gi >= 2:
                    pcp, pe, pxx = _hidx[gi - 2]
                    eng.wait_ge(vP2.h, VI[("hufree", pcp, pe, pxx)])

            @block.scalar
            def _(eng):
                eng = _Guard(eng, sself)
                def emit_sig(c):
                    for hf in range(2):
                        eng.wait_ge(tG.h, TGI[("gate", c, hf)])
                        ins = eng.activation(SCT[:, hf * L:(hf + 1) * L], PGEN[hf % 2][0:16, :], AF.Sigmoid)
                        scS.inc(ins)
                    for tt8 in range(8):
                        eng.wait_ge(tG.h, TGI[("gtp", c, tt8)])
                        ins = eng.activation(SFL[0:1, 1:2], SFL[0:1, 1:2], AF.Copy)
                        scR2.inc(ins)
                    for tt8 in range(8):
                        eng.wait_ge(tG.h, TGI[("ctp", c, tt8)])
                        ins = eng.activation(SFL[0:1, 1:2], SFL[0:1, 1:2], AF.Copy)
                        scR2.inc(ins)

                for cp in range(NCP):
                    if cp == 0:
                        emit_sig(0)
                    if P2_FFN:
                        for e in range(NE):
                            for xx in range(8):
                                eng.wait_ge(tG.h, TGI[("hg", cp, e, xx)])
                                if (cp * NE + e) * 8 + xx >= 2:
                                    pcp, pe, pxx = _hidx[(cp * NE + e) * 8 + xx - 2]
                                    eng.wait_ge(vP2.h, VI[("hufree", pcp, pe, pxx)])
                                ins = eng.activation(SG[:, xx % 2, :], HPG[xx % 2][:], AF.Silu)
                                scS.inc(ins)
                    if P2_ATT:
                        eng.wait_ge(gAT.h, GAT_AT[cp])
                        ins = eng.activation(SQA[:], ATT[:, 0:4, :], AF.Square)
                        scS.inc(ins)
                        eng.wait_ge(vP2.h, VI[("rs2a", cp)])
                        ins = eng.activation(SQA[:], ATT[:, 4:8, :], AF.Square)
                        scS.inc(ins)
                        eng.wait_ge(vP2.h, VI[("gnrec", cp)])
                        ins = eng.activation(RSD[:], RCg[:], AF.Sqrt)
                        scS.inc(ins)
                    if cp + 1 < NCP:
                        emit_sig(cp + 1)
                    if P2_ATT:
                        for ci in range(8):
                            for ah in range(2):
                                eng.wait_ge(tG.h, TGI[("atp", cp, ci, ah)])
                                ins = eng.activation(SFL[0:1, 2:3], SFL[0:1, 2:3], AF.Copy)
                                scR2.inc(ins)
                assert scS.n == SC_TOTAL

            @block.gpsimd
            def _(eng):
                eng.wait_ge(ph.h, 5 * PH[0])
                pid_g = eng.snap(nc.partition_id(engines=[ET.Pool]), min_val=0, max_val=NCORES - 1)
                hg_g = pid_g % 2
                own0 = hg_g * (8 * P)
                for _j in range(NCH):
                    eng.wait_ge(gA[_j].h, 64 * NHC)
                eng.collective_compute(
                    "AllReduce", ALU.add,
                    replica_groups=[[2 * i, 2 * i + 1] for i in range(NCORES // 2)],
                    ins=[part4.opt()], outs=[cc_sum.opt()],
                ).then_inc(csem.h, 1)
                csem.n += 1
                for cp in range(NCP):
                    def load_cmb1(e, cp=cp):
                        de = e % 2
                        eng.wait_ge(vP2.h, VI[("cmbc", cp, 7)])
                        if cp * 16 + e >= 2:
                            eng.wait_ge(tG.h, TGI[("cbb", cp if e >= 2 else cp - 1, (e - 2) % 16, 1)])
                        eng.dma_start(out=CMB1[:, de], in_=CMB[e:e + 1, :]).then_inc(sCBp[de].h, 16)
                        sCBp[de].n += 16

                    if P2_FFN:
                        for e in range(16):
                            load_cmb1(e)
                    if cp > 0:
                        eng.wait_ge(tG.h, TGI[("cpdone", cp - 1)])
                    eng.wait_ge(csem.h, 1)
                    att2 = ATT[:].rearrange("p (c g) f -> p c (g f)", c=2)
                    eng.dma_start(out=att2,
                                  in_=cc_sum[bass.ds(own0 + cp * 2 * P, 2 * P), :].rearrange("(c p) f -> p c f", p=P)).then_inc(gAT.h, 16)
                    gAT.n += 16
                    if DEBUG_DUMPS and cp == 0:
                        eng.wait_ge(gAT.h, gAT.n)
                        eng.dma_start(out=dump_att.rearrange("(a p) c -> p a c", p=P), in_=ATT[:]).then_inc(gAT.h, 16)
                        gAT.n += 16
                    eng.wait_ge(vP2.h, VI[("vdone", cp)])
                    eng.dma_start(out=outT[cp * P:(cp + 1) * P].rearrange("p (c f) -> p c f", c=2),
                                  in_=Y[:].rearrange("p c a t -> p c (a t)")).then_inc(gOUT.h, 16)
                    gOUT.n += 16
                eng.wait_ge(gOUT.h, 16 * NCP)
                if DEBUG_DUMPS:
                    eng.dma_start(out=dump_xnf, in_=xnf_s).then_inc(gOUT.h, 16)
                    eng.dma_start(out=dump_p4, in_=part4).then_inc(gOUT.h, 16)
                    eng.dma_start(out=dump_cc, in_=cc_sum[bass.ds(own0, 2 * P), :]).then_inc(gOUT.h, 16)
                    gOUT.n += 48
                    eng.wait_ge(gOUT.h, gOUT.n)

            @block.sync
            def _(eng):
                eng.wait_ge(ph.h, 5 * PH[0])
                pid_y = eng.snap(nc.partition_id(engines=[ET.SP]), min_val=0, max_val=NCORES - 1)
                own_r0 = (pid_y % 2) * (8 * P)
                wslot_prev = [None, None]
                eng.wait_ge(gCc.h, 32)
                for cp in range(NCP):
                    d = cp % 2
                    if P2_FFN:
                        if cp >= 2:
                            eng.wait_ge(tG.h, TGI[("edone", cp - 2, NE - 1)])
                        eng.dma_start(out=XNB2[:, d].rearrange("p c a t -> p c (a t)"),
                                      in_=xnb_s[bass.ds(own_r0 + cp * 2 * P, 2 * P), :].rearrange("(c p) f -> p c f", p=P)).then_inc(sXB2[d].h, 16)
                        sXB2[d].n += 16

                    def load_xnf(hf, cp=cp):
                        if cp > 0 or hf > 0:
                            pk = ("gate", cp, hf - 1) if hf > 0 else ("gate", cp - 1, 1)
                            eng.wait_ge(tG.h, TGI[pk])
                        eng.dma_start(out=XNF2[:].rearrange("p a t -> p (a t)"),
                                      in_=xnf_s[bass.ds(own_r0 + (cp * 2 + hf) * P, P), :]).then_inc(sXF.h, 16)
                        sXF.n += 16

                    def load_w(e, cp=cp):
                        de = e % 2
                        if wslot_prev[de] is not None:
                            pcp, pe = wslot_prev[de]
                            eng.wait_ge(tG.h, TGI[("edone", pcp, pe)])
                        wslot_prev[de] = (cp, e)
                        eng.dma_start(out=W13B[:, de].rearrange("p a c -> p (a c)"),
                                      in_=w13b_s[e * P:(e + 1) * P]).then_inc(sW2p[de].h, 16)
                        eng.dma_start(out=W2B[:, de].rearrange("p a c -> p (a c)"),
                                      in_=w2b_s[e * P:(e + 1) * P]).then_inc(sW2p[de].h, 16)
                        sW2p[de].n += 32

                    if P2_FFN:
                        load_xnf(0)
                        load_w(0)
                        load_w(1)
                        load_xnf(1)
                        if cp > 0:
                            eng.wait_ge(gOUT.h, 16 * cp)
                        eng.dma_start(out=Y[:].rearrange("p c a t -> p c (a t)"),
                                      in_=xT[bass.ds(own_r0 + cp * 2 * P, 2 * P), :].rearrange("(c p) f -> p c f", p=P)).then_inc(sY.h, 16)
                        sY.n += 16
                        for e in range(2, NE):
                            load_w(e)
                    else:
                        load_xnf(0)
                        load_xnf(1)
                        if cp > 0:
                            eng.wait_ge(gOUT.h, 16 * cp)
                        eng.dma_start(out=Y[:].rearrange("p c a t -> p c (a t)"),
                                      in_=xT[bass.ds(own_r0 + cp * 2 * P, 2 * P), :].rearrange("(c p) f -> p c f", p=P)).then_inc(sY.h, 16)
                        sY.n += 16
                eng.wait_ge(gOUT.h, 16 * NCP)

            @block.tensor
            def _(eng):
                eng.wait_ge(ph.h, 5 * PH[0])
                eng.wait_ge(oneS.h, 1)

                def emit_gating(c):
                    for hf in range(2):
                        eng.wait_ge(sXF.h, 16 * (2 * c + hf + 1))
                        puse_wait(eng, PGEN_USES, PGEN_IDX, ("gate", c, hf))
                        bk = PGEN[hf % 2]
                        for kci in range(8):
                            ins = eng.matmul(bk[0:16, :], GWT[:, kci, :], XNF2[:, kci],
                                             start=(kci == 0), stop=(kci == 7))
                        tG.inc(ins)
                    for tt8 in range(8):
                        eng.wait_ge(scS.h, SCI[("sig", c, 1)])
                        puse_wait(eng, PGEN_USES, PGEN_IDX, ("gtp", c, tt8))
                        bk = PGEN[tt8 % 2]
                        ins = eng.transpose(bk[:, 0:16], SCT[:, tt8 * P:(tt8 + 1) * P], IDF[0:16, 0:16])
                        tG.inc(ins)
                    for tt8 in range(8):
                        eng.wait_ge(vP2.h, VI[("comb", c, 7)])
                        puse_wait(eng, PGEN_USES, PGEN_IDX, ("ctp", c, tt8))
                        bk = PGEN[tt8 % 2]
                        ins = eng.transpose(bk[0:16, 0:P], CMT[:, tt8, :], IDF[:])
                        tG.inc(ins)

                for cp in range(NCP):
                    d = cp % 2
                    if cp == 0:
                        emit_gating(0)
                    for e in (range(NE) if P2_FFN else []):
                        de = e % 2
                        eng.wait_ge(sW2p[de].h, W2WAIT[(cp, e)])
                        if e != 16:
                            eng.wait_ge(sCBp[de].h, CBWAIT[(cp, e)])
                            for hf in range(2):
                                puse_wait(eng, PGEN_USES, PGEN_IDX, ("cbb", cp, e, hf))
                                ins = eng.matmul(PGEN[hf % 2][:], ONE1[:],
                                                 CMB1[0:1, de, hf * L:(hf + 1) * L],
                                                 start=True, stop=True)
                                tG.inc(ins)
                        for xx in range(8):
                            i, hf = xx // 2, xx % 2
                            if e == 0 and xx == 0:
                                eng.wait_ge(sXB2[d].h, 16 * (cp // 2 + 1))
                            hpg_wait(eng, cp, e, xx)
                            bkg = HPG[xx % 2]
                            for kci in range(8):
                                ins = eng.matmul(bkg[:], W13B[:, de, kci, i * P:(i + 1) * P],
                                                 XNB2[:, d, hf, kci, :],
                                                 start=(kci == 0), stop=(kci == 7))
                            tG.inc(ins)
                            hpu_wait(eng, cp, e, xx)
                            bku = HPU[xx % 2]
                            for kci in range(8):
                                ins = eng.matmul(bku[:], W13B[:, de, kci, (i + 4) * P:(i + 5) * P],
                                                 XNB2[:, d, hf, kci, :],
                                                 start=(kci == 0), stop=(kci == 7))
                            tG.inc(ins)
                        for di in range(8):
                            for hf in range(2):
                                eng.wait_ge(vP2.h, VI[("h2done", cp, e)])
                                puse_wait(eng, YPS_USES, YPS_IDX, ("y", cp, e, di, hf))
                                bk = YPS[(2 * di + hf) % 2]
                                for ii in range(4):
                                    ins = eng.matmul(bk[:], W2B[:, de, ii, di * P:(di + 1) * P],
                                                     H2[:, ii, hf * L:(hf + 1) * L],
                                                     start=(ii == 0), stop=(ii == 3))
                                tG.inc(ins)
                    if not P2_FFN and cp == 0:
                        pass
                    if cp + 1 < NCP:
                        emit_gating(cp + 1)
                    if P2_ATT:
                        eng.wait_ge(vP2.h, VI[("gnnorm", cp)])
                    for ci in (range(8) if P2_ATT else []):
                        for ah in range(2):
                            puse_wait(eng, YPS_USES, YPS_IDX, ("atp", cp, ci, ah))
                            bk = YPS[(2 * ci + ah) % 2]
                            for a in range(4):
                                aa = ah * 4 + a
                                ins = eng.transpose(bk[:, a * P:(a + 1) * P],
                                                    ATT[:, aa, ci * P:(ci + 1) * P], IDF[:])
                            tG.inc(ins)
                assert tG.n == TG_TOTAL

            @block.vector
            def _(eng):
                eng = _Guard(eng, vself)
                eng.wait_ge(ph.h, 5 * PH[0])

                def emit_vgate(c):
                    for tt8 in range(8):
                        eng.wait_ge(tG.h, TGI[("gtp", c, tt8)])
                        eng.wait_ge(scR2.h, SCR2[("gtp", c, tt8)])
                        bk = PGEN[tt8 % 2]
                        ins = eng.tensor_tensor(BIS[:, tt8, :], bk[:, 0:16], GBT[:], ALU.add)
                        vP2.inc(ins)
                    for tt8 in range(8):
                        ins = eng.max(out=M8S[:], in_=BIS[:, tt8, :])
                        vP2.inc(ins)
                        ins = eng.tensor_copy(TH[:], M8S[:, 3:4])
                        vP2.inc(ins)
                        ins = eng.tensor_scalar(SEL[:], BIS[:, tt8, :], TH[:, 0:1], None, ALU.is_ge)
                        vP2.inc(ins)
                        ins = eng.tensor_tensor(WSL[:, tt8, :], BIS[:, tt8, :], GBT[:], ALU.subtract)
                        vP2.inc(ins)
                        ins = eng.tensor_tensor(WSL[:, tt8, :], WSL[:, tt8, :], SEL[:], ALU.mult)
                        vP2.inc(ins)
                        ins = eng.reduce_sum(SW[:], WSL[:, tt8, :], axis=AX.X)
                        vP2.inc(ins)
                        ins = eng.reciprocal(IV[:], SW[:])
                        vP2.inc(ins)
                        ins = eng.tensor_scalar(CMT[:, tt8, :], WSL[:, tt8, :], IV[:, 0:1], 2.5,
                                                ALU.mult, ALU.mult)
                        vP2.inc(ins)
                        assert vP2.n == VI[("comb", c, tt8)]
                    for tt8 in range(8):
                        eng.wait_ge(tG.h, TGI[("ctp", c, tt8)])
                        eng.wait_ge(scR2.h, SCR2[("ctp", c, tt8)])
                        bk = PGEN[tt8 % 2]
                        ins = eng.tensor_copy(CMB[:, tt8 * P:(tt8 + 1) * P], bk[0:16, 0:P])
                        vP2.inc(ins)
                        assert vP2.n == VI[("cmbc", c, tt8)]

                for cp in range(NCP):
                    d = cp % 2
                    if cp == 0:
                        emit_vgate(0)
                    eng.wait_ge(sY.h, 16 * (cp + 1))
                    for ci in range(8):
                        ins = eng.tensor_scalar(Y[:, :, ci, :], Y[:, :, ci, :], GNB[:, ci, 0:1], None, ALU.add)
                        vP2.inc(ins)
                    for e in (range(NE) if P2_FFN else []):
                        if e != 16:
                            for hf in range(2):
                                eng.wait_ge(tG.h, TGI[("cbb", cp, e, hf)])
                                ins = eng.tensor_copy(CBC[:, hf * L:(hf + 1) * L], PGEN[hf % 2][:])
                                vP2.inc(ins)
                            assert vP2.n == VI[("cbc", cp, e)]
                        for xx in range(8):
                            i, hf = xx // 2, xx % 2
                            eng.wait_ge(scS.h, SCI[("silu", cp, e, xx)])
                            eng.wait_ge(tG.h, TGI[("hu", cp, e, xx)])
                            ins = eng.tensor_tensor(H2[:, i, hf * L:(hf + 1) * L],
                                                    SG[:, xx % 2, :], HPU[xx % 2][:], ALU.mult)
                            vP2.inc(ins)
                            if e != 16:
                                ins = eng.tensor_tensor(H2[:, i, hf * L:(hf + 1) * L],
                                                        H2[:, i, hf * L:(hf + 1) * L],
                                                        CBC[:, hf * L:(hf + 1) * L], ALU.mult)
                                vP2.inc(ins)
                        assert vP2.n == VI[("h2done", cp, e)]
                        for di in range(8):
                            for hf in range(2):
                                eng.wait_ge(tG.h, TGI[("y", cp, e, di, hf)])
                                bk = YPS[(2 * di + hf) % 2]
                                ins = eng.tensor_tensor(Y[:, hf, di, :],
                                                        Y[:, hf, di, :], bk[:], ALU.add)
                                vP2.inc(ins)
                    if P2_ATT:
                        eng.wait_ge(scS.h, SCI[("sq1", cp)])
                        att4 = ATT[:].rearrange("p a (g k) -> p a g k", k=64)
                        sqa4 = SQA[:].rearrange("p a (g k) -> p a g k", k=64)
                        ins = eng.reduce_sum(RS2[:, 0:4, :], sqa4, axis=AX.X)
                        vP2.inc(ins)
                        assert vP2.n == VI[("rs2a", cp)]
                        eng.wait_ge(scS.h, SCI[("sq2", cp)])
                        ins = eng.reduce_sum(RS1[:], att4, axis=AX.X)
                        vP2.inc(ins)
                        ins = eng.reduce_sum(RS2[:, 4:8, :], sqa4, axis=AX.X)
                        vP2.inc(ins)
                        ins = eng.tensor_scalar(MU[:], RS1[:], 1.0 / 64, None, ALU.mult)
                        vP2.inc(ins)
                        ins = eng.tensor_scalar(MSQ[:], RS2[:], 1.0 / 64, None, ALU.mult)
                        vP2.inc(ins)
                        ins = eng.tensor_tensor(M2G[:], MU[:], MU[:], ALU.mult)
                        vP2.inc(ins)
                        ins = eng.tensor_tensor(VAR[:], MSQ[:], M2G[:], ALU.subtract)
                        vP2.inc(ins)
                        ins = eng.tensor_scalar(RS1[:], VAR[:], EPS, None, ALU.add)
                        vP2.inc(ins)
                        ins = eng.reciprocal(RCg[:], RS1[:])
                        vP2.inc(ins)
                        assert vP2.n == VI[("gnrec", cp)]
                        eng.wait_ge(scS.h, SCI[("sqrt", cp)])
                        for ah in range(4):
                            sl = slice(2 * ah, 2 * ah + 2)
                            a4 = ATT[:, sl, :].rearrange("p a (g k) -> p a g k", k=64)
                            x4 = XGH[:].rearrange("p a (g k) -> p a g k", k=64)
                            mub = MU[:, sl, :].rearrange("p a (g o) -> p a g o", o=1).to_broadcast([P, 2, 16, 64])
                            rsb = RSD[:, sl, :].rearrange("p a (g o) -> p a g o", o=1).to_broadcast([P, 2, 16, 64])
                            ins = eng.tensor_tensor(x4, a4, mub, ALU.subtract)
                            vP2.inc(ins)
                            ins = eng.tensor_tensor(a4, x4, rsb, ALU.mult)
                            vP2.inc(ins)
                        ins = eng.tensor_copy(TH[:], TH[:])
                        vP2.inc(ins)
                        assert vP2.n == VI[("gnnorm", cp)]
                    if cp + 1 < NCP:
                        emit_vgate(cp + 1)
                    for ci in (range(8) if P2_ATT else []):
                        for ah in range(2):
                            eng.wait_ge(tG.h, TGI[("atp", cp, ci, ah)])
                            eng.wait_ge(scR2.h, SCR2[("atp", cp, ci, ah)])
                            bk = YPS[(2 * ci + ah) % 2]
                            ins = eng.scalar_tensor_tensor(Y[:, ah, ci, :], bk[:],
                                                           GNW[:, ci, 0:1],
                                                           Y[:, ah, ci, :],
                                                           ALU.mult, ALU.add)
                            vP2.inc(ins)
                    assert vP2.n == VI[("vdone", cp)]
                assert vP2.n == V_TOTAL

    return nc



def kernel(L0_attn_X_Q, L0_attn_W_Q, L0_attn_W_K, L0_attn_W_V,
           L0_attn_gn_weight, L0_attn_gn_bias,
           L0_ffn_input_ids, L0_ffn_gate_w, L0_ffn_gate_b,
           L0_ffn_w13, L0_ffn_w2, L0_ffn_w13_s, L0_ffn_w2_s):
    x = np.asarray(L0_attn_X_Q, np.float32)
    _ = L0_ffn_input_ids
    consts = _host_constants()
    in_maps = []
    for c in range(NCORES):
        in_maps.append(_prep_core_inputs(
            c, x, np.asarray(L0_attn_W_Q, np.float32), np.asarray(L0_attn_W_K, np.float32),
            np.asarray(L0_attn_W_V, np.float32),
            np.asarray(L0_attn_gn_weight, np.float32), np.asarray(L0_attn_gn_bias, np.float32),
            np.asarray(L0_ffn_gate_w, np.float32), np.asarray(L0_ffn_gate_b, np.float32),
            np.asarray(L0_ffn_w13, np.float32), np.asarray(L0_ffn_w2, np.float32),
            np.asarray(L0_ffn_w13_s, np.float32), np.asarray(L0_ffn_w2_s, np.float32),
            consts))
    nc = build_program()
    res = run_bass_kernel_spmd(nc, in_maps, core_ids=list(range(NCORES)))
    out = np.empty((B, T, C), np.float32)
    for c in range(NCORES):
        b, hg = c // 2, c % 2
        r = np.asarray(res.results[c]["outT"], np.float32).reshape(NCP, P, 2, 8, L)
        out[b, hg * HALF:(hg + 1) * HALF, :] = r.transpose(0, 2, 4, 3, 1).reshape(HALF, C)
    return out
